# version 03
# baseline (speedup 1.0000x reference)
import numpy as np
import ml_dtypes
import contextlib

import concourse.bass as bass
import concourse.mybir as mybir
from concourse.alu_op_type import AluOpType as A
from concourse.bass_utils import run_bass_kernel_spmd

F32 = mybir.dt.float32
BF16 = mybir.dt.bfloat16
I32 = mybir.dt.int32
I16 = mybir.dt.int16
ACT = mybir.ActivationFunctionType

NCORES = 8
D = 2048
DT = 16
S = 1024
NH = 16
TT = 64
SEQ = 8192
HID = 8192
NB = 32
BH = 256
MODW = 12288 // NCORES
SCALE = float(1.0 / np.sqrt(128.0))
LN_EPS = 1e-5
RMS_EPS = float(np.finfo(np.float32).eps)
THR = 0.501953125


class Flow:
    ENGINES = ("sync", "tensor", "vector", "scalar", "gpsimd")

    def __init__(self):
        self.ops = {e: [] for e in self.ENGINES}
        self.cnt = {}
        self.sems = {}

    def sem(self, name):
        self.cnt[name] = 0

    def emit(self, engine, fn):
        self.ops[engine].append(fn)

    def wait(self, engine, sem, val):
        if val is None or val <= 0:
            return
        assert val <= self.cnt[sem], f"wait {sem}>={val} but only {self.cnt[sem]} scheduled"
        self.emit(engine, lambda e, s=sem, v=val: e.wait_ge(self.sems[s], v))

    def dma(self, engine, sem, out, in_, **kw):
        self.cnt[sem] += 16
        def fn(e, out=out, in_=in_, s=sem, kw=kw):
            e.dma_start(out=out, in_=in_, **kw).then_inc(self.sems[s], 16)
        self.emit(engine, fn)
        return self.cnt[sem]

    def op(self, engine, method, *args, sem=None, **kw):
        c = None
        if sem is not None:
            self.cnt[sem] += 1
            c = self.cnt[sem]
        drain = engine in ("vector", "scalar", "gpsimd")
        def fn(e, m=method, args=args, kw=kw, s=sem, d=drain):
            ins = getattr(e, m)(*args, **kw)
            if d:
                e.drain()
            if s is not None:
                ins.then_inc(self.sems[s], 1)
        self.emit(engine, fn)
        return c

    def mm(self, out, lhsT, rhs, start, stop, sem=None, tile_position=None):
        kw = {}
        if tile_position is not None:
            kw["tile_position"] = tile_position
        return self.op("tensor", "matmul", out, lhsT, rhs, start=start, stop=stop,
                       sem=sem, **kw)


def build_program(debug=False):
    nc = bass.Bass("TRN2", target_bir_lowering=False)
    fl = Flow()
    ctx = contextlib.ExitStack()
    sbuf = lambda n, s, d: ctx.enter_context(nc.sbuf_tensor(n, s, d))
    psum = lambda n: ctx.enter_context(nc.psum_tensor(n, [128, 512], F32))

    din = lambda n, s: nc.dram_tensor(n, s, F32, kind="ExternalInput").ap()
    xin = din("xin", [D, S])
    wqb_d = din("wqb", [NH, 128, DT * 128])
    wkb_d = din("wkb", [128, DT * 128])
    wvb_d = din("wvb", [128, DT * 128])
    wo_d = din("wo", [D, D])
    wfcb_d = din("wfcb", [NB, 128, DT * BH])
    wpjb_d = din("wpjb", [NB, 128, 2 * D])
    c_d = din("cvec", [16, 128])
    wmod_d = din("wmodc", [D, MODW])
    ident_d = nc.dram_tensor("ident", [128, 128], BF16, kind="ExternalInput").ap()
    out_d = nc.dram_tensor("out", [D, S], F32, kind="ExternalOutput").ap()

    if debug:
        dbg = {n: nc.dram_tensor("dbg_" + n, sh, dt, kind="ExternalOutput").ap()
               for n, sh, dt in [
                   ("gates", [128, 96], F32), ("h1", [128, 1024], BF16),
                   ("qt0", [128, 1024], BF16), ("kt", [128, 2048], BF16),
                   ("vaug", [128, 258], BF16), ("ao0", [128, 1024], BF16),
                   ("x1", [128, 1024], F32), ("h2", [128, 1024], BF16),
                   ("wq1b", [128, 4096], BF16), ("acc0", [128, 1024], F32),
                   ("rstd3", [1, 1024], F32), ("wq2b", [128, 4096], BF16),
               ]}
    mod_in = nc.dram_tensor("mod_in", [1, MODW], F32).ap()
    mod_out = nc.dram_tensor("mod_out", [NCORES, MODW], F32).ap()
    kt_in = nc.dram_tensor("kt_in", [128, S], BF16).ap()
    kt_out = nc.dram_tensor("kt_out", [NCORES * 128, S], BF16).ap()
    vn_in = nc.dram_tensor("vn_in", [S, 128], BF16).ap()
    vn_out = nc.dram_tensor("vn_out", [SEQ, 128], BF16).ap()
    x1_sp = nc.dram_tensor("x1_spill", [D, S], F32).ap()
    ssc_fc_d = nc.dram_tensor("ssc_fc", [128, 1024], BF16).ap()
    tlo_fc_d = nc.dram_tensor("tlo_fc", [128, 1024], BF16).ap()
    ssc_pj_d = nc.dram_tensor("ssc_pj", [128, 2048], BF16).ap()
    tlo_pj_d = nc.dram_tensor("tlo_pj", [128, 2048], BF16).ap()

    xT = sbuf("xT", [128, DT, S], F32)
    hh = sbuf("hh", [128, DT, S], BF16)
    AR = sbuf("AR", [128, 24576], F32)
    ident = sbuf("ident_s", [128, 128], BF16)
    identf = sbuf("identf", [128, 128], F32)
    ones1 = sbuf("ones1", [128, 1], BF16)
    onesf = sbuf("onesf", [1, 128], F32)
    G2 = sbuf("G2", [128, 2], BF16)
    maskI = sbuf("maskI", [128, 1], I32)
    maskA32 = sbuf("maskA32", [128, 1], I32)
    maskA16 = sbuf("maskA16", [128, 1], I16)
    maskPK = sbuf("maskPK", [128, 1], I32)
    modT = sbuf("modT", [128, 96], F32)
    modraw = sbuf("modraw", [96, 128], F32)
    gates = sbuf("gates", [128, 96], F32)
    silu_s = sbuf("silu_s", [16, 128], F32)
    siluT = sbuf("siluT", [128, 16], F32)
    siluh = sbuf("siluh", [128, 16], BF16)
    silul = sbuf("silul", [128, 16], BF16)
    silul_f = sbuf("silul_f", [128, 16], F32)
    statacc = sbuf("statacc", [1, S], F32)
    zt = sbuf("zt", [128, 512], F32)
    rr = sbuf("rr", [128, 512], F32)
    rr2 = sbuf("rr2", [128, 512], F32)
    hsqT = sbuf("hsqT", [128, 1024], BF16)
    ao_inv = sbuf("ao_inv", [128, 4], F32)

    def arv(off, nf32, dtype=F32):
        ap = AR[:, off:off + nf32]
        return ap.bitcast(dtype) if dtype != F32 else ap

    wms = [arv(0, 1536), arv(1536, 1536)]
    wmh = arv(3072, 768, BF16)
    wml = arv(3840, 768, BF16)
    xb_t = [arv(4608, 512, BF16), arv(5120, 512, BF16)]
    xsq_t = [arv(5632, 512, BF16), arv(6144, 512, BF16)]
    QT = arv(0, 8192, BF16)
    KT = arv(8192, 4096, BF16)
    Vaug = arv(12288, 4128, BF16)
    probs = arv(16416, 1024, BF16)
    wqh = [arv(17440, 1024, BF16), arv(18464, 1024, BF16)]
    kv_ev = arv(19488, 1024, BF16)
    vnat = arv(20512, 512, BF16)
    wo_b = [arv(0, 1024, BF16), arv(1024, 1024, BF16)]
    wbs1 = [arv(0, 2048, BF16), arv(2048, 2048, BF16)]
    wbs2 = [arv(4096, 2048, BF16), arv(6144, 2048, BF16)]
    wq1 = [arv(8192, 2048, BF16), arv(10240, 2048, BF16)]
    wq2 = arv(12288, 4096, BF16)
    hsb = arv(16384, 2048, BF16)
    scb = arv(18432, 1024, BF16)
    scb2 = arv(19456, 1024, BF16)
    sct = arv(20480, 1024)
    ab_t = arv(21504, 1024, BF16)
    bc_s = arv(22528, 1024, BF16)
    bc_t = arv(23552, 1024, BF16)
    x1rel = [arv(0, 1024), arv(1024, 1024)]
    p0tmpA = arv(6656, 1024)
    p0tmpB = arv(7680, 512)
    aox = arv(23072, 256, BF16)
    wkb = arv(21024, 1024, BF16)
    wvb = arv(22048, 1024, BF16)
    stat = arv(6656, 512)[0:1, :]
    stat2 = arv(7168, 512)[0:1, :]
    stat3 = arv(7680, 512)[0:1, :]
    zt0 = zt

    pA, pB, pC, pD = psum("pA"), psum("pB"), psum("pC"), psum("pD")
    pE, pF, pG, pH = psum("pE"), psum("pF"), psum("pG"), psum("pH")
    pAb = pA[:, 0:64].bitcast(BF16)
    pBb = pB[:, 0:64].bitcast(BF16)

    for s in ("ld", "gq", "sq", "pe", "ve", "ac", "cc", "ms", "lc", "lx",
              "lm0", "lm1", "lq0", "lq1", "lo0", "lo1", "lf0", "lf1",
              "lp0", "lp1", "lr0", "lr1", "dbgs", "db1", "db2", "db3", "db4", "db5", "lmr", "lkv", "lsc", "sq0", "sq1"):
        fl.sem(s)

    SY, PE, VE, SC, GP = "sync", "tensor", "vector", "scalar", "gpsimd"
    W, DMA, OP, MM = fl.wait, fl.dma, fl.op, fl.mm

    OP(VE, "memset", ones1[:, :], 1.0, sem="ms")
    OP(VE, "memset", onesf[:, :], 1.0, sem="ms")
    OP(VE, "memset", G2[0:64, 0:1], 1.0, sem="ms")
    OP(VE, "memset", G2[64:128, 0:1], 0.0, sem="ms")
    OP(VE, "memset", G2[0:64, 1:2], 0.0, sem="ms")
    OP(VE, "memset", G2[64:128, 1:2], 1.0, sem="ms")
    OP(VE, "memset", maskI[:, :], -65536, sem="ms")
    OP(VE, "memset", maskA32[:, :], 2147483647, sem="ms")
    OP(VE, "memset", maskA16[:, :], 32767, sem="ms")
    OP(VE, "memset", maskPK[:, :], 2147450879, sem="ms")
    OP(VE, "memset", Vaug[:, :], 1.0, sem="ms")
    OP(VE, "memset", statacc[:, :], 0.0, sem="ms")
    ms_const = fl.cnt["ms"]
    c_id = DMA(SY, "lc", ident[:, :], ident_d[:, :])
    c_c = DMA(SY, "lc", silu_s[:, :], c_d[:, :])
    W(VE, "lc", 32)
    OP(VE, "tensor_copy", identf[:, :], ident[:, :], sem="ve")

    W(PE, "ms", ms_const)
    W(VE, "ms", ms_const)
    W(SC, "ms", ms_const)
    W(GP, "ms", ms_const)
    W(PE, "lc", 32)
    W(PE, "ve", fl.cnt["ve"])
    OP(PE, "transpose", pA[:, 0:16], silu_s[:, :], identf[0:16, 0:16], sem="pe")
    W(VE, "pe", fl.cnt["pe"])
    OP(VE, "tensor_copy", siluT[:, :], pA[:, 0:16], sem="ve")
    W(SC, "ve", fl.cnt["ve"])
    OP(SC, "activation", siluT[:, :], siluT[:, :], ACT.Silu, sem="ac")
    W(VE, "ac", fl.cnt["ac"])
    OP(VE, "tensor_copy", siluh[:, :], siluT[:, :])
    OP(VE, "tensor_copy", silul_f[:, :], siluh[:, :])
    OP(VE, "tensor_tensor", silul_f[:, :], siluT[:, :], silul_f[:, :], A.subtract)
    OP(VE, "tensor_copy", silul[:, :], silul_f[:, :], sem="ve")
    ve_silu = fl.cnt["ve"]

    pe_modslab = [None] * DT
    for t in range(DT):
        if t >= 2:
            W(SY, "pe", pe_modslab[t - 2])
        c_wm = DMA(SY, "lm0" if t % 2 == 0 else "lm1", wms[t % 2][:, :],
                   wmod_d[128 * t:128 * (t + 1), :])
        W(VE, "lm0" if t % 2 == 0 else "lm1", c_wm)
        if t >= 1:
            W(VE, "pe", pe_modslab[t - 1])
        OP(VE, "tensor_copy", wmh[:, :], wms[t % 2][:, :])
        OP(VE, "tensor_copy", p0tmpA[:, :], wmh[:, 0:1024])
        OP(VE, "tensor_copy", p0tmpB[:, :], wmh[:, 1024:1536])
        OP(VE, "tensor_tensor", p0tmpA[:, :], wms[t % 2][:, 0:1024], p0tmpA[:, :],
           A.subtract)
        OP(VE, "tensor_tensor", p0tmpB[:, :], wms[t % 2][:, 1024:1536], p0tmpB[:, :],
           A.subtract)
        OP(VE, "tensor_copy", wml[:, 0:1024], p0tmpA[:, :])
        OP(VE, "tensor_copy", wml[:, 1024:1536], p0tmpB[:, :], sem="ve")
        W(PE, "ve", fl.cnt["ve"])
        if t == 0:
            W(PE, "ve", ve_silu)
        for ci, bank in ((0, pB), (1, pC), (2, pD)):
            n0 = 512 * ci
            MM(bank[0:1, :], siluh[:, t:t + 1], wmh[:, n0:n0 + 512],
               start=(t == 0), stop=False)
            MM(bank[0:1, :], siluh[:, t:t + 1], wml[:, n0:n0 + 512],
               start=False, stop=False)
            pe_modslab[t] = MM(bank[0:1, :], silul[:, t:t + 1], wmh[:, n0:n0 + 512],
                               start=False, stop=(t == DT - 1), sem="pe")
    pe_mod = fl.cnt["pe"]
    W(VE, "pe", pe_mod)
    OP(VE, "tensor_copy", stat[0:1, :], pB[0:1, :])
    OP(VE, "tensor_copy", stat2[0:1, :], pC[0:1, :])
    OP(VE, "tensor_copy", stat3[0:1, :], pD[0:1, :], sem="ve")
    ve_modev = fl.cnt["ve"]
    gq_modw = None
    W(GP, "ve", ve_modev)
    for ci, src in ((0, stat), (1, stat2), (2, stat3)):
        gq_modw = DMA(GP, "gq", mod_in[0:1, 512 * ci:512 * (ci + 1)], src[0:1, :])
    W(GP, "gq", gq_modw)
    cc_mod = OP(GP, "collective_compute", "AllGather", mybir.AluOpType.bypass,
                replica_groups=[list(range(NCORES))],
                ins=[mod_in], outs=[mod_out], sem="cc")
    W(GP, "cc", cc_mod)
    gq_modr = DMA(GP, "lmr", modraw[:, :],
                  mod_out.rearrange("a b -> (a b)").rearrange("(r c) -> r c", c=128))
    W(PE, "lmr", gq_modr)
    OP(PE, "transpose", pA[:, 0:96], modraw[:, :], identf[0:96, 0:96], sem="pe")
    pe_modT = fl.cnt["pe"]
    W(VE, "pe", pe_modT)
    OP(VE, "tensor_copy", modT[:, :], pA[:, 0:96])
    OP(VE, "tensor_scalar", gates[:, 0:16], modT[:, 16:32], 1.0, None, A.add)
    OP(VE, "tensor_copy", gates[:, 16:32], modT[:, 0:16])
    OP(VE, "tensor_scalar", gates[:, 48:64], modT[:, 64:80], 1.0, None, A.add)
    OP(VE, "tensor_copy", gates[:, 64:80], modT[:, 48:64])
    for (gsrc, gdst) in ((modT[:, 32:48], gates[:, 32:48]),
                         (modT[:, 80:96], gates[:, 80:96])):
        OP(VE, "tensor_scalar", zt0[:, 0:16], gsrc, 0.0, 2.0, A.is_ge, A.mult)
        OP(VE, "tensor_scalar", zt0[:, 0:16], zt0[:, 0:16], 1.0, None, A.subtract)
        OP(VE, "tensor_scalar", zt0[:, 16:32].bitcast(I32), gsrc.bitcast(I32),
           maskA32[:, :], None, A.bitwise_and)
        OP(VE, "tensor_scalar", zt0[:, 16:32], zt0[:, 16:32], 0.25, None, A.max)
        OP(VE, "tensor_tensor", gdst, zt0[:, 0:16], zt0[:, 16:32], A.mult)
    OP(VE, "tensor_copy", zt0[:, 0:1], zt0[:, 0:1], sem="ve")
    ve_gates = fl.cnt["ve"]

    c_x = [DMA(SY, "lx", xT[:, t, :], xin[128 * t:128 * (t + 1), :]) for t in range(DT)]
    pe_stat = [None] * DT
    for t in range(DT):
        W(VE, "lx", c_x[DT - 1])
        if t >= 2:
            W(VE, "pe", pe_stat[t - 2])
        OP(VE, "tensor_copy", xb_t[t % 2][:, :], xT[:, t, :], sem="ve")
        W(SC, "ve", fl.cnt["ve"])
        OP(SC, "activation", xsq_t[t % 2][:, :], xb_t[t % 2][:, :], ACT.Square, sem="ac")
        W(PE, "ac", fl.cnt["ac"])
        MM(pE[0:1, :], ones1[:, :], xb_t[t % 2][:, 0:512], start=(t == 0), stop=(t == DT - 1))
        MM(pF[0:1, :], ones1[:, :], xb_t[t % 2][:, 512:1024], start=(t == 0), stop=(t == DT - 1))
        MM(pG[0:1, :], ones1[:, :], xsq_t[t % 2][:, 0:512], start=(t == 0), stop=(t == DT - 1))
        pe_stat[t] = MM(pH[0:1, :], ones1[:, :], xsq_t[t % 2][:, 512:1024],
                        start=(t == 0), stop=(t == DT - 1), sem="pe")

    def ln_apply(gcol0, pre_pe_waits):
        for c in range(2):
            bmu = pE if c == 0 else pF
            bms = pG if c == 0 else pH
            W(VE, "pe", fl.cnt["pe"])
            OP(VE, "tensor_scalar", stat[0:1, :], bmu[0:1, :], 1.0 / D, None, A.mult)
            OP(VE, "tensor_scalar", stat2[0:1, :], bms[0:1, :], 1.0 / D, None, A.mult)
            OP(VE, "tensor_tensor", stat3[0:1, :], stat[0:1, :], stat[0:1, :], A.mult)
            OP(VE, "tensor_tensor", stat2[0:1, :], stat2[0:1, :], stat3[0:1, :],
               A.subtract, sem="ve")
            OP(VE, "tensor_scalar", stat2[0:1, :], stat2[0:1, :], LN_EPS, None, A.add,
               sem="ve")
            W(SC, "ve", fl.cnt["ve"])
            OP(SC, "activation", stat2[0:1, :], stat2[0:1, :], ACT.Ln)
            OP(SC, "activation", stat2[0:1, :], stat2[0:1, :], ACT.Exp, scale=-0.5,
               sem="ac")
            for sname, v in pre_pe_waits:
                W(PE, sname, v)
            W(PE, "ac", fl.cnt["ac"])
            MM(pA[:, :], onesf[:, :], stat[0:1, :], start=True, stop=True)
            MM(pB[:, :], onesf[:, :], stat2[0:1, :], start=True, stop=True, sem="pe")
            pe_bc = fl.cnt["pe"]
            for t in range(DT):
                W(VE, "pe", pe_bc)
                OP(VE, "tensor_tensor", zt0[:, :], xT[:, t, 512 * c:512 * (c + 1)],
                   pA[:, :], A.subtract)
                OP(VE, "tensor_tensor", zt0[:, :], zt0[:, :], pB[:, :], A.mult)
                OP(VE, "scalar_tensor_tensor", hh[:, t, 512 * c:512 * (c + 1)],
                   zt0[:, :], gates[:, gcol0 + t:gcol0 + t + 1],
                   gates[:, gcol0 + 16 + t:gcol0 + 16 + t + 1].broadcast_to([128, 512]),
                   A.mult, A.add, sem="ve")
            W(PE, "ve", fl.cnt["ve"])

    W(VE, "ve", ve_gates)
    ln_apply(0, [("gq", gq_modw), ("ve", ve_gates)])
    if debug:
        W(SY, "ve", ve_gates)
        DMA(SY, "db1", dbg["gates"], gates[:, :])
    ve_h1 = fl.cnt["ve"]
    d_h1 = 0
    if debug:
        W(SY, "ve", ve_h1)
        d_h1 = DMA(SY, "db1", dbg["h1"], hh[:, 0, :])

    c_wk = DMA(GP, "lkv", wkb[:, :], wkb_d[:, :])
    c_wv = DMA(GP, "lkv", wvb[:, :], wvb_d[:, :])
    c_wq = [None] * NH
    for h in range(2):
        c_wq[h] = DMA(GP, "lq0" if h % 2 == 0 else "lq1", wqh[h][:, :], wqb_d[h, :, :])
    ac_qev = [None] * NH
    W(SC, "ve", fl.cnt["ve"])
    for h in range(NH):
        W(PE, "lq0" if h % 2 == 0 else "lq1", c_wq[h])
        if h == 0:
            W(PE, "ve", ve_h1)
        pe_q = None
        for c, bank in ((0, pE), (1, pF)):
            for t in range(DT):
                pe_q = MM(bank[:, :], wqh[h % 2][:, 128 * t:128 * (t + 1)],
                          hh[:, t, 512 * c:512 * (c + 1)], start=(t == 0),
                          stop=(t == DT - 1),
                          sem=("pe" if (c == 1 and t == DT - 1) else None))
        if h + 2 < NH:
            W(GP, "pe", pe_q)
            c_wq[h + 2] = DMA(GP, "lq0" if h % 2 == 0 else "lq1", wqh[h % 2][:, :],
                              wqb_d[h + 2, :, :])
        W(SC, "pe", pe_q)
        OP(SC, "activation", QT[:, 1024 * h:1024 * h + 512], pE[:, :], ACT.Copy)
        ac_qev[h] = OP(SC, "activation", QT[:, 1024 * h + 512:1024 * h + 1024], pF[:, :],
                       ACT.Copy, sem="ac")
        if h + 1 < NH:
            W(PE, "ac", ac_qev[h])
    W(PE, "lkv", 32)
    pe_k = None
    for c, bank in ((0, pG), (1, pH)):
        for t in range(DT):
            pe_k = MM(bank[:, :], wkb[:, 128 * t:128 * (t + 1)],
                      hh[:, t, 512 * c:512 * (c + 1)], start=(t == 0), stop=(t == DT - 1),
                      sem=("pe" if (c == 1 and t == DT - 1) else None))
    W(SC, "pe", pe_k)
    OP(SC, "activation", kv_ev[:, 0:512], pG[:, :], ACT.Copy)
    ac_kev = OP(SC, "activation", kv_ev[:, 512:1024], pH[:, :], ACT.Copy, sem="ac")
    W(PE, "ac", ac_kev)
    pe_v = None
    for c, bank in ((0, pG), (1, pH)):
        for t in range(DT):
            pe_v = MM(bank[:, :], wvb[:, 128 * t:128 * (t + 1)],
                      hh[:, t, 512 * c:512 * (c + 1)], start=(t == 0), stop=(t == DT - 1),
                      sem=("pe" if (c == 1 and t == DT - 1) else None))
    W(SC, "pe", pe_v)
    OP(SC, "activation", kv_ev[:, 1024:1536], pG[:, :], ACT.Copy)
    ac_vev = OP(SC, "activation", kv_ev[:, 1536:2048], pH[:, :], ACT.Copy, sem="ac")
    W(GP, "ac", ac_kev)
    gq_k = DMA(GP, "gq", kt_in[:, :], kv_ev[:, 0:1024])
    W(PE, "ac", ac_vev)
    ve_vn = None
    for i in range(8):
        tb = pAb if i % 2 == 0 else pBb
        pe_tv = OP(PE, "transpose", tb,
                   kv_ev[:, 1024 + 128 * i:1024 + 128 * (i + 1)], ident[:, :], sem="pe")
        W(VE, "pe", pe_tv)
        ve_vn = OP(VE, "tensor_copy", vnat[:, 128 * i:128 * (i + 1)], tb,
                   sem="ve")
        W(PE, "ve", ve_vn)
    W(GP, "ve", ve_vn)
    gq_v = DMA(GP, "gq", vn_in.rearrange("(i p) j -> p i j", p=128),
               vnat[:, :].rearrange("p (i j) -> p i j", j=128))
    W(GP, "gq", gq_v)
    cc_k = OP(GP, "collective_compute", "AllGather", mybir.AluOpType.bypass,
              replica_groups=[list(range(NCORES))], ins=[kt_in], outs=[kt_out], sem="cc")
    cc_v = OP(GP, "collective_compute", "AllGather", mybir.AluOpType.bypass,
              replica_groups=[list(range(NCORES))], ins=[vn_in], outs=[vn_out], sem="cc")
    W(GP, "cc", cc_v)
    gq_kt = None
    for j in range(NCORES):
        gq_kt = DMA(GP, "gq", KT[:, 1024 * j:1024 * (j + 1)], kt_out[128 * j:128 * (j + 1), :])
    for tt in range(TT):
        gq_kt = DMA(GP, "gq", Vaug[:, 129 * tt:129 * tt + 128],
                    vn_out[128 * tt:128 * (tt + 1), :])
    gq_kv_done = gq_kt
    if debug:
        W(SY, "gq", gq_kv_done)
        W(SY, "ac", ac_qev[0])
        DMA(SY, "db2", dbg["qt0"], QT[:, 0:1024])
        DMA(SY, "db2", dbg["kt"], KT[:, 0:2048])
        d_kv = DMA(SY, "db2", dbg["vaug"], Vaug[:, 0:258])

    W(PE, "gq", gq_kv_done)
    W(PE, "ms", ms_const)
    if debug:
        W(VE, "db1", d_h1)
    avb = (pE, pF, pG, pH)
    last_av_slot = {}
    last_exp_bank = {}
    for h in range(NH):
        for c in range(2):
            qs = QT[:, 1024 * h + 512 * c:1024 * h + 512 * (c + 1)]
            av_pe = {}
            exp_local = {}
            for tt in range(TT):
                bank_i = tt % 2
                sb_bank = pC if bank_i == 0 else pD
                if tt >= 2:
                    W(PE, "ac", exp_local[tt - 2])
                else:
                    W(PE, "ac", last_exp_bank.get(bank_i))
                MM(sb_bank[:, :], KT[:, 128 * tt:128 * (tt + 1)], qs, start=True,
                   stop=True, sem="pe")
                pe_sc = fl.cnt["pe"]
                slot = tt % 4
                W(SC, "pe", pe_sc)
                W(SC, "pe", last_av_slot.get(slot))
                OP(SC, "activation", probs[:, 512 * slot:512 * (slot + 1)], sb_bank[:, :],
                   ACT.Exp, scale=SCALE, sem="ac")
                exp_local[tt] = fl.cnt["ac"]
                last_exp_bank[bank_i] = exp_local[tt]
                W(PE, "ac", exp_local[tt])
                for sl in range(4):
                    av_pe[tt] = MM(
                        avb[sl][:, 0:129],
                        probs[:, 512 * slot + 128 * sl:512 * slot + 128 * (sl + 1)],
                        Vaug[:, 129 * tt:129 * (tt + 1)],
                        start=(tt == 0), stop=(tt == TT - 1),
                        sem=("pe" if sl == 3 else None))
                last_av_slot[slot] = av_pe[tt]
            W(VE, "pe", av_pe[TT - 1])
            for sl in range(4):
                OP(VE, "reciprocal", ao_inv[:, sl:sl + 1], avb[sl][:, 128:129],
                   sem=("ve" if sl == 3 else None))
            W(SC, "ve", fl.cnt["ve"])
            ac_ao = None
            for sl in range(4):
                ac_ao = OP(SC, "activation", aox[:, 128 * sl:128 * (sl + 1)],
                           avb[sl][:, 0:128], ACT.Copy, scale=ao_inv[:, sl:sl + 1],
                           sem=("ac" if sl == 3 else None))
            W(PE, "ac", ac_ao)
            ve_ao = None
            ve_ao_sl = {}
            for sl in range(4):
                tb = pAb if sl % 2 == 0 else pBb
                if sl >= 2:
                    W(PE, "ve", ve_ao_sl[sl - 2])
                pe_tr = OP(PE, "transpose", tb, aox[:, 128 * sl:128 * (sl + 1)],
                           ident[:, :], sem="pe")
                W(VE, "pe", pe_tr)
                ve_ao = OP(VE, "tensor_copy",
                           hh[:, h, 512 * c + 128 * sl:512 * c + 128 * (sl + 1)],
                           tb, sem="ve")
                ve_ao_sl[sl] = ve_ao
            W(PE, "ve", ve_ao)

    W(PE, "ac", fl.cnt["ac"])
    W(PE, "ve", fl.cnt["ve"])
    d_ao0 = 0
    if debug:
        W(SY, "ve", fl.cnt["ve"])
        d_ao0 = DMA(SY, "db3", dbg["ao0"], hh[:, 0, :])
    banks8 = (pA, pB, pC, pD, pE, pF, pG, pH)
    order = [(c, half) for c in range(2) for half in range(2)]
    slab_ids = [(r, q) for r in range(4) for q in range(DT)]
    c_wo = {}
    W(GP, "pe", fl.cnt["pe"])
    W(GP, "ac", fl.cnt["ac"])
    W(GP, "ve", fl.cnt["ve"])
    if debug:
        W(GP, "db2", d_kv)
    for i in range(2):
        r, q = slab_ids[i]
        c_wo[(r, q)] = DMA(GP, "lo0" if i % 2 == 0 else "lo1", wo_b[i % 2][:, :],
                           wo_d[128 * q:128 * (q + 1), :])
    idx = 0
    ve_res = None
    for r, (c, half) in enumerate(order):
        for q in range(DT):
            W(PE, "lo0" if idx % 2 == 0 else "lo1", c_wo[(r, q)])
            pe_woq = None
            for k in range(8):
                pe_woq = MM(banks8[k][:, :],
                            wo_b[idx % 2][:, 1024 * half + 128 * k:1024 * half + 128 * (k + 1)],
                            hh[:, q, 512 * c:512 * (c + 1)], start=(q == 0),
                            stop=(q == DT - 1), sem=("pe" if k == 7 else None))
            if idx + 2 < len(slab_ids):
                r2, q2 = slab_ids[idx + 2]
                W(GP, "pe", pe_woq)
                c_wo[(r2, q2)] = DMA(GP, "lo0" if idx % 2 == 0 else "lo1",
                                     wo_b[idx % 2][:, :],
                                     wo_d[128 * q2:128 * (q2 + 1), :])
            idx += 1
        pe_wor = fl.cnt["pe"]
        ve_k2 = {}
        for k in range(8):
            dt_out = 8 * half + k
            rtmp = rr if k % 2 == 0 else rr2
            W(SC, "pe", pe_wor)
            W(SC, "ve", ve_k2.get(k - 2))
            ac_r = OP(SC, "activation", rtmp[:, :], banks8[k][:, :], ACT.Copy,
                      scale=gates[:, 32 + dt_out:32 + dt_out + 1], sem="ac")
            W(VE, "ac", ac_r)
            ve_res = OP(VE, "tensor_tensor", xT[:, dt_out, 512 * c:512 * (c + 1)],
                        xT[:, dt_out, 512 * c:512 * (c + 1)], rtmp[:, :], A.add, sem="ve")
            ve_k2[k] = ve_res
        W(PE, "ve", ve_res)

    d_x1m = 0
    if debug:
        W(SY, "ve", ve_res)
        d_x1m = DMA(SY, "db3", dbg["x1"], xT[:, 0, :])
    xb2 = wbs1
    xq2 = wbs2
    if debug:
        W(VE, "db3", max(d_ao0, d_x1m))
    pe_stat2 = [None] * DT
    for t in range(DT):
        W(VE, "ve", ve_res)
        if t >= 2:
            W(VE, "pe", pe_stat2[t - 2])
        OP(VE, "tensor_copy", xb2[t % 2][:, 0:1024], xT[:, t, :], sem="ve")
        W(SC, "ve", fl.cnt["ve"])
        OP(SC, "activation", xq2[t % 2][:, 0:1024], xb2[t % 2][:, 0:1024], ACT.Square,
           sem="ac")
        W(PE, "ac", fl.cnt["ac"])
        MM(pE[0:1, :], ones1[:, :], xb2[t % 2][:, 0:512], start=(t == 0), stop=(t == DT - 1))
        MM(pF[0:1, :], ones1[:, :], xb2[t % 2][:, 512:1024], start=(t == 0), stop=(t == DT - 1))
        MM(pG[0:1, :], ones1[:, :], xq2[t % 2][:, 0:512], start=(t == 0), stop=(t == DT - 1))
        pe_stat2[t] = MM(pH[0:1, :], ones1[:, :], xq2[t % 2][:, 512:1024],
                         start=(t == 0), stop=(t == DT - 1), sem="pe")
    ln_apply(48, [("ve", fl.cnt["ve"])])
    ve_h2 = fl.cnt["ve"]
    if debug:
        W(SY, "ve", ve_h2)
        DMA(SY, "db4", dbg["h2"], hh[:, 0, :])
    gq_spill = None
    W(GP, "ve", ve_h2)
    for t in range(DT):
        gq_spill = DMA(GP, "gq", x1_sp[128 * t:128 * (t + 1), :], xT[:, t, :])
    W(VE, "gq", gq_spill)
    OP(VE, "tensor_scalar", xT[:, :, :], xT[:, :, :], 0.0, None, A.mult, sem="ve")
    ve_zeroacc = fl.cnt["ve"]

    def quant(kind, wbstage, wqdst):
        sscd, tlod = (ssc_fc_d, tlo_fc_d) if kind == "fc" else (ssc_pj_d, tlo_pj_d)
        wide = 1024 if kind == "fc" else 2048
        OP(VE, "tensor_scalar", ab_t[:, :].bitcast(I32), wbstage[:, 0:2048].bitcast(I32),
           maskPK[:, :], None, A.bitwise_and)
        OP(VE, "tensor_scalar", scb[:, :].bitcast(I32), wbstage[:, 2048:4096].bitcast(I32),
           maskPK[:, :], None, A.bitwise_and, sem="ve")
        W(PE, "ve", fl.cnt["ve"])
        sb4 = (pA, pB, pC, pD)
        pe_sc_mm = None
        if kind == "fc":
            for t in range(DT):
                src = ab_t if t < 8 else scb
                rh = src[:, 256 * (t % 8):256 * (t % 8 + 1)]
                j = t % 4
                pe_sc_mm = MM(sb4[t // 4][32 * j:32 * j + 2, 0:256], G2[:, :], rh,
                              start=True, stop=True, tile_position=(0, 32 * j),
                              sem=("pe" if t == DT - 1 else None))
        else:
            for ht in range(2):
                src = ab_t if ht == 0 else scb
                for chn in range(4):
                    j = chn // 2
                    pe_sc_mm = MM(sb4[2 * ht + chn % 2][32 * j:32 * j + 2, 0:512],
                                  G2[:, :], src[:, 512 * chn:512 * (chn + 1)],
                                  start=True, stop=True, tile_position=(0, 32 * j),
                                  sem=("pe" if (ht == 1 and chn == 3) else None))
        W(VE, "pe", pe_sc_mm)
        seg = wide // 4
        for i in range(4):
            OP(VE, "tensor_scalar", scb2[:, seg * i:seg * (i + 1)], sb4[i][:, 0:seg],
               1.0 / 64.0, 1e-8, A.mult, A.max)
        for hw in range(wide // 1024):
            o = 1024 * hw
            OP(VE, "tensor_scalar", sct[:, :], scb2[:, o:o + 1024], THR, None, A.mult)
            OP(VE, "tensor_scalar", sct[:, :].bitcast(I32), sct[:, :].bitcast(I32),
               maskI[:, :], None, A.bitwise_and)
            OP(VE, "tensor_copy", scb[:, o:o + 1024], sct[:, :])
        OP(VE, "tensor_copy", zt0[:, 0:1], zt0[:, 0:1], sem="ve")
        ve_rows = fl.cnt["ve"]
        W(GP, "ve", ve_rows)
        DMA(GP, "lsc", sscd[:, 0:wide], scb2[:, 0:wide])
        c_rows = DMA(GP, "lsc", tlod[:, 0:wide], scb[:, 0:wide])
        W(GP, "lsc", c_rows)
        ve_q = None
        for half in range(2):
            if half == 1:
                W(GP, "ve", fl.cnt["ve"])
            gq_b = None
            for (dr_, dst) in ((sscd, bc_s), (tlod, bc_t)):
                for u in range(2):
                    if kind == "fc":
                        for bl in range(2):
                            bg = 2 * half + bl
                            src = dr_[:, 0:1024].rearrange(
                                "(j z) (b n) -> j z b n", j=4, b=4)
                            src = src[:, u:u + 1, bg:bg + 1].rearrange(
                                "j z b n -> (z b) j n")
                            src = src.broadcast_to([64, 4, 256])
                            dstv = dst[64 * u:64 * (u + 1),
                                       1024 * bl:1024 * (bl + 1)].rearrange(
                                "q (j n) -> q j n", j=4)
                            gq_b = DMA(GP, "gq", dstv, src)
                    else:
                        for c2 in range(2):
                            src = dr_[:, 0:2048].rearrange(
                                "(c2 z) (c1h n) -> c2 z c1h n", c2=4, c1h=4)
                            src = src[c2:c2 + 1, u:u + 1,
                                      2 * half:2 * half + 2].rearrange(
                                "c2 z c1 n -> (c2 z) c1 n")
                            src = src.broadcast_to([64, 2, 512])
                            dstv = dst[64 * u:64 * (u + 1),
                                       1024 * c2:1024 * (c2 + 1)].rearrange(
                                "q (c1 n) -> q c1 n", c1=2)
                            gq_b = DMA(GP, "gq", dstv, src)
            W(VE, "gq", gq_b)
            wb_h = wbstage[:, 2048 * half:2048 * (half + 1)]
            gt = ab_t[:, 0:2048]
            L = sct[:, :].bitcast(BF16)
            nw = scb2[:, 0:2048]
            OP(VE, "tensor_scalar", nw, wb_h, -1.0, None, A.mult)
            OP(VE, "tensor_tensor", gt, wb_h, bc_t[:, :], A.is_gt)
            OP(VE, "tensor_tensor", L, nw, bc_t[:, :], A.is_gt)
            OP(VE, "tensor_tensor", gt, gt, L, A.subtract)
            ve_q = OP(VE, "tensor_tensor", wqdst[:, 2048 * half:2048 * (half + 1)],
                      gt, bc_s[:, :], A.mult, sem="ve")
        return ve_q

    W(GP, "ve", ve_zeroacc)
    W(GP, "pe", fl.cnt["pe"])
    W(GP, "ac", fl.cnt["ac"])
    c_fc = [None] * NB
    c_pj = [None] * NB
    for i in range(2):
        c_fc[i] = DMA(GP, "lf0" if i % 2 == 0 else "lf1", wbs1[i][:, :], wfcb_d[i, :, :])
        c_pj[i] = DMA(GP, "lp0" if i % 2 == 0 else "lp1", wbs2[i][:, :], wpjb_d[i, :, :])
    ve_q2 = [None] * NB
    pe_fc = [None] * NB
    pe_pj = [None] * (NB // 2)
    gq_acc = [None] * (NB // 2)
    ac_relu = {}
    last_hsq_mm = {}
    for b in range(NB):
        sbx = b % 2
        if b >= 2:
            W(VE, "pe", pe_pj[b // 2 - 1])
            W(VE, "pe", pe_fc[b - 2])
        if debug and b == 2:
            W(VE, "db4", d_wq)
        W(VE, "lf0" if b % 2 == 0 else "lf1", c_fc[b])
        ve_q1 = quant("fc", wbs1[sbx], wq1[sbx])
        W(VE, "lp0" if b % 2 == 0 else "lp1", c_pj[b])
        ve_q2[b] = quant("pj", wbs2[sbx], wq2[:, 4096 * sbx:4096 * (sbx + 1)])
        if debug and b == 0:
            W(SY, "ve", ve_q2[b])
            d_wq = DMA(SY, "db4", dbg["wq1b"], wq1[0][:, :])
            d_wq = DMA(SY, "db4", dbg["wq2b"], wq2[:, 0:4096])

        if b + 2 < NB:
            W(GP, "ve", ve_q2[b])
            c_fc[b + 2] = DMA(GP, "lf0" if sbx == 0 else "lf1", wbs1[sbx][:, :],
                              wfcb_d[b + 2, :, :])
            c_pj[b + 2] = DMA(GP, "lp0" if sbx == 0 else "lp1", wbs2[sbx][:, :],
                              wpjb_d[b + 2, :, :])
        W(PE, "ve", ve_q1)
        if b == 0:
            W(PE, "ve", ve_h2)
        gidx = 0
        for ht in range(2):
            for c in range(2):
                bank = pE if gidx % 2 == 0 else pF
                W(PE, "ac", ac_relu.get(gidx - 2))
                pe_h1 = None
                for t in range(DT):
                    pe_h1 = MM(bank[:, :],
                               wq1[sbx][:, 256 * t + 128 * ht:256 * t + 128 * (ht + 1)],
                               hh[:, t, 512 * c:512 * (c + 1)], start=(t == 0),
                               stop=(t == DT - 1), sem=("pe" if t == DT - 1 else None))
                W(SC, "pe", pe_h1)
                ac_r = OP(SC, "activation", rr[:, :], bank[:, :], ACT.Relu, sem="ac")
                ac_relu[gidx] = ac_r
                W(VE, "ac", ac_r)
                W(VE, "pe", last_hsq_mm.get(c))
                hslot = hsb[:, 1024 * (2 * sbx + ht) + 512 * c:
                            1024 * (2 * sbx + ht) + 512 * (c + 1)]
                OP(VE, "tensor_tensor", hslot, rr[:, :], rr[:, :], A.mult)
                OP(VE, "tensor_tensor", hsqT[:, 512 * c:512 * (c + 1)], hslot, hslot,
                   A.mult, sem="ve")
                W(PE, "ve", fl.cnt["ve"])
                last_hsq_mm[c] = MM((pG if c == 0 else pH)[0:1, :], ones1[:, :],
                                    hsqT[:, 512 * c:512 * (c + 1)],
                                    start=(sbx == 0 and ht == 0),
                                    stop=(sbx == 1 and ht == 1), sem="pe")
                gidx += 1
        pe_fc[b] = fl.cnt["pe"]
        if b % 2 == 1:
            sbi = b // 2
            W(VE, "pe", pe_fc[b])
            OP(VE, "tensor_tensor", statacc[0:1, 0:512], pG[0:1, :], statacc[0:1, 0:512],
               A.add)
            OP(VE, "tensor_tensor", statacc[0:1, 512:1024], pH[0:1, :],
               statacc[0:1, 512:1024], A.add, sem="ve")
            ve_st = fl.cnt["ve"]
            W(PE, "ve", ve_st)
            W(PE, "ve", ve_q2[b])
            for rnd in range(8):
                slots = [(2 * rnd + a, cc2) for a in range(2) for cc2 in range(2)]
                pe_prd = None
                for sl, (dtl, c) in enumerate(slots):
                    bank = (pE, pF, pG, pH)[sl]
                    for k in range(4):
                        pe_prd = MM(bank[:, :],
                                    wq2[:, 2048 * k + 128 * dtl:2048 * k + 128 * (dtl + 1)],
                                    hsb[:, 1024 * k + 512 * c:1024 * k + 512 * (c + 1)],
                                    start=(k == 0), stop=(k == 3),
                                    sem=("pe" if (sl == 3 and k == 3) else None))
                W(VE, "pe", pe_prd)
                for sl, (dtl, c) in enumerate(slots):
                    bank = (pE, pF, pG, pH)[sl]
                    gq_acc[sbi] = OP(VE, "tensor_tensor",
                                     xT[:, dtl, 512 * c:512 * (c + 1)], bank[:, :],
                                     xT[:, dtl, 512 * c:512 * (c + 1)], A.add, sem="ve")
                W(PE, "ve", gq_acc[sbi])
            pe_pj[sbi] = fl.cnt["pe"]

    W(VE, "pe", pe_pj[-1])
    W(VE, "ve", gq_acc[-1])
    if debug:
        W(SY, "pe", pe_pj[-1])
        DMA(SY, "db5", dbg["acc0"], xT[:, 0, :])
        d_st0 = DMA(SY, "db5", dbg["rstd3"], statacc[0:1, :])
        W(VE, "db5", d_st0)
    OP(VE, "tensor_scalar", statacc[0:1, :], statacc[0:1, :], 1.0 / HID, None, A.mult,
       sem="ve")
    OP(VE, "tensor_scalar", statacc[0:1, :], statacc[0:1, :], RMS_EPS, None, A.add,
       sem="ve")
    W(SC, "ve", fl.cnt["ve"])
    OP(SC, "activation", statacc[0:1, :], statacc[0:1, :], ACT.Ln)
    OP(SC, "activation", statacc[0:1, :], statacc[0:1, :], ACT.Exp, scale=-0.5, sem="ac")
    W(PE, "ac", fl.cnt["ac"])
    MM(pA[:, :], onesf[:, :], statacc[0:1, 0:512], start=True, stop=True)
    pe_r3 = MM(pB[:, :], onesf[:, :], statacc[0:1, 512:1024], start=True, stop=True,
               sem="pe")
    c_x1 = [None] * DT
    for t in range(2):
        W(SY, "ve", fl.cnt["ve"])
        c_x1[t] = DMA(SY, "lr0" if t % 2 == 0 else "lr1", x1rel[t][:, :],
                      x1_sp[128 * t:128 * (t + 1), :])
    sq_out = None
    last_sq = {}
    for t in range(DT):
        W(VE, "lr0" if t % 2 == 0 else "lr1", c_x1[t])
        W(VE, "pe", pe_r3)
        for c in range(2):
            W(VE, "sq0" if c == 0 else "sq1", last_sq.get(c))
            OP(VE, "tensor_tensor", zt[:, :], xT[:, t, 512 * c:512 * (c + 1)],
               (pA if c == 0 else pB)[:, :], A.mult)
            OP(VE, "scalar_tensor_tensor", rr[:, :] if c == 0 else rr2[:, :], zt[:, :],
               gates[:, 80 + t:80 + t + 1], x1rel[t % 2][:, 512 * c:512 * (c + 1)],
               A.mult, A.add, sem="ve")
            W(SC, "ve", fl.cnt["ve"])
            sq_out = DMA(SC, "sq0" if c == 0 else "sq1",
                         out_d[128 * t:128 * (t + 1), 512 * c:512 * (c + 1)],
                         rr[:, :] if c == 0 else rr2[:, :])
            last_sq[c] = sq_out
        if t + 2 < DT:
            W(SY, "ve", fl.cnt["ve"])
            c_x1[t + 2] = DMA(SY, "lr0" if t % 2 == 0 else "lr1", x1rel[t % 2][:, :],
                              x1_sp[128 * (t + 2):128 * (t + 3), :])
    W(SC, "sq0", last_sq[0])
    W(SC, "sq1", last_sq[1])
    if debug:
        for dsn in ("db1", "db2", "db3", "db4", "db5"):
            W(SC, dsn, fl.cnt[dsn])

    sctx = contextlib.ExitStack()
    if True:
        for s in fl.cnt:
            fl.sems[s] = sctx.enter_context(nc.semaphore(s))
        with nc.Block() as block:
            @block.sync
            def _(e):
                for fn in fl.ops["sync"]:
                    fn(e)

            @block.tensor
            def _(e):
                for fn in fl.ops["tensor"]:
                    fn(e)

            @block.vector
            def _(e):
                for fn in fl.ops["vector"]:
                    fn(e)

            @block.scalar
            def _(e):
                for fn in fl.ops["scalar"]:
                    fn(e)

            @block.gpsimd
            def _(e):
                for fn in fl.ops["gpsimd"]:
                    fn(e)
    nc._keepalive = (ctx, sctx)
    return nc


_NC_CACHE = None


def kernel(L0_attn_Q_input, L0_attn_Wq, L0_attn_Wk, L0_attn_Wv, L0_attn_Wo,
           L0_ffn_W_fc, L0_ffn_W_proj, L0_c, L0_wmod):
    global _NC_CACHE
    f32 = np.float32
    x = np.ascontiguousarray(np.asarray(L0_attn_Q_input, f32).reshape(SEQ, D).T)
    wq = np.ascontiguousarray(
        np.asarray(L0_attn_Wq, f32).reshape(DT, 128, NH, 128).transpose(2, 1, 0, 3)
    ).reshape(NH, 128, DT * 128)
    wk = np.ascontiguousarray(
        np.asarray(L0_attn_Wk, f32).reshape(DT, 128, 128).transpose(1, 0, 2)
    ).reshape(128, DT * 128)
    wv = np.ascontiguousarray(
        np.asarray(L0_attn_Wv, f32).reshape(DT, 128, 128).transpose(1, 0, 2)
    ).reshape(128, DT * 128)
    wo = np.ascontiguousarray(np.asarray(L0_attn_Wo, f32))
    wfcb = np.ascontiguousarray(
        np.asarray(L0_ffn_W_fc, f32).reshape(NB, BH, DT, 128).transpose(0, 3, 2, 1)
    ).reshape(NB, 128, DT * BH)
    wpjb = np.ascontiguousarray(
        np.asarray(L0_ffn_W_proj, f32).T.reshape(NB, 2, 128, D).transpose(0, 2, 1, 3)
    ).reshape(NB, 128, 2 * D)
    cvec = np.ascontiguousarray(np.asarray(L0_c, f32).reshape(16, 128))
    wmod = np.asarray(L0_wmod, f32)
    identm = np.eye(128, dtype=ml_dtypes.bfloat16)

    if _NC_CACHE is None:
        _NC_CACHE = build_program()
    nc = _NC_CACHE

    in_maps = []
    for i in range(NCORES):
        in_maps.append({
            "xin": np.ascontiguousarray(x[:, S * i:S * (i + 1)]),
            "wqb": wq, "wkb": wk, "wvb": wv, "wo": wo,
            "wfcb": wfcb, "wpjb": wpjb, "cvec": cvec,
            "wmodc": np.ascontiguousarray(wmod[:, MODW * i:MODW * (i + 1)]),
            "ident": identm,
        })
    res = run_bass_kernel_spmd(nc, in_maps, core_ids=list(range(NCORES)))
    out = np.concatenate([np.asarray(r["out"]).T for r in res.results], axis=0)
    return np.ascontiguousarray(out.reshape(1, SEQ, D), dtype=np.float32)


# version 04
# speedup vs baseline: 1.0575x; 1.0575x over previous
import numpy as np
import ml_dtypes
import contextlib

import concourse.bass as bass
import concourse.mybir as mybir
from concourse.alu_op_type import AluOpType as A
from concourse.bass_utils import run_bass_kernel_spmd

F32 = mybir.dt.float32
BF16 = mybir.dt.bfloat16
I32 = mybir.dt.int32
I16 = mybir.dt.int16
ACT = mybir.ActivationFunctionType

NCORES = 8
D = 2048
DT = 16
S = 1024
NH = 16
TT = 64
SEQ = 8192
HID = 8192
NB = 32
BH = 256
MODW = 12288 // NCORES
SCALE = float(1.0 / np.sqrt(128.0))
LN_EPS = 1e-5
RMS_EPS = float(np.finfo(np.float32).eps)
THR = 0.501953125


class Flow:
    ENGINES = ("sync", "tensor", "vector", "scalar", "gpsimd")

    def __init__(self):
        self.ops = {e: [] for e in self.ENGINES}
        self.cnt = {}
        self.sems = {}

    def sem(self, name):
        self.cnt[name] = 0

    def emit(self, engine, fn):
        self.ops[engine].append(fn)

    def wait(self, engine, sem, val):
        if val is None or val <= 0:
            return
        assert val <= self.cnt[sem], f"wait {sem}>={val} but only {self.cnt[sem]} scheduled"
        self.emit(engine, lambda e, s=sem, v=val: e.wait_ge(self.sems[s], v))

    def dma(self, engine, sem, out, in_, **kw):
        self.cnt[sem] += 16
        def fn(e, out=out, in_=in_, s=sem, kw=kw):
            e.dma_start(out=out, in_=in_, **kw).then_inc(self.sems[s], 16)
        self.emit(engine, fn)
        return self.cnt[sem]

    def op(self, engine, method, *args, sem=None, **kw):
        c = None
        if sem is not None:
            self.cnt[sem] += 1
            c = self.cnt[sem]
        drain = engine in ("vector", "scalar", "gpsimd")
        def fn(e, m=method, args=args, kw=kw, s=sem, d=drain):
            ins = getattr(e, m)(*args, **kw)
            if d:
                e.drain()
            if s is not None:
                ins.then_inc(self.sems[s], 1)
        self.emit(engine, fn)
        return c

    def mm(self, out, lhsT, rhs, start, stop, sem=None, tile_position=None):
        kw = {}
        if tile_position is not None:
            kw["tile_position"] = tile_position
        return self.op("tensor", "matmul", out, lhsT, rhs, start=start, stop=stop,
                       sem=sem, **kw)


def build_program(debug=False):
    nc = bass.Bass("TRN2", target_bir_lowering=False)
    fl = Flow()
    ctx = contextlib.ExitStack()
    sbuf = lambda n, s, d: ctx.enter_context(nc.sbuf_tensor(n, s, d))
    psum = lambda n: ctx.enter_context(nc.psum_tensor(n, [128, 512], F32))

    din = lambda n, s: nc.dram_tensor(n, s, F32, kind="ExternalInput").ap()
    xin = din("xin", [D, S])
    wqb_d = din("wqb", [NH, 128, DT * 128])
    wkb_d = din("wkb", [128, DT * 128])
    wvb_d = din("wvb", [128, DT * 128])
    wo_d = din("wo", [D, D])
    wfcb_d = din("wfcb", [NB, 128, DT * BH])
    wpjb_d = din("wpjb", [NB, 128, 2 * D])
    c_d = din("cvec", [16, 128])
    wmod_d = din("wmodc", [D, MODW])
    ident_d = nc.dram_tensor("ident", [128, 128], BF16, kind="ExternalInput").ap()
    out_d = nc.dram_tensor("out", [D, S], F32, kind="ExternalOutput").ap()

    if debug:
        dbg = {n: nc.dram_tensor("dbg_" + n, sh, dt, kind="ExternalOutput").ap()
               for n, sh, dt in [
                   ("gates", [128, 96], F32), ("h1", [128, 1024], BF16),
                   ("qt0", [128, 1024], BF16), ("kt", [128, 2048], BF16),
                   ("vaug", [128, 258], BF16), ("ao0", [128, 1024], BF16),
                   ("x1", [128, 1024], F32), ("h2", [128, 1024], BF16),
                   ("wq1b", [128, 4096], BF16), ("acc0", [128, 1024], F32),
                   ("rstd3", [1, 1024], F32), ("wq2b", [128, 4096], BF16),
               ]}
    mod_in = nc.dram_tensor("mod_in", [1, MODW], F32).ap()
    mod_out = nc.dram_tensor("mod_out", [NCORES, MODW], F32).ap()
    kt_in = nc.dram_tensor("kt_in", [128, S], BF16).ap()
    kt_out = nc.dram_tensor("kt_out", [NCORES * 128, S], BF16).ap()
    vn_in = nc.dram_tensor("vn_in", [S, 128], BF16).ap()
    vn_out = nc.dram_tensor("vn_out", [SEQ, 128], BF16).ap()
    x1_sp = nc.dram_tensor("x1_spill", [D, S], F32).ap()
    ssc_fc_d = nc.dram_tensor("ssc_fc", [128, 1024], BF16).ap()
    tlo_fc_d = nc.dram_tensor("tlo_fc", [128, 1024], BF16).ap()
    ssc_pj_d = nc.dram_tensor("ssc_pj", [128, 2048], BF16).ap()
    tlo_pj_d = nc.dram_tensor("tlo_pj", [128, 2048], BF16).ap()

    xT = sbuf("xT", [128, DT, S], F32)
    hh = sbuf("hh", [128, DT, S], BF16)
    AR = sbuf("AR", [128, 24576], F32)
    ident = sbuf("ident_s", [128, 128], BF16)
    identf = sbuf("identf", [128, 128], F32)
    ones1 = sbuf("ones1", [128, 1], BF16)
    onesf = sbuf("onesf", [1, 128], F32)
    G2 = sbuf("G2", [128, 2], BF16)
    maskI = sbuf("maskI", [128, 1], I32)
    maskA32 = sbuf("maskA32", [128, 1], I32)
    maskA16 = sbuf("maskA16", [128, 1], I16)
    maskPK = sbuf("maskPK", [128, 1], I32)
    modT = sbuf("modT", [128, 96], F32)
    modraw = sbuf("modraw", [96, 128], F32)
    gates = sbuf("gates", [128, 96], F32)
    silu_s = sbuf("silu_s", [16, 128], F32)
    siluT = sbuf("siluT", [128, 16], F32)
    siluh = sbuf("siluh", [128, 16], BF16)
    silul = sbuf("silul", [128, 16], BF16)
    silul_f = sbuf("silul_f", [128, 16], F32)
    statacc = sbuf("statacc", [1, S], F32)
    zt = sbuf("zt", [128, 512], F32)
    rr = sbuf("rr", [128, 512], F32)
    rr2 = sbuf("rr2", [128, 512], F32)
    hsqT = sbuf("hsqT", [128, 1024], BF16)
    ao_inv = sbuf("ao_inv", [128, 4], F32)

    def arv(off, nf32, dtype=F32):
        ap = AR[:, off:off + nf32]
        return ap.bitcast(dtype) if dtype != F32 else ap

    wms = [arv(0, 1536), arv(1536, 1536)]
    wmh = arv(3072, 768, BF16)
    wml = arv(3840, 768, BF16)
    xb_t = [arv(4608, 512, BF16), arv(5120, 512, BF16)]
    xsq_t = [arv(5632, 512, BF16), arv(6144, 512, BF16)]
    QT = arv(0, 8192, BF16)
    KT = arv(8192, 4096, BF16)
    Vaug = arv(12288, 4128, BF16)
    probs = arv(16416, 1024, BF16)
    wqh = [arv(17440, 1024, BF16), arv(18464, 1024, BF16)]
    kv_ev = arv(19488, 1024, BF16)
    vnat = arv(20512, 512, BF16)
    wo_b = [arv(0, 1024, BF16), arv(1024, 1024, BF16)]
    wbs1 = [arv(0, 2048, BF16), arv(2048, 2048, BF16)]
    wbs2 = [arv(4096, 2048, BF16), arv(6144, 2048, BF16)]
    wq1 = [arv(8192, 2048, BF16), arv(10240, 2048, BF16)]
    wq2 = arv(12288, 4096, BF16)
    hsb = arv(16384, 2048, BF16)
    scb = arv(18432, 1024, BF16)
    scb2 = arv(19456, 1024, BF16)
    sct = arv(20480, 1024)
    ab_t = arv(21504, 1024, BF16)
    bc_s = arv(22528, 1024, BF16)
    bc_t = arv(23552, 1024, BF16)
    x1rel = [arv(0, 1024), arv(1024, 1024)]
    p0tmpA = arv(6656, 1024)
    p0tmpB = arv(7680, 512)
    aox = arv(23072, 256, BF16)
    wkb = arv(21024, 1024, BF16)
    wvb = arv(22048, 1024, BF16)
    stat = arv(6656, 512)[0:1, :]
    stat2 = arv(7168, 512)[0:1, :]
    stat3 = arv(7680, 512)[0:1, :]
    zt0 = zt

    pA, pB = psum("pA"), psum("pB")
    pCD = ctx.enter_context(nc.psum_tensor("pCD", [128, 1024], F32))
    pC, pD = pCD[:, 0:512], pCD[:, 512:1024]
    pE, pF, pG, pH = psum("pE"), psum("pF"), psum("pG"), psum("pH")
    pAb = pA[:, 0:64].bitcast(BF16)
    pBb = pB[:, 0:64].bitcast(BF16)

    for s in ("ld", "gq", "sq", "pe", "ve", "ac", "cc", "ms", "lc", "lx",
              "lm0", "lm1", "lq0", "lq1", "lo0", "lo1", "lf0", "lf1",
              "lp0", "lp1", "lr0", "lr1", "dbgs", "db1", "db2", "db3", "db4", "db5", "lmr", "lkv", "lsc", "sq0", "sq1"):
        fl.sem(s)

    SY, PE, VE, SC, GP = "sync", "tensor", "vector", "scalar", "gpsimd"
    W, DMA, OP, MM = fl.wait, fl.dma, fl.op, fl.mm

    OP(VE, "memset", ones1[:, :], 1.0, sem="ms")
    OP(VE, "memset", onesf[:, :], 1.0, sem="ms")
    OP(VE, "memset", G2[0:64, 0:1], 1.0, sem="ms")
    OP(VE, "memset", G2[64:128, 0:1], 0.0, sem="ms")
    OP(VE, "memset", G2[0:64, 1:2], 0.0, sem="ms")
    OP(VE, "memset", G2[64:128, 1:2], 1.0, sem="ms")
    OP(VE, "memset", maskI[:, :], -65536, sem="ms")
    OP(VE, "memset", maskA32[:, :], 2147483647, sem="ms")
    OP(VE, "memset", maskA16[:, :], 32767, sem="ms")
    OP(VE, "memset", maskPK[:, :], 2147450879, sem="ms")
    OP(VE, "memset", Vaug[:, :], 1.0, sem="ms")
    OP(VE, "memset", statacc[:, :], 0.0, sem="ms")
    ms_const = fl.cnt["ms"]
    c_id = DMA(SY, "lc", ident[:, :], ident_d[:, :])
    c_c = DMA(SY, "lc", silu_s[:, :], c_d[:, :])
    W(VE, "lc", 32)
    OP(VE, "tensor_copy", identf[:, :], ident[:, :], sem="ve")

    W(PE, "ms", ms_const)
    W(VE, "ms", ms_const)
    W(SC, "ms", ms_const)
    W(GP, "ms", ms_const)
    W(PE, "lc", 32)
    W(PE, "ve", fl.cnt["ve"])
    OP(PE, "transpose", pA[:, 0:16], silu_s[:, :], identf[0:16, 0:16], sem="pe")
    W(VE, "pe", fl.cnt["pe"])
    OP(VE, "tensor_copy", siluT[:, :], pA[:, 0:16], sem="ve")
    W(SC, "ve", fl.cnt["ve"])
    OP(SC, "activation", siluT[:, :], siluT[:, :], ACT.Silu, sem="ac")
    W(VE, "ac", fl.cnt["ac"])
    OP(VE, "tensor_copy", siluh[:, :], siluT[:, :])
    OP(VE, "tensor_copy", silul_f[:, :], siluh[:, :])
    OP(VE, "tensor_tensor", silul_f[:, :], siluT[:, :], silul_f[:, :], A.subtract)
    OP(VE, "tensor_copy", silul[:, :], silul_f[:, :], sem="ve")
    ve_silu = fl.cnt["ve"]

    pe_modslab = [None] * DT
    for t in range(DT):
        if t >= 2:
            W(SY, "pe", pe_modslab[t - 2])
        c_wm = DMA(SY, "lm0" if t % 2 == 0 else "lm1", wms[t % 2][:, :],
                   wmod_d[128 * t:128 * (t + 1), :])
        W(VE, "lm0" if t % 2 == 0 else "lm1", c_wm)
        if t >= 1:
            W(VE, "pe", pe_modslab[t - 1])
        OP(VE, "tensor_copy", wmh[:, :], wms[t % 2][:, :])
        OP(VE, "tensor_copy", p0tmpA[:, :], wmh[:, 0:1024])
        OP(VE, "tensor_copy", p0tmpB[:, :], wmh[:, 1024:1536])
        OP(VE, "tensor_tensor", p0tmpA[:, :], wms[t % 2][:, 0:1024], p0tmpA[:, :],
           A.subtract)
        OP(VE, "tensor_tensor", p0tmpB[:, :], wms[t % 2][:, 1024:1536], p0tmpB[:, :],
           A.subtract)
        OP(VE, "tensor_copy", wml[:, 0:1024], p0tmpA[:, :])
        OP(VE, "tensor_copy", wml[:, 1024:1536], p0tmpB[:, :], sem="ve")
        W(PE, "ve", fl.cnt["ve"])
        if t == 0:
            W(PE, "ve", ve_silu)
        for ci, bank in ((0, pB), (1, pC), (2, pD)):
            n0 = 512 * ci
            MM(bank[0:1, :], siluh[:, t:t + 1], wmh[:, n0:n0 + 512],
               start=(t == 0), stop=False)
            MM(bank[0:1, :], siluh[:, t:t + 1], wml[:, n0:n0 + 512],
               start=False, stop=False)
            pe_modslab[t] = MM(bank[0:1, :], silul[:, t:t + 1], wmh[:, n0:n0 + 512],
                               start=False, stop=(t == DT - 1), sem="pe")
    pe_mod = fl.cnt["pe"]
    W(VE, "pe", pe_mod)
    OP(VE, "tensor_copy", stat[0:1, :], pB[0:1, :])
    OP(VE, "tensor_copy", stat2[0:1, :], pC[0:1, :])
    OP(VE, "tensor_copy", stat3[0:1, :], pD[0:1, :], sem="ve")
    ve_modev = fl.cnt["ve"]
    gq_modw = None
    W(GP, "ve", ve_modev)
    for ci, src in ((0, stat), (1, stat2), (2, stat3)):
        gq_modw = DMA(GP, "gq", mod_in[0:1, 512 * ci:512 * (ci + 1)], src[0:1, :])
    W(GP, "gq", gq_modw)
    cc_mod = OP(GP, "collective_compute", "AllGather", mybir.AluOpType.bypass,
                replica_groups=[list(range(NCORES))],
                ins=[mod_in], outs=[mod_out], sem="cc")
    W(GP, "cc", cc_mod)
    gq_modr = DMA(GP, "lmr", modraw[:, :],
                  mod_out.rearrange("a b -> (a b)").rearrange("(r c) -> r c", c=128))
    W(PE, "lmr", gq_modr)
    OP(PE, "transpose", pA[:, 0:96], modraw[:, :], identf[0:96, 0:96], sem="pe")
    pe_modT = fl.cnt["pe"]
    W(VE, "pe", pe_modT)
    OP(VE, "tensor_copy", modT[:, :], pA[:, 0:96])
    OP(VE, "tensor_scalar", gates[:, 0:16], modT[:, 16:32], 1.0, None, A.add)
    OP(VE, "tensor_copy", gates[:, 16:32], modT[:, 0:16])
    OP(VE, "tensor_scalar", gates[:, 48:64], modT[:, 64:80], 1.0, None, A.add)
    OP(VE, "tensor_copy", gates[:, 64:80], modT[:, 48:64])
    for (gsrc, gdst) in ((modT[:, 32:48], gates[:, 32:48]),
                         (modT[:, 80:96], gates[:, 80:96])):
        OP(VE, "tensor_scalar", zt0[:, 0:16], gsrc, 0.0, 2.0, A.is_ge, A.mult)
        OP(VE, "tensor_scalar", zt0[:, 0:16], zt0[:, 0:16], 1.0, None, A.subtract)
        OP(VE, "tensor_scalar", zt0[:, 16:32].bitcast(I32), gsrc.bitcast(I32),
           maskA32[:, :], None, A.bitwise_and)
        OP(VE, "tensor_scalar", zt0[:, 16:32], zt0[:, 16:32], 0.25, None, A.max)
        OP(VE, "tensor_tensor", gdst, zt0[:, 0:16], zt0[:, 16:32], A.mult)
    OP(VE, "tensor_copy", zt0[:, 0:1], zt0[:, 0:1], sem="ve")
    ve_gates = fl.cnt["ve"]

    c_x = [DMA(SY, "lx", xT[:, t, :], xin[128 * t:128 * (t + 1), :]) for t in range(DT)]
    pe_stat = [None] * DT
    for t in range(DT):
        W(VE, "lx", c_x[DT - 1])
        if t >= 2:
            W(VE, "pe", pe_stat[t - 2])
        OP(VE, "tensor_copy", xb_t[t % 2][:, :], xT[:, t, :], sem="ve")
        W(SC, "ve", fl.cnt["ve"])
        OP(SC, "activation", xsq_t[t % 2][:, :], xb_t[t % 2][:, :], ACT.Square, sem="ac")
        W(PE, "ac", fl.cnt["ac"])
        MM(pE[0:1, :], ones1[:, :], xb_t[t % 2][:, 0:512], start=(t == 0), stop=(t == DT - 1))
        MM(pF[0:1, :], ones1[:, :], xb_t[t % 2][:, 512:1024], start=(t == 0), stop=(t == DT - 1))
        MM(pG[0:1, :], ones1[:, :], xsq_t[t % 2][:, 0:512], start=(t == 0), stop=(t == DT - 1))
        pe_stat[t] = MM(pH[0:1, :], ones1[:, :], xsq_t[t % 2][:, 512:1024],
                        start=(t == 0), stop=(t == DT - 1), sem="pe")

    def ln_apply(gcol0, pre_pe_waits):
        for c in range(2):
            bmu = pE if c == 0 else pF
            bms = pG if c == 0 else pH
            W(VE, "pe", fl.cnt["pe"])
            OP(VE, "tensor_scalar", stat[0:1, :], bmu[0:1, :], 1.0 / D, None, A.mult)
            OP(VE, "tensor_scalar", stat2[0:1, :], bms[0:1, :], 1.0 / D, None, A.mult)
            OP(VE, "tensor_tensor", stat3[0:1, :], stat[0:1, :], stat[0:1, :], A.mult)
            OP(VE, "tensor_tensor", stat2[0:1, :], stat2[0:1, :], stat3[0:1, :],
               A.subtract, sem="ve")
            OP(VE, "tensor_scalar", stat2[0:1, :], stat2[0:1, :], LN_EPS, None, A.add,
               sem="ve")
            W(SC, "ve", fl.cnt["ve"])
            OP(SC, "activation", stat2[0:1, :], stat2[0:1, :], ACT.Ln)
            OP(SC, "activation", stat2[0:1, :], stat2[0:1, :], ACT.Exp, scale=-0.5,
               sem="ac")
            for sname, v in pre_pe_waits:
                W(PE, sname, v)
            W(PE, "ac", fl.cnt["ac"])
            MM(pA[:, :], onesf[:, :], stat[0:1, :], start=True, stop=True)
            MM(pB[:, :], onesf[:, :], stat2[0:1, :], start=True, stop=True, sem="pe")
            pe_bc = fl.cnt["pe"]
            for t in range(DT):
                W(VE, "pe", pe_bc)
                OP(VE, "tensor_tensor", zt0[:, :], xT[:, t, 512 * c:512 * (c + 1)],
                   pA[:, :], A.subtract)
                OP(VE, "tensor_tensor", zt0[:, :], zt0[:, :], pB[:, :], A.mult)
                OP(VE, "scalar_tensor_tensor", hh[:, t, 512 * c:512 * (c + 1)],
                   zt0[:, :], gates[:, gcol0 + t:gcol0 + t + 1],
                   gates[:, gcol0 + 16 + t:gcol0 + 16 + t + 1].broadcast_to([128, 512]),
                   A.mult, A.add, sem="ve")
            W(PE, "ve", fl.cnt["ve"])

    W(VE, "ve", ve_gates)
    ln_apply(0, [("gq", gq_modw), ("ve", ve_gates)])
    if debug:
        W(SY, "ve", ve_gates)
        DMA(SY, "db1", dbg["gates"], gates[:, :])
    ve_h1 = fl.cnt["ve"]
    d_h1 = 0
    if debug:
        W(SY, "ve", ve_h1)
        d_h1 = DMA(SY, "db1", dbg["h1"], hh[:, 0, :])

    c_wk = DMA(GP, "lkv", wkb[:, :], wkb_d[:, :])
    c_wv = DMA(GP, "lkv", wvb[:, :], wvb_d[:, :])
    c_wq = [None] * NH
    for h in range(2):
        c_wq[h] = DMA(GP, "lq0" if h % 2 == 0 else "lq1", wqh[h][:, :], wqb_d[h, :, :])
    ac_qev = [None] * NH
    W(SC, "ve", fl.cnt["ve"])
    for h in range(NH):
        W(PE, "lq0" if h % 2 == 0 else "lq1", c_wq[h])
        if h == 0:
            W(PE, "ve", ve_h1)
        pe_q = None
        for c, bank in ((0, pE), (1, pF)):
            for t in range(DT):
                pe_q = MM(bank[:, :], wqh[h % 2][:, 128 * t:128 * (t + 1)],
                          hh[:, t, 512 * c:512 * (c + 1)], start=(t == 0),
                          stop=(t == DT - 1),
                          sem=("pe" if (c == 1 and t == DT - 1) else None))
        if h + 2 < NH:
            W(GP, "pe", pe_q)
            c_wq[h + 2] = DMA(GP, "lq0" if h % 2 == 0 else "lq1", wqh[h % 2][:, :],
                              wqb_d[h + 2, :, :])
        W(SC, "pe", pe_q)
        OP(SC, "activation", QT[:, 1024 * h:1024 * h + 512], pE[:, :], ACT.Copy)
        ac_qev[h] = OP(SC, "activation", QT[:, 1024 * h + 512:1024 * h + 1024], pF[:, :],
                       ACT.Copy, sem="ac")
        if h + 1 < NH:
            W(PE, "ac", ac_qev[h])
    W(PE, "lkv", 32)
    pe_k = None
    for c, bank in ((0, pG), (1, pH)):
        for t in range(DT):
            pe_k = MM(bank[:, :], wkb[:, 128 * t:128 * (t + 1)],
                      hh[:, t, 512 * c:512 * (c + 1)], start=(t == 0), stop=(t == DT - 1),
                      sem=("pe" if (c == 1 and t == DT - 1) else None))
    W(SC, "pe", pe_k)
    OP(SC, "activation", kv_ev[:, 0:512], pG[:, :], ACT.Copy)
    ac_kev = OP(SC, "activation", kv_ev[:, 512:1024], pH[:, :], ACT.Copy, sem="ac")
    W(PE, "ac", ac_kev)
    pe_v = None
    for c, bank in ((0, pG), (1, pH)):
        for t in range(DT):
            pe_v = MM(bank[:, :], wvb[:, 128 * t:128 * (t + 1)],
                      hh[:, t, 512 * c:512 * (c + 1)], start=(t == 0), stop=(t == DT - 1),
                      sem=("pe" if (c == 1 and t == DT - 1) else None))
    W(SC, "pe", pe_v)
    OP(SC, "activation", kv_ev[:, 1024:1536], pG[:, :], ACT.Copy)
    ac_vev = OP(SC, "activation", kv_ev[:, 1536:2048], pH[:, :], ACT.Copy, sem="ac")
    W(GP, "ac", ac_kev)
    gq_k = DMA(GP, "gq", kt_in[:, :], kv_ev[:, 0:1024])
    W(PE, "ac", ac_vev)
    ve_vn = None
    for i in range(8):
        tb = pAb if i % 2 == 0 else pBb
        pe_tv = OP(PE, "transpose", tb,
                   kv_ev[:, 1024 + 128 * i:1024 + 128 * (i + 1)], ident[:, :], sem="pe")
        W(VE, "pe", pe_tv)
        ve_vn = OP(VE, "tensor_copy", vnat[:, 128 * i:128 * (i + 1)], tb,
                   sem="ve")
        W(PE, "ve", ve_vn)
    W(GP, "ve", ve_vn)
    gq_v = DMA(GP, "gq", vn_in.rearrange("(i p) j -> p i j", p=128),
               vnat[:, :].rearrange("p (i j) -> p i j", j=128))
    W(GP, "gq", gq_v)
    cc_k = OP(GP, "collective_compute", "AllGather", mybir.AluOpType.bypass,
              replica_groups=[list(range(NCORES))], ins=[kt_in], outs=[kt_out], sem="cc")
    cc_v = OP(GP, "collective_compute", "AllGather", mybir.AluOpType.bypass,
              replica_groups=[list(range(NCORES))], ins=[vn_in], outs=[vn_out], sem="cc")
    W(GP, "cc", cc_v)
    gq_kt = None
    for j in range(NCORES):
        gq_kt = DMA(GP, "gq", KT[:, 1024 * j:1024 * (j + 1)], kt_out[128 * j:128 * (j + 1), :])
    for tt in range(TT):
        gq_kt = DMA(GP, "gq", Vaug[:, 129 * tt:129 * tt + 128],
                    vn_out[128 * tt:128 * (tt + 1), :])
    gq_kv_done = gq_kt
    if debug:
        W(SY, "gq", gq_kv_done)
        W(SY, "ac", ac_qev[0])
        DMA(SY, "db2", dbg["qt0"], QT[:, 0:1024])
        DMA(SY, "db2", dbg["kt"], KT[:, 0:2048])
        d_kv = DMA(SY, "db2", dbg["vaug"], Vaug[:, 0:258])

    W(PE, "gq", gq_kv_done)
    W(PE, "ms", ms_const)
    if debug:
        W(VE, "db1", d_h1)
    avb = (pE, pF, pG, pH)
    last_av_slot = {}
    last_exp_bank = {}
    for h in range(NH):
        for c in range(2):
            qs = QT[:, 1024 * h + 512 * c:1024 * h + 512 * (c + 1)]
            av_pe = {}
            exp_pair = {}
            for pk in range(TT // 2):
                t0 = 2 * pk
                if pk >= 1:
                    W(PE, "ac", exp_pair[pk - 1])
                else:
                    W(PE, "ac", last_exp_bank.get(0))
                MM(pC[:, :], KT[:, 128 * t0:128 * (t0 + 1)], qs, start=True,
                   stop=True)
                MM(pD[:, :], KT[:, 128 * (t0 + 1):128 * (t0 + 2)], qs, start=True,
                   stop=True, sem="pe")
                pe_sc = fl.cnt["pe"]
                s0 = t0 % 4
                W(SC, "pe", pe_sc)
                W(SC, "pe", last_av_slot.get(s0))
                OP(SC, "activation", probs[:, 512 * s0:512 * (s0 + 2)], pCD[:, :],
                   ACT.Exp, scale=SCALE, sem="ac")
                exp_pair[pk] = fl.cnt["ac"]
                last_exp_bank[0] = exp_pair[pk]
                W(PE, "ac", exp_pair[pk])
                for tt in (t0, t0 + 1):
                    slot = tt % 4
                    for sl in range(4):
                        av_pe[tt] = MM(
                            avb[sl][:, 0:129],
                            probs[:, 512 * slot + 128 * sl:512 * slot + 128 * (sl + 1)],
                            Vaug[:, 129 * tt:129 * (tt + 1)],
                            start=(tt == 0), stop=(tt == TT - 1),
                            sem=("pe" if sl == 3 else None))
                last_av_slot[s0] = av_pe[t0 + 1]
            W(VE, "pe", av_pe[TT - 1])
            for sl in range(4):
                OP(VE, "reciprocal", ao_inv[:, sl:sl + 1], avb[sl][:, 128:129],
                   sem=("ve" if sl == 3 else None))
            W(SC, "ve", fl.cnt["ve"])
            ac_ao = None
            for sl in range(4):
                ac_ao = OP(SC, "activation", aox[:, 128 * sl:128 * (sl + 1)],
                           avb[sl][:, 0:128], ACT.Copy, scale=ao_inv[:, sl:sl + 1],
                           sem=("ac" if sl == 3 else None))
            W(PE, "ac", ac_ao)
            ve_ao = None
            ve_ao_sl = {}
            for sl in range(4):
                tb = pAb if sl % 2 == 0 else pBb
                if sl >= 2:
                    W(PE, "ve", ve_ao_sl[sl - 2])
                pe_tr = OP(PE, "transpose", tb, aox[:, 128 * sl:128 * (sl + 1)],
                           ident[:, :], sem="pe")
                W(VE, "pe", pe_tr)
                ve_ao = OP(VE, "tensor_copy",
                           hh[:, h, 512 * c + 128 * sl:512 * c + 128 * (sl + 1)],
                           tb, sem="ve")
                ve_ao_sl[sl] = ve_ao
            W(PE, "ve", ve_ao)

    W(PE, "ac", fl.cnt["ac"])
    W(PE, "ve", fl.cnt["ve"])
    d_ao0 = 0
    if debug:
        W(SY, "ve", fl.cnt["ve"])
        d_ao0 = DMA(SY, "db3", dbg["ao0"], hh[:, 0, :])
    banks8 = (pA, pB, pC, pD, pE, pF, pG, pH)
    order = [(c, half) for c in range(2) for half in range(2)]
    slab_ids = [(r, q) for r in range(4) for q in range(DT)]
    c_wo = {}
    W(GP, "pe", fl.cnt["pe"])
    W(GP, "ac", fl.cnt["ac"])
    W(GP, "ve", fl.cnt["ve"])
    if debug:
        W(GP, "db2", d_kv)
    for i in range(2):
        r, q = slab_ids[i]
        c_wo[(r, q)] = DMA(GP, "lo0" if i % 2 == 0 else "lo1", wo_b[i % 2][:, :],
                           wo_d[128 * q:128 * (q + 1), :])
    idx = 0
    ve_res = None
    for r, (c, half) in enumerate(order):
        for q in range(DT):
            W(PE, "lo0" if idx % 2 == 0 else "lo1", c_wo[(r, q)])
            pe_woq = None
            for k in range(8):
                pe_woq = MM(banks8[k][:, :],
                            wo_b[idx % 2][:, 1024 * half + 128 * k:1024 * half + 128 * (k + 1)],
                            hh[:, q, 512 * c:512 * (c + 1)], start=(q == 0),
                            stop=(q == DT - 1), sem=("pe" if k == 7 else None))
            if idx + 2 < len(slab_ids):
                r2, q2 = slab_ids[idx + 2]
                W(GP, "pe", pe_woq)
                c_wo[(r2, q2)] = DMA(GP, "lo0" if idx % 2 == 0 else "lo1",
                                     wo_b[idx % 2][:, :],
                                     wo_d[128 * q2:128 * (q2 + 1), :])
            idx += 1
        pe_wor = fl.cnt["pe"]
        ve_k2 = {}
        for k in range(8):
            dt_out = 8 * half + k
            rtmp = rr if k % 2 == 0 else rr2
            W(SC, "pe", pe_wor)
            W(SC, "ve", ve_k2.get(k - 2))
            ac_r = OP(SC, "activation", rtmp[:, :], banks8[k][:, :], ACT.Copy,
                      scale=gates[:, 32 + dt_out:32 + dt_out + 1], sem="ac")
            W(VE, "ac", ac_r)
            ve_res = OP(VE, "tensor_tensor", xT[:, dt_out, 512 * c:512 * (c + 1)],
                        xT[:, dt_out, 512 * c:512 * (c + 1)], rtmp[:, :], A.add, sem="ve")
            ve_k2[k] = ve_res
        W(PE, "ve", ve_res)

    d_x1m = 0
    if debug:
        W(SY, "ve", ve_res)
        d_x1m = DMA(SY, "db3", dbg["x1"], xT[:, 0, :])
    xb2 = wbs1
    xq2 = wbs2
    if debug:
        W(VE, "db3", max(d_ao0, d_x1m))
    pe_stat2 = [None] * DT
    for t in range(DT):
        W(VE, "ve", ve_res)
        if t >= 2:
            W(VE, "pe", pe_stat2[t - 2])
        OP(VE, "tensor_copy", xb2[t % 2][:, 0:1024], xT[:, t, :], sem="ve")
        W(SC, "ve", fl.cnt["ve"])
        OP(SC, "activation", xq2[t % 2][:, 0:1024], xb2[t % 2][:, 0:1024], ACT.Square,
           sem="ac")
        W(PE, "ac", fl.cnt["ac"])
        MM(pE[0:1, :], ones1[:, :], xb2[t % 2][:, 0:512], start=(t == 0), stop=(t == DT - 1))
        MM(pF[0:1, :], ones1[:, :], xb2[t % 2][:, 512:1024], start=(t == 0), stop=(t == DT - 1))
        MM(pG[0:1, :], ones1[:, :], xq2[t % 2][:, 0:512], start=(t == 0), stop=(t == DT - 1))
        pe_stat2[t] = MM(pH[0:1, :], ones1[:, :], xq2[t % 2][:, 512:1024],
                         start=(t == 0), stop=(t == DT - 1), sem="pe")
    ln_apply(48, [("ve", fl.cnt["ve"])])
    ve_h2 = fl.cnt["ve"]
    if debug:
        W(SY, "ve", ve_h2)
        DMA(SY, "db4", dbg["h2"], hh[:, 0, :])
    gq_spill = None
    W(GP, "ve", ve_h2)
    for t in range(DT):
        gq_spill = DMA(GP, "gq", x1_sp[128 * t:128 * (t + 1), :], xT[:, t, :])
    W(VE, "gq", gq_spill)
    OP(VE, "tensor_scalar", xT[:, :, :], xT[:, :, :], 0.0, None, A.mult, sem="ve")
    ve_zeroacc = fl.cnt["ve"]

    def quant(kind, wbstage, wqdst):
        sscd, tlod = (ssc_fc_d, tlo_fc_d) if kind == "fc" else (ssc_pj_d, tlo_pj_d)
        wide = 1024 if kind == "fc" else 2048
        OP(VE, "tensor_scalar", ab_t[:, :].bitcast(I32), wbstage[:, 0:2048].bitcast(I32),
           maskPK[:, :], None, A.bitwise_and)
        OP(VE, "tensor_scalar", scb[:, :].bitcast(I32), wbstage[:, 2048:4096].bitcast(I32),
           maskPK[:, :], None, A.bitwise_and, sem="ve")
        W(PE, "ve", fl.cnt["ve"])
        sb4 = (pA, pB, pC, pD)
        pe_sc_mm = None
        if kind == "fc":
            for t in range(DT):
                src = ab_t if t < 8 else scb
                rh = src[:, 256 * (t % 8):256 * (t % 8 + 1)]
                j = t % 4
                pe_sc_mm = MM(sb4[t // 4][32 * j:32 * j + 2, 0:256], G2[:, :], rh,
                              start=True, stop=True, tile_position=(0, 32 * j),
                              sem=("pe" if t == DT - 1 else None))
        else:
            for ht in range(2):
                src = ab_t if ht == 0 else scb
                for chn in range(4):
                    j = chn // 2
                    pe_sc_mm = MM(sb4[2 * ht + chn % 2][32 * j:32 * j + 2, 0:512],
                                  G2[:, :], src[:, 512 * chn:512 * (chn + 1)],
                                  start=True, stop=True, tile_position=(0, 32 * j),
                                  sem=("pe" if (ht == 1 and chn == 3) else None))
        W(VE, "pe", pe_sc_mm)
        seg = wide // 4
        for i in range(4):
            OP(VE, "tensor_scalar", scb2[:, seg * i:seg * (i + 1)], sb4[i][:, 0:seg],
               1.0 / 64.0, 1e-8, A.mult, A.max)
        for hw in range(wide // 1024):
            o = 1024 * hw
            OP(VE, "tensor_scalar", sct[:, :], scb2[:, o:o + 1024], THR, None, A.mult)
            OP(VE, "tensor_scalar", sct[:, :].bitcast(I32), sct[:, :].bitcast(I32),
               maskI[:, :], None, A.bitwise_and)
            OP(VE, "tensor_copy", scb[:, o:o + 1024], sct[:, :])
        OP(VE, "tensor_copy", zt0[:, 0:1], zt0[:, 0:1], sem="ve")
        ve_rows = fl.cnt["ve"]
        W(GP, "ve", ve_rows)
        DMA(GP, "lsc", sscd[:, 0:wide], scb2[:, 0:wide])
        c_rows = DMA(GP, "lsc", tlod[:, 0:wide], scb[:, 0:wide])
        W(GP, "lsc", c_rows)
        ve_q = None
        for half in range(2):
            if half == 1:
                W(GP, "ve", fl.cnt["ve"])
            gq_b = None
            for (dr_, dst) in ((sscd, bc_s), (tlod, bc_t)):
                for u in range(2):
                    if kind == "fc":
                        for bl in range(2):
                            bg = 2 * half + bl
                            src = dr_[:, 0:1024].rearrange(
                                "(j z) (b n) -> j z b n", j=4, b=4)
                            src = src[:, u:u + 1, bg:bg + 1].rearrange(
                                "j z b n -> (z b) j n")
                            src = src.broadcast_to([64, 4, 256])
                            dstv = dst[64 * u:64 * (u + 1),
                                       1024 * bl:1024 * (bl + 1)].rearrange(
                                "q (j n) -> q j n", j=4)
                            gq_b = DMA(GP, "gq", dstv, src)
                    else:
                        for c2 in range(2):
                            src = dr_[:, 0:2048].rearrange(
                                "(c2 z) (c1h n) -> c2 z c1h n", c2=4, c1h=4)
                            src = src[c2:c2 + 1, u:u + 1,
                                      2 * half:2 * half + 2].rearrange(
                                "c2 z c1 n -> (c2 z) c1 n")
                            src = src.broadcast_to([64, 2, 512])
                            dstv = dst[64 * u:64 * (u + 1),
                                       1024 * c2:1024 * (c2 + 1)].rearrange(
                                "q (c1 n) -> q c1 n", c1=2)
                            gq_b = DMA(GP, "gq", dstv, src)
            W(VE, "gq", gq_b)
            wb_h = wbstage[:, 2048 * half:2048 * (half + 1)]
            gt = ab_t[:, 0:2048]
            L = sct[:, :].bitcast(BF16)
            nw = scb2[:, 0:2048]
            OP(VE, "tensor_scalar", nw, wb_h, -1.0, None, A.mult)
            OP(VE, "tensor_tensor", gt, wb_h, bc_t[:, :], A.is_gt)
            OP(VE, "tensor_tensor", L, nw, bc_t[:, :], A.is_gt)
            OP(VE, "tensor_tensor", gt, gt, L, A.subtract)
            ve_q = OP(VE, "tensor_tensor", wqdst[:, 2048 * half:2048 * (half + 1)],
                      gt, bc_s[:, :], A.mult, sem="ve")
        return ve_q

    W(GP, "ve", ve_zeroacc)
    W(GP, "pe", fl.cnt["pe"])
    W(GP, "ac", fl.cnt["ac"])
    c_fc = [None] * NB
    c_pj = [None] * NB
    for i in range(2):
        c_fc[i] = DMA(GP, "lf0" if i % 2 == 0 else "lf1", wbs1[i][:, :], wfcb_d[i, :, :])
        c_pj[i] = DMA(GP, "lp0" if i % 2 == 0 else "lp1", wbs2[i][:, :], wpjb_d[i, :, :])
    ve_q2 = [None] * NB
    pe_fc = [None] * NB
    pe_pj = [None] * (NB // 2)
    gq_acc = [None] * (NB // 2)
    ac_relu = {}
    last_hsq_mm = {}
    for b in range(NB):
        sbx = b % 2
        if b >= 2:
            W(VE, "pe", pe_pj[b // 2 - 1])
            W(VE, "pe", pe_fc[b - 2])
        if debug and b == 2:
            W(VE, "db4", d_wq)
        W(VE, "lf0" if b % 2 == 0 else "lf1", c_fc[b])
        ve_q1 = quant("fc", wbs1[sbx], wq1[sbx])
        W(VE, "lp0" if b % 2 == 0 else "lp1", c_pj[b])
        ve_q2[b] = quant("pj", wbs2[sbx], wq2[:, 4096 * sbx:4096 * (sbx + 1)])
        if debug and b == 0:
            W(SY, "ve", ve_q2[b])
            d_wq = DMA(SY, "db4", dbg["wq1b"], wq1[0][:, :])
            d_wq = DMA(SY, "db4", dbg["wq2b"], wq2[:, 0:4096])

        if b + 2 < NB:
            W(GP, "ve", ve_q2[b])
            c_fc[b + 2] = DMA(GP, "lf0" if sbx == 0 else "lf1", wbs1[sbx][:, :],
                              wfcb_d[b + 2, :, :])
            c_pj[b + 2] = DMA(GP, "lp0" if sbx == 0 else "lp1", wbs2[sbx][:, :],
                              wpjb_d[b + 2, :, :])
        W(PE, "ve", ve_q1)
        if b == 0:
            W(PE, "ve", ve_h2)
        gidx = 0
        for ht in range(2):
            for c in range(2):
                bank = pE if gidx % 2 == 0 else pF
                W(PE, "ac", ac_relu.get(gidx - 2))
                pe_h1 = None
                for t in range(DT):
                    pe_h1 = MM(bank[:, :],
                               wq1[sbx][:, 256 * t + 128 * ht:256 * t + 128 * (ht + 1)],
                               hh[:, t, 512 * c:512 * (c + 1)], start=(t == 0),
                               stop=(t == DT - 1), sem=("pe" if t == DT - 1 else None))
                W(SC, "pe", pe_h1)
                ac_r = OP(SC, "activation", rr[:, :], bank[:, :], ACT.Relu, sem="ac")
                ac_relu[gidx] = ac_r
                W(VE, "ac", ac_r)
                W(VE, "pe", last_hsq_mm.get(c))
                hslot = hsb[:, 1024 * (2 * sbx + ht) + 512 * c:
                            1024 * (2 * sbx + ht) + 512 * (c + 1)]
                OP(VE, "tensor_tensor", hslot, rr[:, :], rr[:, :], A.mult)
                OP(VE, "tensor_tensor", hsqT[:, 512 * c:512 * (c + 1)], hslot, hslot,
                   A.mult, sem="ve")
                W(PE, "ve", fl.cnt["ve"])
                last_hsq_mm[c] = MM((pG if c == 0 else pH)[0:1, :], ones1[:, :],
                                    hsqT[:, 512 * c:512 * (c + 1)],
                                    start=(sbx == 0 and ht == 0),
                                    stop=(sbx == 1 and ht == 1), sem="pe")
                gidx += 1
        pe_fc[b] = fl.cnt["pe"]
        if b % 2 == 1:
            sbi = b // 2
            W(VE, "pe", pe_fc[b])
            OP(VE, "tensor_tensor", statacc[0:1, 0:512], pG[0:1, :], statacc[0:1, 0:512],
               A.add)
            OP(VE, "tensor_tensor", statacc[0:1, 512:1024], pH[0:1, :],
               statacc[0:1, 512:1024], A.add, sem="ve")
            ve_st = fl.cnt["ve"]
            W(PE, "ve", ve_st)
            W(PE, "ve", ve_q2[b])
            for rnd in range(8):
                slots = [(2 * rnd + a, cc2) for a in range(2) for cc2 in range(2)]
                pe_prd = None
                for sl, (dtl, c) in enumerate(slots):
                    bank = (pE, pF, pG, pH)[sl]
                    for k in range(4):
                        pe_prd = MM(bank[:, :],
                                    wq2[:, 2048 * k + 128 * dtl:2048 * k + 128 * (dtl + 1)],
                                    hsb[:, 1024 * k + 512 * c:1024 * k + 512 * (c + 1)],
                                    start=(k == 0), stop=(k == 3),
                                    sem=("pe" if (sl == 3 and k == 3) else None))
                W(VE, "pe", pe_prd)
                for sl, (dtl, c) in enumerate(slots):
                    bank = (pE, pF, pG, pH)[sl]
                    gq_acc[sbi] = OP(VE, "tensor_tensor",
                                     xT[:, dtl, 512 * c:512 * (c + 1)], bank[:, :],
                                     xT[:, dtl, 512 * c:512 * (c + 1)], A.add, sem="ve")
                W(PE, "ve", gq_acc[sbi])
            pe_pj[sbi] = fl.cnt["pe"]

    W(VE, "pe", pe_pj[-1])
    W(VE, "ve", gq_acc[-1])
    if debug:
        W(SY, "pe", pe_pj[-1])
        DMA(SY, "db5", dbg["acc0"], xT[:, 0, :])
        d_st0 = DMA(SY, "db5", dbg["rstd3"], statacc[0:1, :])
        W(VE, "db5", d_st0)
    OP(VE, "tensor_scalar", statacc[0:1, :], statacc[0:1, :], 1.0 / HID, None, A.mult,
       sem="ve")
    OP(VE, "tensor_scalar", statacc[0:1, :], statacc[0:1, :], RMS_EPS, None, A.add,
       sem="ve")
    W(SC, "ve", fl.cnt["ve"])
    OP(SC, "activation", statacc[0:1, :], statacc[0:1, :], ACT.Ln)
    OP(SC, "activation", statacc[0:1, :], statacc[0:1, :], ACT.Exp, scale=-0.5, sem="ac")
    W(PE, "ac", fl.cnt["ac"])
    MM(pA[:, :], onesf[:, :], statacc[0:1, 0:512], start=True, stop=True)
    pe_r3 = MM(pB[:, :], onesf[:, :], statacc[0:1, 512:1024], start=True, stop=True,
               sem="pe")
    c_x1 = [None] * DT
    for t in range(2):
        W(SY, "ve", fl.cnt["ve"])
        c_x1[t] = DMA(SY, "lr0" if t % 2 == 0 else "lr1", x1rel[t][:, :],
                      x1_sp[128 * t:128 * (t + 1), :])
    sq_out = None
    last_sq = {}
    for t in range(DT):
        W(VE, "lr0" if t % 2 == 0 else "lr1", c_x1[t])
        W(VE, "pe", pe_r3)
        for c in range(2):
            W(VE, "sq0" if c == 0 else "sq1", last_sq.get(c))
            OP(VE, "tensor_tensor", zt[:, :], xT[:, t, 512 * c:512 * (c + 1)],
               (pA if c == 0 else pB)[:, :], A.mult)
            OP(VE, "scalar_tensor_tensor", rr[:, :] if c == 0 else rr2[:, :], zt[:, :],
               gates[:, 80 + t:80 + t + 1], x1rel[t % 2][:, 512 * c:512 * (c + 1)],
               A.mult, A.add, sem="ve")
            W(SC, "ve", fl.cnt["ve"])
            sq_out = DMA(SC, "sq0" if c == 0 else "sq1",
                         out_d[128 * t:128 * (t + 1), 512 * c:512 * (c + 1)],
                         rr[:, :] if c == 0 else rr2[:, :])
            last_sq[c] = sq_out
        if t + 2 < DT:
            W(SY, "ve", fl.cnt["ve"])
            c_x1[t + 2] = DMA(SY, "lr0" if t % 2 == 0 else "lr1", x1rel[t % 2][:, :],
                              x1_sp[128 * (t + 2):128 * (t + 3), :])
    W(SC, "sq0", last_sq[0])
    W(SC, "sq1", last_sq[1])
    if debug:
        for dsn in ("db1", "db2", "db3", "db4", "db5"):
            W(SC, dsn, fl.cnt[dsn])

    sctx = contextlib.ExitStack()
    if True:
        for s in fl.cnt:
            fl.sems[s] = sctx.enter_context(nc.semaphore(s))
        with nc.Block() as block:
            @block.sync
            def _(e):
                for fn in fl.ops["sync"]:
                    fn(e)

            @block.tensor
            def _(e):
                for fn in fl.ops["tensor"]:
                    fn(e)

            @block.vector
            def _(e):
                for fn in fl.ops["vector"]:
                    fn(e)

            @block.scalar
            def _(e):
                for fn in fl.ops["scalar"]:
                    fn(e)

            @block.gpsimd
            def _(e):
                for fn in fl.ops["gpsimd"]:
                    fn(e)
    nc._keepalive = (ctx, sctx)
    return nc


_NC_CACHE = None


def kernel(L0_attn_Q_input, L0_attn_Wq, L0_attn_Wk, L0_attn_Wv, L0_attn_Wo,
           L0_ffn_W_fc, L0_ffn_W_proj, L0_c, L0_wmod):
    global _NC_CACHE
    f32 = np.float32
    x = np.ascontiguousarray(np.asarray(L0_attn_Q_input, f32).reshape(SEQ, D).T)
    wq = np.ascontiguousarray(
        np.asarray(L0_attn_Wq, f32).reshape(DT, 128, NH, 128).transpose(2, 1, 0, 3)
    ).reshape(NH, 128, DT * 128)
    wk = np.ascontiguousarray(
        np.asarray(L0_attn_Wk, f32).reshape(DT, 128, 128).transpose(1, 0, 2)
    ).reshape(128, DT * 128)
    wv = np.ascontiguousarray(
        np.asarray(L0_attn_Wv, f32).reshape(DT, 128, 128).transpose(1, 0, 2)
    ).reshape(128, DT * 128)
    wo = np.ascontiguousarray(np.asarray(L0_attn_Wo, f32))
    wfcb = np.ascontiguousarray(
        np.asarray(L0_ffn_W_fc, f32).reshape(NB, BH, DT, 128).transpose(0, 3, 2, 1)
    ).reshape(NB, 128, DT * BH)
    wpjb = np.ascontiguousarray(
        np.asarray(L0_ffn_W_proj, f32).T.reshape(NB, 2, 128, D).transpose(0, 2, 1, 3)
    ).reshape(NB, 128, 2 * D)
    cvec = np.ascontiguousarray(np.asarray(L0_c, f32).reshape(16, 128))
    wmod = np.asarray(L0_wmod, f32)
    identm = np.eye(128, dtype=ml_dtypes.bfloat16)

    if _NC_CACHE is None:
        _NC_CACHE = build_program()
    nc = _NC_CACHE

    in_maps = []
    for i in range(NCORES):
        in_maps.append({
            "xin": np.ascontiguousarray(x[:, S * i:S * (i + 1)]),
            "wqb": wq, "wkb": wk, "wvb": wv, "wo": wo,
            "wfcb": wfcb, "wpjb": wpjb, "cvec": cvec,
            "wmodc": np.ascontiguousarray(wmod[:, MODW * i:MODW * (i + 1)]),
            "ident": identm,
        })
    res = run_bass_kernel_spmd(nc, in_maps, core_ids=list(range(NCORES)))
    out = np.concatenate([np.asarray(r["out"]).T for r in res.results], axis=0)
    return np.ascontiguousarray(out.reshape(1, SEQ, D), dtype=np.float32)


# version 05
# speedup vs baseline: 1.0788x; 1.0201x over previous
import numpy as np
import ml_dtypes
import contextlib

import concourse.bass as bass
import concourse.mybir as mybir
from concourse.alu_op_type import AluOpType as A
from concourse.bass_utils import run_bass_kernel_spmd

F32 = mybir.dt.float32
BF16 = mybir.dt.bfloat16
I32 = mybir.dt.int32
I16 = mybir.dt.int16
ACT = mybir.ActivationFunctionType

NCORES = 8
D = 2048
DT = 16
S = 1024
NH = 16
TT = 64
SEQ = 8192
HID = 8192
NB = 32
BH = 256
MODW = 12288 // NCORES
SCALE = float(1.0 / np.sqrt(128.0))
LN_EPS = 1e-5
RMS_EPS = float(np.finfo(np.float32).eps)
THR = 0.501953125


class Flow:
    ENGINES = ("sync", "tensor", "vector", "scalar", "gpsimd")

    def __init__(self):
        self.ops = {e: [] for e in self.ENGINES}
        self.cnt = {}
        self.sems = {}

    def sem(self, name):
        self.cnt[name] = 0

    def emit(self, engine, fn):
        self.ops[engine].append(fn)

    def wait(self, engine, sem, val):
        if val is None or val <= 0:
            return
        assert val <= self.cnt[sem], f"wait {sem}>={val} but only {self.cnt[sem]} scheduled"
        self.emit(engine, lambda e, s=sem, v=val: e.wait_ge(self.sems[s], v))

    def dma(self, engine, sem, out, in_, **kw):
        self.cnt[sem] += 16
        def fn(e, out=out, in_=in_, s=sem, kw=kw):
            e.dma_start(out=out, in_=in_, **kw).then_inc(self.sems[s], 16)
        self.emit(engine, fn)
        return self.cnt[sem]

    def op(self, engine, method, *args, sem=None, drain=None, **kw):
        c = None
        if sem is not None:
            self.cnt[sem] += 1
            c = self.cnt[sem]
        if drain is None:
            drain = engine in ("vector", "scalar", "gpsimd")
        def fn(e, m=method, args=args, kw=kw, s=sem, d=drain):
            ins = getattr(e, m)(*args, **kw)
            if d:
                e.drain()
            if s is not None:
                ins.then_inc(self.sems[s], 1)
        self.emit(engine, fn)
        return c

    def mm(self, out, lhsT, rhs, start, stop, sem=None, tile_position=None):
        kw = {}
        if tile_position is not None:
            kw["tile_position"] = tile_position
        return self.op("tensor", "matmul", out, lhsT, rhs, start=start, stop=stop,
                       sem=sem, **kw)


def build_program(debug=False):
    nc = bass.Bass("TRN2", target_bir_lowering=False)
    fl = Flow()
    ctx = contextlib.ExitStack()
    sbuf = lambda n, s, d: ctx.enter_context(nc.sbuf_tensor(n, s, d))
    psum = lambda n: ctx.enter_context(nc.psum_tensor(n, [128, 512], F32))

    din = lambda n, s: nc.dram_tensor(n, s, F32, kind="ExternalInput").ap()
    xin = din("xin", [D, S])
    wqb_d = din("wqb", [NH, 128, DT * 128])
    wkb_d = din("wkb", [128, DT * 128])
    wvb_d = din("wvb", [128, DT * 128])
    wo_d = din("wo", [D, D])
    wfcb_d = din("wfcb", [NB, 128, DT * BH])
    wpjb_d = din("wpjb", [NB, 128, 2 * D])
    c_d = din("cvec", [16, 128])
    wmod_d = din("wmodc", [D, MODW])
    ident_d = nc.dram_tensor("ident", [128, 128], BF16, kind="ExternalInput").ap()
    out_d = nc.dram_tensor("out", [D, S], F32, kind="ExternalOutput").ap()

    if debug:
        dbg = {n: nc.dram_tensor("dbg_" + n, sh, dt, kind="ExternalOutput").ap()
               for n, sh, dt in [
                   ("gates", [128, 96], F32), ("h1", [128, 1024], BF16),
                   ("qt0", [128, 1024], BF16), ("kt", [128, 2048], BF16),
                   ("vaug", [128, 258], BF16), ("ao0", [128, 1024], BF16),
                   ("x1", [128, 1024], F32), ("h2", [128, 1024], BF16),
                   ("wq1b", [128, 4096], BF16), ("acc0", [128, 1024], F32),
                   ("rstd3", [1, 1024], F32), ("wq2b", [128, 4096], BF16),
               ]}
    mod_in = nc.dram_tensor("mod_in", [1, MODW], F32).ap()
    mod_out = nc.dram_tensor("mod_out", [NCORES, MODW], F32).ap()
    kt_in = nc.dram_tensor("kt_in", [128, S], BF16).ap()
    kt_out = nc.dram_tensor("kt_out", [NCORES * 128, S], BF16).ap()
    vn_in = nc.dram_tensor("vn_in", [S, 128], BF16).ap()
    vn_out = nc.dram_tensor("vn_out", [SEQ, 128], BF16).ap()
    x1_sp = nc.dram_tensor("x1_spill", [D, S], F32).ap()
    ssc_fc_d = nc.dram_tensor("ssc_fc", [128, 1024], BF16).ap()
    tlo_fc_d = nc.dram_tensor("tlo_fc", [128, 1024], BF16).ap()
    ssc_pj_d = nc.dram_tensor("ssc_pj", [128, 2048], BF16).ap()
    tlo_pj_d = nc.dram_tensor("tlo_pj", [128, 2048], BF16).ap()

    xT = sbuf("xT", [128, DT, S], F32)
    hh = sbuf("hh", [128, DT, S], BF16)
    AR = sbuf("AR", [128, 24576], F32)
    ident = sbuf("ident_s", [128, 128], BF16)
    identf = sbuf("identf", [128, 128], F32)
    ones1 = sbuf("ones1", [128, 1], BF16)
    onesf = sbuf("onesf", [1, 128], F32)
    G2 = sbuf("G2", [128, 2], BF16)
    maskI = sbuf("maskI", [128, 1], I32)
    maskA32 = sbuf("maskA32", [128, 1], I32)
    maskA16 = sbuf("maskA16", [128, 1], I16)
    maskPK = sbuf("maskPK", [128, 1], I32)
    modT = sbuf("modT", [128, 96], F32)
    modraw = sbuf("modraw", [96, 128], F32)
    gates = sbuf("gates", [128, 96], F32)
    silu_s = sbuf("silu_s", [16, 128], F32)
    siluT = sbuf("siluT", [128, 16], F32)
    siluh = sbuf("siluh", [128, 16], BF16)
    silul = sbuf("silul", [128, 16], BF16)
    silul_f = sbuf("silul_f", [128, 16], F32)
    statacc = sbuf("statacc", [1, S], F32)
    zt = sbuf("zt", [128, 512], F32)
    rr = sbuf("rr", [128, 512], F32)
    rr2 = sbuf("rr2", [128, 512], F32)
    hsqT = sbuf("hsqT", [128, 1024], BF16)
    ao_inv = sbuf("ao_inv", [128, 4], F32)

    def arv(off, nf32, dtype=F32):
        ap = AR[:, off:off + nf32]
        return ap.bitcast(dtype) if dtype != F32 else ap

    wms = [arv(0, 1536), arv(1536, 1536)]
    wmh = arv(3072, 768, BF16)
    wml = arv(3840, 768, BF16)
    xb_t = [arv(4608, 512, BF16), arv(5120, 512, BF16)]
    xsq_t = [arv(5632, 512, BF16), arv(6144, 512, BF16)]
    QT = arv(0, 8192, BF16)
    KT = arv(8192, 4096, BF16)
    Vaug = arv(12288, 4128, BF16)
    probs = arv(16416, 1024, BF16)
    wqh = [arv(17440, 1024, BF16), arv(18464, 1024, BF16)]
    kv_ev = arv(19488, 1024, BF16)
    vnat = arv(20512, 512, BF16)
    wo_b = [arv(0, 1024, BF16), arv(1024, 1024, BF16)]
    wbs1 = [arv(0, 2048, BF16), arv(2048, 2048, BF16)]
    wbs2 = [arv(4096, 2048, BF16), arv(6144, 2048, BF16)]
    wq1 = [arv(8192, 2048, BF16), arv(10240, 2048, BF16)]
    wq2 = arv(12288, 4096, BF16)
    hsb = arv(16384, 2048, BF16)
    scb = arv(18432, 1024, BF16)
    scb2 = arv(19456, 1024, BF16)
    sct = arv(20480, 1024)
    ab_t = arv(21504, 1024, BF16)
    bc_s = arv(22528, 1024, BF16)
    bc_t = arv(23552, 1024, BF16)
    x1rel = [arv(0, 1024), arv(1024, 1024)]
    p0tmpA = arv(6656, 1024)
    p0tmpB = arv(7680, 512)
    aox = arv(23072, 256, BF16)
    wkb = arv(21024, 1024, BF16)
    wvb = arv(22048, 1024, BF16)
    stat = arv(6656, 512)[0:1, :]
    stat2 = arv(7168, 512)[0:1, :]
    stat3 = arv(7680, 512)[0:1, :]
    zt0 = zt

    pA, pB = psum("pA"), psum("pB")
    pCD = ctx.enter_context(nc.psum_tensor("pCD", [128, 1024], F32))
    pC, pD = pCD[:, 0:512], pCD[:, 512:1024]
    pEF = ctx.enter_context(nc.psum_tensor("pEF", [128, 1024], F32))
    pGH = ctx.enter_context(nc.psum_tensor("pGH", [128, 1024], F32))
    pE, pF = pEF[:, 0:512], pEF[:, 512:1024]
    pG, pH = pGH[:, 0:512], pGH[:, 512:1024]
    pAb = pA[:, 0:64].bitcast(BF16)
    pBb = pB[:, 0:64].bitcast(BF16)

    for s in ("ld", "gq", "sq", "pe", "ve", "ac", "cc", "ms", "lc", "lx",
              "lm0", "lm1", "lq0", "lq1", "lo0", "lo1", "lf0", "lf1",
              "lp0", "lp1", "lr0", "lr1", "dbgs", "db1", "db2", "db3", "db4", "db5", "lmr", "lkv", "lsc", "sq0", "sq1"):
        fl.sem(s)

    SY, PE, VE, SC, GP = "sync", "tensor", "vector", "scalar", "gpsimd"
    W, DMA, OP, MM = fl.wait, fl.dma, fl.op, fl.mm

    OP(VE, "memset", ones1[:, :], 1.0, sem="ms")
    OP(VE, "memset", onesf[:, :], 1.0, sem="ms")
    OP(VE, "memset", G2[0:64, 0:1], 1.0, sem="ms")
    OP(VE, "memset", G2[64:128, 0:1], 0.0, sem="ms")
    OP(VE, "memset", G2[0:64, 1:2], 0.0, sem="ms")
    OP(VE, "memset", G2[64:128, 1:2], 1.0, sem="ms")
    OP(VE, "memset", maskI[:, :], -65536, sem="ms")
    OP(VE, "memset", maskA32[:, :], 2147483647, sem="ms")
    OP(VE, "memset", maskA16[:, :], 32767, sem="ms")
    OP(VE, "memset", maskPK[:, :], 2147450879, sem="ms")
    OP(VE, "memset", Vaug[:, :], 1.0, sem="ms")
    OP(VE, "memset", statacc[:, :], 0.0, sem="ms")
    ms_const = fl.cnt["ms"]
    c_id = DMA(SY, "lc", ident[:, :], ident_d[:, :])
    c_c = DMA(SY, "lc", silu_s[:, :], c_d[:, :])
    W(VE, "lc", 32)
    OP(VE, "tensor_copy", identf[:, :], ident[:, :], sem="ve")

    W(PE, "ms", ms_const)
    W(VE, "ms", ms_const)
    W(SC, "ms", ms_const)
    W(GP, "ms", ms_const)
    W(PE, "lc", 32)
    W(PE, "ve", fl.cnt["ve"])
    OP(PE, "transpose", pA[:, 0:16], silu_s[:, :], identf[0:16, 0:16], sem="pe")
    W(VE, "pe", fl.cnt["pe"])
    OP(VE, "tensor_copy", siluT[:, :], pA[:, 0:16], sem="ve")
    W(SC, "ve", fl.cnt["ve"])
    OP(SC, "activation", siluT[:, :], siluT[:, :], ACT.Silu, sem="ac")
    W(VE, "ac", fl.cnt["ac"])
    OP(VE, "tensor_copy", siluh[:, :], siluT[:, :])
    OP(VE, "tensor_copy", silul_f[:, :], siluh[:, :])
    OP(VE, "tensor_tensor", silul_f[:, :], siluT[:, :], silul_f[:, :], A.subtract)
    OP(VE, "tensor_copy", silul[:, :], silul_f[:, :], sem="ve")
    ve_silu = fl.cnt["ve"]

    pe_modslab = [None] * DT
    for t in range(DT):
        if t >= 2:
            W(SY, "pe", pe_modslab[t - 2])
        c_wm = DMA(SY, "lm0" if t % 2 == 0 else "lm1", wms[t % 2][:, :],
                   wmod_d[128 * t:128 * (t + 1), :])
        W(VE, "lm0" if t % 2 == 0 else "lm1", c_wm)
        if t >= 1:
            W(VE, "pe", pe_modslab[t - 1])
        OP(VE, "tensor_copy", wmh[:, :], wms[t % 2][:, :])
        OP(VE, "tensor_copy", p0tmpA[:, :], wmh[:, 0:1024])
        OP(VE, "tensor_copy", p0tmpB[:, :], wmh[:, 1024:1536])
        OP(VE, "tensor_tensor", p0tmpA[:, :], wms[t % 2][:, 0:1024], p0tmpA[:, :],
           A.subtract)
        OP(VE, "tensor_tensor", p0tmpB[:, :], wms[t % 2][:, 1024:1536], p0tmpB[:, :],
           A.subtract)
        OP(VE, "tensor_copy", wml[:, 0:1024], p0tmpA[:, :])
        OP(VE, "tensor_copy", wml[:, 1024:1536], p0tmpB[:, :], sem="ve")
        W(PE, "ve", fl.cnt["ve"])
        if t == 0:
            W(PE, "ve", ve_silu)
        for ci, bank in ((0, pB), (1, pC), (2, pD)):
            n0 = 512 * ci
            MM(bank[0:1, :], siluh[:, t:t + 1], wmh[:, n0:n0 + 512],
               start=(t == 0), stop=False)
            MM(bank[0:1, :], siluh[:, t:t + 1], wml[:, n0:n0 + 512],
               start=False, stop=False)
            pe_modslab[t] = MM(bank[0:1, :], silul[:, t:t + 1], wmh[:, n0:n0 + 512],
                               start=False, stop=(t == DT - 1), sem="pe")
    pe_mod = fl.cnt["pe"]
    W(VE, "pe", pe_mod)
    OP(VE, "tensor_copy", stat[0:1, :], pB[0:1, :])
    OP(VE, "tensor_copy", stat2[0:1, :], pC[0:1, :])
    OP(VE, "tensor_copy", stat3[0:1, :], pD[0:1, :], sem="ve")
    ve_modev = fl.cnt["ve"]
    gq_modw = None
    W(GP, "ve", ve_modev)
    for ci, src in ((0, stat), (1, stat2), (2, stat3)):
        gq_modw = DMA(GP, "gq", mod_in[0:1, 512 * ci:512 * (ci + 1)], src[0:1, :])
    W(GP, "gq", gq_modw)
    cc_mod = OP(GP, "collective_compute", "AllGather", mybir.AluOpType.bypass,
                replica_groups=[list(range(NCORES))],
                ins=[mod_in], outs=[mod_out], sem="cc")
    W(GP, "cc", cc_mod)
    gq_modr = DMA(GP, "lmr", modraw[:, :],
                  mod_out.rearrange("a b -> (a b)").rearrange("(r c) -> r c", c=128))
    W(PE, "lmr", gq_modr)
    OP(PE, "transpose", pA[:, 0:96], modraw[:, :], identf[0:96, 0:96], sem="pe")
    pe_modT = fl.cnt["pe"]
    W(VE, "pe", pe_modT)
    OP(VE, "tensor_copy", modT[:, :], pA[:, 0:96])
    OP(VE, "tensor_scalar", gates[:, 0:16], modT[:, 16:32], 1.0, None, A.add)
    OP(VE, "tensor_copy", gates[:, 16:32], modT[:, 0:16])
    OP(VE, "tensor_scalar", gates[:, 48:64], modT[:, 64:80], 1.0, None, A.add)
    OP(VE, "tensor_copy", gates[:, 64:80], modT[:, 48:64])
    for (gsrc, gdst) in ((modT[:, 32:48], gates[:, 32:48]),
                         (modT[:, 80:96], gates[:, 80:96])):
        OP(VE, "tensor_scalar", zt0[:, 0:16], gsrc, 0.0, 2.0, A.is_ge, A.mult)
        OP(VE, "tensor_scalar", zt0[:, 0:16], zt0[:, 0:16], 1.0, None, A.subtract)
        OP(VE, "tensor_scalar", zt0[:, 16:32].bitcast(I32), gsrc.bitcast(I32),
           maskA32[:, :], None, A.bitwise_and)
        OP(VE, "tensor_scalar", zt0[:, 16:32], zt0[:, 16:32], 0.25, None, A.max)
        OP(VE, "tensor_tensor", gdst, zt0[:, 0:16], zt0[:, 16:32], A.mult)
    OP(VE, "tensor_copy", zt0[:, 0:1], zt0[:, 0:1], sem="ve")
    ve_gates = fl.cnt["ve"]

    c_x = [DMA(SY, "lx", xT[:, t, :], xin[128 * t:128 * (t + 1), :]) for t in range(DT)]
    pe_stat = [None] * DT
    for t in range(DT):
        W(VE, "lx", c_x[DT - 1])
        if t >= 2:
            W(VE, "pe", pe_stat[t - 2])
        OP(VE, "tensor_copy", xb_t[t % 2][:, :], xT[:, t, :], sem="ve")
        W(SC, "ve", fl.cnt["ve"])
        OP(SC, "activation", xsq_t[t % 2][:, :], xb_t[t % 2][:, :], ACT.Square, sem="ac")
        W(PE, "ac", fl.cnt["ac"])
        MM(pE[0:1, :], ones1[:, :], xb_t[t % 2][:, 0:512], start=(t == 0), stop=(t == DT - 1))
        MM(pF[0:1, :], ones1[:, :], xb_t[t % 2][:, 512:1024], start=(t == 0), stop=(t == DT - 1))
        MM(pG[0:1, :], ones1[:, :], xsq_t[t % 2][:, 0:512], start=(t == 0), stop=(t == DT - 1))
        pe_stat[t] = MM(pH[0:1, :], ones1[:, :], xsq_t[t % 2][:, 512:1024],
                        start=(t == 0), stop=(t == DT - 1), sem="pe")

    def ln_apply(gcol0, pre_pe_waits):
        for c in range(2):
            bmu = pE if c == 0 else pF
            bms = pG if c == 0 else pH
            W(VE, "pe", fl.cnt["pe"])
            OP(VE, "tensor_scalar", stat[0:1, :], bmu[0:1, :], 1.0 / D, None, A.mult)
            OP(VE, "tensor_scalar", stat2[0:1, :], bms[0:1, :], 1.0 / D, None, A.mult)
            OP(VE, "tensor_tensor", stat3[0:1, :], stat[0:1, :], stat[0:1, :], A.mult)
            OP(VE, "tensor_tensor", stat2[0:1, :], stat2[0:1, :], stat3[0:1, :],
               A.subtract, sem="ve")
            OP(VE, "tensor_scalar", stat2[0:1, :], stat2[0:1, :], LN_EPS, None, A.add,
               sem="ve")
            W(SC, "ve", fl.cnt["ve"])
            OP(SC, "activation", stat2[0:1, :], stat2[0:1, :], ACT.Ln)
            OP(SC, "activation", stat2[0:1, :], stat2[0:1, :], ACT.Exp, scale=-0.5,
               sem="ac")
            for sname, v in pre_pe_waits:
                W(PE, sname, v)
            W(PE, "ac", fl.cnt["ac"])
            MM(pA[:, :], onesf[:, :], stat[0:1, :], start=True, stop=True)
            MM(pB[:, :], onesf[:, :], stat2[0:1, :], start=True, stop=True, sem="pe")
            pe_bc = fl.cnt["pe"]
            for t in range(DT):
                W(VE, "pe", pe_bc)
                OP(VE, "tensor_tensor", zt0[:, :], xT[:, t, 512 * c:512 * (c + 1)],
                   pA[:, :], A.subtract)
                OP(VE, "tensor_tensor", zt0[:, :], zt0[:, :], pB[:, :], A.mult)
                OP(VE, "scalar_tensor_tensor", hh[:, t, 512 * c:512 * (c + 1)],
                   zt0[:, :], gates[:, gcol0 + t:gcol0 + t + 1],
                   gates[:, gcol0 + 16 + t:gcol0 + 16 + t + 1].broadcast_to([128, 512]),
                   A.mult, A.add, sem="ve")
            W(PE, "ve", fl.cnt["ve"])

    W(VE, "ve", ve_gates)
    ln_apply(0, [("gq", gq_modw), ("ve", ve_gates)])
    if debug:
        W(SY, "ve", ve_gates)
        DMA(SY, "db1", dbg["gates"], gates[:, :])
    ve_h1 = fl.cnt["ve"]
    d_h1 = 0
    if debug:
        W(SY, "ve", ve_h1)
        d_h1 = DMA(SY, "db1", dbg["h1"], hh[:, 0, :])

    c_wk = DMA(GP, "lkv", wkb[:, :], wkb_d[:, :])
    c_wv = DMA(GP, "lkv", wvb[:, :], wvb_d[:, :])
    c_wq = [None] * NH
    for h in range(2):
        c_wq[h] = DMA(GP, "lq0" if h % 2 == 0 else "lq1", wqh[h][:, :], wqb_d[h, :, :])
    ac_qev = [None] * NH
    W(SC, "ve", fl.cnt["ve"])
    for h in range(NH):
        W(PE, "lq0" if h % 2 == 0 else "lq1", c_wq[h])
        if h == 0:
            W(PE, "ve", ve_h1)
        pe_q = None
        for c, bank in ((0, pE), (1, pF)):
            for t in range(DT):
                pe_q = MM(bank[:, :], wqh[h % 2][:, 128 * t:128 * (t + 1)],
                          hh[:, t, 512 * c:512 * (c + 1)], start=(t == 0),
                          stop=(t == DT - 1),
                          sem=("pe" if (c == 1 and t == DT - 1) else None))
        if h + 2 < NH:
            W(GP, "pe", pe_q)
            c_wq[h + 2] = DMA(GP, "lq0" if h % 2 == 0 else "lq1", wqh[h % 2][:, :],
                              wqb_d[h + 2, :, :])
        W(SC, "pe", pe_q)
        OP(SC, "activation", QT[:, 1024 * h:1024 * h + 512], pE[:, :], ACT.Copy,
           drain=False)
        ac_qev[h] = OP(SC, "activation", QT[:, 1024 * h + 512:1024 * h + 1024], pF[:, :],
                       ACT.Copy, sem="ac")
        if h + 1 < NH:
            W(PE, "ac", ac_qev[h])
    W(PE, "lkv", 32)
    pe_k = None
    for c, bank in ((0, pG), (1, pH)):
        for t in range(DT):
            pe_k = MM(bank[:, :], wkb[:, 128 * t:128 * (t + 1)],
                      hh[:, t, 512 * c:512 * (c + 1)], start=(t == 0), stop=(t == DT - 1),
                      sem=("pe" if (c == 1 and t == DT - 1) else None))
    W(SC, "pe", pe_k)
    OP(SC, "activation", kv_ev[:, 0:512], pG[:, :], ACT.Copy)
    ac_kev = OP(SC, "activation", kv_ev[:, 512:1024], pH[:, :], ACT.Copy, sem="ac")
    W(PE, "ac", ac_kev)
    pe_v = None
    for c, bank in ((0, pG), (1, pH)):
        for t in range(DT):
            pe_v = MM(bank[:, :], wvb[:, 128 * t:128 * (t + 1)],
                      hh[:, t, 512 * c:512 * (c + 1)], start=(t == 0), stop=(t == DT - 1),
                      sem=("pe" if (c == 1 and t == DT - 1) else None))
    W(SC, "pe", pe_v)
    OP(SC, "activation", kv_ev[:, 1024:1536], pG[:, :], ACT.Copy)
    ac_vev = OP(SC, "activation", kv_ev[:, 1536:2048], pH[:, :], ACT.Copy, sem="ac")
    W(GP, "ac", ac_kev)
    gq_k = DMA(GP, "gq", kt_in[:, :], kv_ev[:, 0:1024])
    W(PE, "ac", ac_vev)
    ve_vn = None
    for i in range(8):
        tb = pAb if i % 2 == 0 else pBb
        pe_tv = OP(PE, "transpose", tb,
                   kv_ev[:, 1024 + 128 * i:1024 + 128 * (i + 1)], ident[:, :], sem="pe")
        W(VE, "pe", pe_tv)
        ve_vn = OP(VE, "tensor_copy", vnat[:, 128 * i:128 * (i + 1)], tb,
                   sem="ve")
        W(PE, "ve", ve_vn)
    W(GP, "ve", ve_vn)
    gq_v = DMA(GP, "gq", vn_in.rearrange("(i p) j -> p i j", p=128),
               vnat[:, :].rearrange("p (i j) -> p i j", j=128))
    W(GP, "gq", gq_v)
    cc_k = OP(GP, "collective_compute", "AllGather", mybir.AluOpType.bypass,
              replica_groups=[list(range(NCORES))], ins=[kt_in], outs=[kt_out], sem="cc")
    cc_v = OP(GP, "collective_compute", "AllGather", mybir.AluOpType.bypass,
              replica_groups=[list(range(NCORES))], ins=[vn_in], outs=[vn_out], sem="cc")
    W(GP, "cc", cc_v)
    gq_kt = None
    for j in range(NCORES):
        gq_kt = DMA(GP, "gq", KT[:, 1024 * j:1024 * (j + 1)], kt_out[128 * j:128 * (j + 1), :])
    for tt in range(TT):
        gq_kt = DMA(GP, "gq", Vaug[:, 129 * tt:129 * tt + 128],
                    vn_out[128 * tt:128 * (tt + 1), :])
    gq_kv_done = gq_kt
    if debug:
        W(SY, "gq", gq_kv_done)
        W(SY, "ac", ac_qev[0])
        DMA(SY, "db2", dbg["qt0"], QT[:, 0:1024])
        DMA(SY, "db2", dbg["kt"], KT[:, 0:2048])
        d_kv = DMA(SY, "db2", dbg["vaug"], Vaug[:, 0:258])

    W(PE, "gq", gq_kv_done)
    W(PE, "ms", ms_const)
    if debug:
        W(VE, "db1", d_h1)
    avb = (pE, pF, pG, pH)
    last_av_slot = {}
    last_exp_bank = {}
    for h in range(NH):
        for c in range(2):
            qs = QT[:, 1024 * h + 512 * c:1024 * h + 512 * (c + 1)]
            av_pe = {}
            exp_pair = {}
            for pk in range(TT // 2):
                t0 = 2 * pk
                if pk >= 1:
                    W(PE, "ac", exp_pair[pk - 1])
                else:
                    W(PE, "ac", last_exp_bank.get(0))
                MM(pC[:, :], KT[:, 128 * t0:128 * (t0 + 1)], qs, start=True,
                   stop=True)
                MM(pD[:, :], KT[:, 128 * (t0 + 1):128 * (t0 + 2)], qs, start=True,
                   stop=True, sem="pe")
                pe_sc = fl.cnt["pe"]
                s0 = t0 % 4
                W(SC, "pe", pe_sc)
                W(SC, "pe", last_av_slot.get(s0))
                OP(SC, "activation", probs[:, 512 * s0:512 * (s0 + 2)], pCD[:, :],
                   ACT.Exp, scale=SCALE, sem="ac", drain=False)
                exp_pair[pk] = fl.cnt["ac"]
                last_exp_bank[0] = exp_pair[pk]
                W(PE, "ac", exp_pair[pk])
                for tt in (t0, t0 + 1):
                    slot = tt % 4
                    for sl in range(4):
                        av_pe[tt] = MM(
                            avb[sl][:, 0:129],
                            probs[:, 512 * slot + 128 * sl:512 * slot + 128 * (sl + 1)],
                            Vaug[:, 129 * tt:129 * (tt + 1)],
                            start=(tt == 0), stop=(tt == TT - 1),
                            sem=("pe" if sl == 3 else None))
                last_av_slot[s0] = av_pe[t0 + 1]
            W(VE, "pe", av_pe[TT - 1])
            for sl in range(4):
                OP(VE, "reciprocal", ao_inv[:, sl:sl + 1], avb[sl][:, 128:129],
                   sem=("ve" if sl == 3 else None))
            W(SC, "ve", fl.cnt["ve"])
            ac_ao = None
            for sl in range(4):
                ac_ao = OP(SC, "activation", aox[:, 128 * sl:128 * (sl + 1)],
                           avb[sl][:, 0:128], ACT.Copy, scale=ao_inv[:, sl:sl + 1],
                           sem=("ac" if sl == 3 else None), drain=(sl == 3))
            W(PE, "ac", ac_ao)
            ve_ao = None
            ve_ao_sl = {}
            for sl in range(4):
                tb = pAb if sl % 2 == 0 else pBb
                if sl >= 2:
                    W(PE, "ve", ve_ao_sl[sl - 2])
                pe_tr = OP(PE, "transpose", tb, aox[:, 128 * sl:128 * (sl + 1)],
                           ident[:, :], sem="pe")
                W(VE, "pe", pe_tr)
                ve_ao = OP(VE, "tensor_copy",
                           hh[:, h, 512 * c + 128 * sl:512 * c + 128 * (sl + 1)],
                           tb, sem="ve")
                ve_ao_sl[sl] = ve_ao
            W(PE, "ve", ve_ao)

    W(PE, "ac", fl.cnt["ac"])
    W(PE, "ve", fl.cnt["ve"])
    d_ao0 = 0
    if debug:
        W(SY, "ve", fl.cnt["ve"])
        d_ao0 = DMA(SY, "db3", dbg["ao0"], hh[:, 0, :])
    banks8 = (pA, pB, pC, pD, pE, pF, pG, pH)
    order = [(c, half) for c in range(2) for half in range(2)]
    slab_ids = [(r, q) for r in range(4) for q in range(DT)]
    c_wo = {}
    W(GP, "pe", fl.cnt["pe"])
    W(GP, "ac", fl.cnt["ac"])
    W(GP, "ve", fl.cnt["ve"])
    if debug:
        W(GP, "db2", d_kv)
    for i in range(2):
        r, q = slab_ids[i]
        c_wo[(r, q)] = DMA(GP, "lo0" if i % 2 == 0 else "lo1", wo_b[i % 2][:, :],
                           wo_d[128 * q:128 * (q + 1), :])
    idx = 0
    ve_res = None
    for r, (c, half) in enumerate(order):
        for q in range(DT):
            W(PE, "lo0" if idx % 2 == 0 else "lo1", c_wo[(r, q)])
            pe_woq = None
            for k in range(8):
                pe_woq = MM(banks8[k][:, :],
                            wo_b[idx % 2][:, 1024 * half + 128 * k:1024 * half + 128 * (k + 1)],
                            hh[:, q, 512 * c:512 * (c + 1)], start=(q == 0),
                            stop=(q == DT - 1), sem=("pe" if k == 7 else None))
            if idx + 2 < len(slab_ids):
                r2, q2 = slab_ids[idx + 2]
                W(GP, "pe", pe_woq)
                c_wo[(r2, q2)] = DMA(GP, "lo0" if idx % 2 == 0 else "lo1",
                                     wo_b[idx % 2][:, :],
                                     wo_d[128 * q2:128 * (q2 + 1), :])
            idx += 1
        pe_wor = fl.cnt["pe"]
        ve_k2 = {}
        for k in range(8):
            dt_out = 8 * half + k
            rtmp = rr if k % 2 == 0 else rr2
            W(SC, "pe", pe_wor)
            W(SC, "ve", ve_k2.get(k - 2))
            ac_r = OP(SC, "activation", rtmp[:, :], banks8[k][:, :], ACT.Copy,
                      scale=gates[:, 32 + dt_out:32 + dt_out + 1], sem="ac")
            W(VE, "ac", ac_r)
            ve_res = OP(VE, "tensor_tensor", xT[:, dt_out, 512 * c:512 * (c + 1)],
                        xT[:, dt_out, 512 * c:512 * (c + 1)], rtmp[:, :], A.add, sem="ve")
            ve_k2[k] = ve_res
        W(PE, "ve", ve_res)

    d_x1m = 0
    if debug:
        W(SY, "ve", ve_res)
        d_x1m = DMA(SY, "db3", dbg["x1"], xT[:, 0, :])
    xb2 = wbs1
    xq2 = wbs2
    if debug:
        W(VE, "db3", max(d_ao0, d_x1m))
    pe_stat2 = [None] * DT
    for t in range(DT):
        W(VE, "ve", ve_res)
        if t >= 2:
            W(VE, "pe", pe_stat2[t - 2])
        OP(VE, "tensor_copy", xb2[t % 2][:, 0:1024], xT[:, t, :], sem="ve")
        W(SC, "ve", fl.cnt["ve"])
        OP(SC, "activation", xq2[t % 2][:, 0:1024], xb2[t % 2][:, 0:1024], ACT.Square,
           sem="ac")
        W(PE, "ac", fl.cnt["ac"])
        MM(pE[0:1, :], ones1[:, :], xb2[t % 2][:, 0:512], start=(t == 0), stop=(t == DT - 1))
        MM(pF[0:1, :], ones1[:, :], xb2[t % 2][:, 512:1024], start=(t == 0), stop=(t == DT - 1))
        MM(pG[0:1, :], ones1[:, :], xq2[t % 2][:, 0:512], start=(t == 0), stop=(t == DT - 1))
        pe_stat2[t] = MM(pH[0:1, :], ones1[:, :], xq2[t % 2][:, 512:1024],
                         start=(t == 0), stop=(t == DT - 1), sem="pe")
    ln_apply(48, [("ve", fl.cnt["ve"])])
    ve_h2 = fl.cnt["ve"]
    if debug:
        W(SY, "ve", ve_h2)
        DMA(SY, "db4", dbg["h2"], hh[:, 0, :])
    gq_spill = None
    W(GP, "ve", ve_h2)
    for t in range(DT):
        gq_spill = DMA(GP, "gq", x1_sp[128 * t:128 * (t + 1), :], xT[:, t, :])
    W(VE, "gq", gq_spill)
    OP(VE, "tensor_scalar", xT[:, :, :], xT[:, :, :], 0.0, None, A.mult, sem="ve")
    ve_zeroacc = fl.cnt["ve"]

    def quant(kind, wbstage, wqdst):
        sscd, tlod = (ssc_fc_d, tlo_fc_d) if kind == "fc" else (ssc_pj_d, tlo_pj_d)
        wide = 1024 if kind == "fc" else 2048
        OP(VE, "tensor_scalar", ab_t[:, :].bitcast(I32), wbstage[:, 0:2048].bitcast(I32),
           maskPK[:, :], None, A.bitwise_and)
        OP(VE, "tensor_scalar", scb[:, :].bitcast(I32), wbstage[:, 2048:4096].bitcast(I32),
           maskPK[:, :], None, A.bitwise_and, sem="ve")
        W(PE, "ve", fl.cnt["ve"])
        sb4 = (pA, pB, pC, pD)
        pe_sc_mm = None
        if kind == "fc":
            for t in range(DT):
                src = ab_t if t < 8 else scb
                rh = src[:, 256 * (t % 8):256 * (t % 8 + 1)]
                j = t % 4
                pe_sc_mm = MM(sb4[t // 4][32 * j:32 * j + 2, 0:256], G2[:, :], rh,
                              start=True, stop=True, tile_position=(0, 32 * j),
                              sem=("pe" if t == DT - 1 else None))
        else:
            for ht in range(2):
                src = ab_t if ht == 0 else scb
                for chn in range(4):
                    j = chn // 2
                    pe_sc_mm = MM(sb4[2 * ht + chn % 2][32 * j:32 * j + 2, 0:512],
                                  G2[:, :], src[:, 512 * chn:512 * (chn + 1)],
                                  start=True, stop=True, tile_position=(0, 32 * j),
                                  sem=("pe" if (ht == 1 and chn == 3) else None))
        W(VE, "pe", pe_sc_mm)
        seg = wide // 4
        for i in range(4):
            OP(VE, "tensor_scalar", scb2[:, seg * i:seg * (i + 1)], sb4[i][:, 0:seg],
               1.0 / 64.0, 1e-8, A.mult, A.max)
        for hw in range(wide // 1024):
            o = 1024 * hw
            OP(VE, "tensor_scalar", sct[:, :], scb2[:, o:o + 1024], THR, None, A.mult)
            OP(VE, "tensor_scalar", sct[:, :].bitcast(I32), sct[:, :].bitcast(I32),
               maskI[:, :], None, A.bitwise_and)
            OP(VE, "tensor_copy", scb[:, o:o + 1024], sct[:, :])
        OP(VE, "tensor_copy", zt0[:, 0:1], zt0[:, 0:1], sem="ve")
        ve_rows = fl.cnt["ve"]
        W(GP, "ve", ve_rows)
        DMA(GP, "lsc", sscd[:, 0:wide], scb2[:, 0:wide])
        c_rows = DMA(GP, "lsc", tlod[:, 0:wide], scb[:, 0:wide])
        W(GP, "lsc", c_rows)
        ve_q = None
        for half in range(2):
            if half == 1:
                W(GP, "ve", fl.cnt["ve"])
            gq_b = None
            for (dr_, dst) in ((sscd, bc_s), (tlod, bc_t)):
                for u in range(2):
                    if kind == "fc":
                        for bl in range(2):
                            bg = 2 * half + bl
                            src = dr_[:, 0:1024].rearrange(
                                "(j z) (b n) -> j z b n", j=4, b=4)
                            src = src[:, u:u + 1, bg:bg + 1].rearrange(
                                "j z b n -> (z b) j n")
                            src = src.broadcast_to([64, 4, 256])
                            dstv = dst[64 * u:64 * (u + 1),
                                       1024 * bl:1024 * (bl + 1)].rearrange(
                                "q (j n) -> q j n", j=4)
                            gq_b = DMA(GP, "gq", dstv, src)
                    else:
                        for c2 in range(2):
                            src = dr_[:, 0:2048].rearrange(
                                "(c2 z) (c1h n) -> c2 z c1h n", c2=4, c1h=4)
                            src = src[c2:c2 + 1, u:u + 1,
                                      2 * half:2 * half + 2].rearrange(
                                "c2 z c1 n -> (c2 z) c1 n")
                            src = src.broadcast_to([64, 2, 512])
                            dstv = dst[64 * u:64 * (u + 1),
                                       1024 * c2:1024 * (c2 + 1)].rearrange(
                                "q (c1 n) -> q c1 n", c1=2)
                            gq_b = DMA(GP, "gq", dstv, src)
            W(VE, "gq", gq_b)
            wb_h = wbstage[:, 2048 * half:2048 * (half + 1)]
            gt = ab_t[:, 0:2048]
            L = sct[:, :].bitcast(BF16)
            nw = scb2[:, 0:2048]
            OP(VE, "tensor_scalar", nw, wb_h, -1.0, None, A.mult)
            OP(VE, "tensor_tensor", gt, wb_h, bc_t[:, :], A.is_gt)
            OP(VE, "tensor_tensor", L, nw, bc_t[:, :], A.is_gt)
            OP(VE, "tensor_tensor", gt, gt, L, A.subtract)
            ve_q = OP(VE, "tensor_tensor", wqdst[:, 2048 * half:2048 * (half + 1)],
                      gt, bc_s[:, :], A.mult, sem="ve")
        return ve_q

    W(GP, "ve", ve_zeroacc)
    W(GP, "pe", fl.cnt["pe"])
    W(GP, "ac", fl.cnt["ac"])
    c_fc = [None] * NB
    c_pj = [None] * NB
    for i in range(2):
        c_fc[i] = DMA(GP, "lf0" if i % 2 == 0 else "lf1", wbs1[i][:, :], wfcb_d[i, :, :])
        c_pj[i] = DMA(GP, "lp0" if i % 2 == 0 else "lp1", wbs2[i][:, :], wpjb_d[i, :, :])
    ve_q2 = [None] * NB
    pe_fc = [None] * NB
    pe_pj = [None] * (NB // 2)
    gq_acc = [None] * (NB // 2)
    ac_relu = {}
    last_hsq_mm = {}
    for b in range(NB):
        sbx = b % 2
        if b >= 2:
            W(VE, "pe", pe_pj[b // 2 - 1])
            W(VE, "pe", pe_fc[b - 2])
        if debug and b == 2:
            W(VE, "db4", d_wq)
        W(VE, "lf0" if b % 2 == 0 else "lf1", c_fc[b])
        ve_q1 = quant("fc", wbs1[sbx], wq1[sbx])
        W(VE, "lp0" if b % 2 == 0 else "lp1", c_pj[b])
        ve_q2[b] = quant("pj", wbs2[sbx], wq2[:, 4096 * sbx:4096 * (sbx + 1)])
        if debug and b == 0:
            W(SY, "ve", ve_q2[b])
            d_wq = DMA(SY, "db4", dbg["wq1b"], wq1[0][:, :])
            d_wq = DMA(SY, "db4", dbg["wq2b"], wq2[:, 0:4096])

        if b + 2 < NB:
            W(GP, "ve", ve_q2[b])
            c_fc[b + 2] = DMA(GP, "lf0" if sbx == 0 else "lf1", wbs1[sbx][:, :],
                              wfcb_d[b + 2, :, :])
            c_pj[b + 2] = DMA(GP, "lp0" if sbx == 0 else "lp1", wbs2[sbx][:, :],
                              wpjb_d[b + 2, :, :])
        W(PE, "ve", ve_q1)
        if b == 0:
            W(PE, "ve", ve_h2)
        gidx = 0
        for ht in range(2):
            for c in range(2):
                bank = pE if gidx % 2 == 0 else pF
                W(PE, "ac", ac_relu.get(gidx - 2))
                pe_h1 = None
                for t in range(DT):
                    pe_h1 = MM(bank[:, :],
                               wq1[sbx][:, 256 * t + 128 * ht:256 * t + 128 * (ht + 1)],
                               hh[:, t, 512 * c:512 * (c + 1)], start=(t == 0),
                               stop=(t == DT - 1), sem=("pe" if t == DT - 1 else None))
                W(SC, "pe", pe_h1)
                ac_r = OP(SC, "activation", rr[:, :], bank[:, :], ACT.Relu, sem="ac")
                ac_relu[gidx] = ac_r
                W(VE, "ac", ac_r)
                W(VE, "pe", last_hsq_mm.get(c))
                hslot = hsb[:, 1024 * (2 * sbx + ht) + 512 * c:
                            1024 * (2 * sbx + ht) + 512 * (c + 1)]
                OP(VE, "tensor_tensor", hslot, rr[:, :], rr[:, :], A.mult)
                OP(VE, "tensor_tensor", hsqT[:, 512 * c:512 * (c + 1)], hslot, hslot,
                   A.mult, sem="ve")
                W(PE, "ve", fl.cnt["ve"])
                last_hsq_mm[c] = MM((pG if c == 0 else pH)[0:1, :], ones1[:, :],
                                    hsqT[:, 512 * c:512 * (c + 1)],
                                    start=(sbx == 0 and ht == 0),
                                    stop=(sbx == 1 and ht == 1), sem="pe")
                gidx += 1
        pe_fc[b] = fl.cnt["pe"]
        if b % 2 == 1:
            sbi = b // 2
            W(VE, "pe", pe_fc[b])
            OP(VE, "tensor_tensor", statacc[0:1, 0:512], pG[0:1, :], statacc[0:1, 0:512],
               A.add)
            OP(VE, "tensor_tensor", statacc[0:1, 512:1024], pH[0:1, :],
               statacc[0:1, 512:1024], A.add, sem="ve")
            ve_st = fl.cnt["ve"]
            W(PE, "ve", ve_st)
            W(PE, "ve", ve_q2[b])
            for rnd in range(8):
                slots = [(2 * rnd + a, cc2) for a in range(2) for cc2 in range(2)]
                pe_prd = None
                for sl, (dtl, c) in enumerate(slots):
                    bank = (pE, pF, pG, pH)[sl]
                    for k in range(4):
                        pe_prd = MM(bank[:, :],
                                    wq2[:, 2048 * k + 128 * dtl:2048 * k + 128 * (dtl + 1)],
                                    hsb[:, 1024 * k + 512 * c:1024 * k + 512 * (c + 1)],
                                    start=(k == 0), stop=(k == 3),
                                    sem=("pe" if (sl == 3 and k == 3) else None))
                W(VE, "pe", pe_prd)
                dt0 = 2 * rnd
                OP(VE, "tensor_tensor", xT[:, dt0, :], pEF[:, :], xT[:, dt0, :],
                   A.add, sem="ve", drain=False)
                gq_acc[sbi] = OP(VE, "tensor_tensor", xT[:, dt0 + 1, :], pGH[:, :],
                                 xT[:, dt0 + 1, :], A.add, sem="ve")
                W(PE, "ve", gq_acc[sbi])
            pe_pj[sbi] = fl.cnt["pe"]

    W(VE, "pe", pe_pj[-1])
    W(VE, "ve", gq_acc[-1])
    if debug:
        W(SY, "pe", pe_pj[-1])
        DMA(SY, "db5", dbg["acc0"], xT[:, 0, :])
        d_st0 = DMA(SY, "db5", dbg["rstd3"], statacc[0:1, :])
        W(VE, "db5", d_st0)
    OP(VE, "tensor_scalar", statacc[0:1, :], statacc[0:1, :], 1.0 / HID, None, A.mult,
       sem="ve")
    OP(VE, "tensor_scalar", statacc[0:1, :], statacc[0:1, :], RMS_EPS, None, A.add,
       sem="ve")
    W(SC, "ve", fl.cnt["ve"])
    OP(SC, "activation", statacc[0:1, :], statacc[0:1, :], ACT.Ln)
    OP(SC, "activation", statacc[0:1, :], statacc[0:1, :], ACT.Exp, scale=-0.5, sem="ac")
    W(PE, "ac", fl.cnt["ac"])
    MM(pA[:, :], onesf[:, :], statacc[0:1, 0:512], start=True, stop=True)
    pe_r3 = MM(pB[:, :], onesf[:, :], statacc[0:1, 512:1024], start=True, stop=True,
               sem="pe")
    c_x1 = [None] * DT
    for t in range(2):
        W(SY, "ve", fl.cnt["ve"])
        c_x1[t] = DMA(SY, "lr0" if t % 2 == 0 else "lr1", x1rel[t][:, :],
                      x1_sp[128 * t:128 * (t + 1), :])
    sq_out = None
    last_sq = {}
    for t in range(DT):
        W(VE, "lr0" if t % 2 == 0 else "lr1", c_x1[t])
        W(VE, "pe", pe_r3)
        for c in range(2):
            W(VE, "sq0" if c == 0 else "sq1", last_sq.get(c))
            OP(VE, "tensor_tensor", zt[:, :], xT[:, t, 512 * c:512 * (c + 1)],
               (pA if c == 0 else pB)[:, :], A.mult)
            OP(VE, "scalar_tensor_tensor", rr[:, :] if c == 0 else rr2[:, :], zt[:, :],
               gates[:, 80 + t:80 + t + 1], x1rel[t % 2][:, 512 * c:512 * (c + 1)],
               A.mult, A.add, sem="ve")
            W(SC, "ve", fl.cnt["ve"])
            sq_out = DMA(SC, "sq0" if c == 0 else "sq1",
                         out_d[128 * t:128 * (t + 1), 512 * c:512 * (c + 1)],
                         rr[:, :] if c == 0 else rr2[:, :])
            last_sq[c] = sq_out
        if t + 2 < DT:
            W(SY, "ve", fl.cnt["ve"])
            c_x1[t + 2] = DMA(SY, "lr0" if t % 2 == 0 else "lr1", x1rel[t % 2][:, :],
                              x1_sp[128 * (t + 2):128 * (t + 3), :])
    W(SC, "sq0", last_sq[0])
    W(SC, "sq1", last_sq[1])
    if debug:
        for dsn in ("db1", "db2", "db3", "db4", "db5"):
            W(SC, dsn, fl.cnt[dsn])

    sctx = contextlib.ExitStack()
    if True:
        for s in fl.cnt:
            fl.sems[s] = sctx.enter_context(nc.semaphore(s))
        with nc.Block() as block:
            @block.sync
            def _(e):
                for fn in fl.ops["sync"]:
                    fn(e)

            @block.tensor
            def _(e):
                for fn in fl.ops["tensor"]:
                    fn(e)

            @block.vector
            def _(e):
                for fn in fl.ops["vector"]:
                    fn(e)

            @block.scalar
            def _(e):
                for fn in fl.ops["scalar"]:
                    fn(e)

            @block.gpsimd
            def _(e):
                for fn in fl.ops["gpsimd"]:
                    fn(e)
    nc._keepalive = (ctx, sctx)
    return nc


_NC_CACHE = None


def kernel(L0_attn_Q_input, L0_attn_Wq, L0_attn_Wk, L0_attn_Wv, L0_attn_Wo,
           L0_ffn_W_fc, L0_ffn_W_proj, L0_c, L0_wmod):
    global _NC_CACHE
    f32 = np.float32
    x = np.ascontiguousarray(np.asarray(L0_attn_Q_input, f32).reshape(SEQ, D).T)
    wq = np.ascontiguousarray(
        np.asarray(L0_attn_Wq, f32).reshape(DT, 128, NH, 128).transpose(2, 1, 0, 3)
    ).reshape(NH, 128, DT * 128)
    wk = np.ascontiguousarray(
        np.asarray(L0_attn_Wk, f32).reshape(DT, 128, 128).transpose(1, 0, 2)
    ).reshape(128, DT * 128)
    wv = np.ascontiguousarray(
        np.asarray(L0_attn_Wv, f32).reshape(DT, 128, 128).transpose(1, 0, 2)
    ).reshape(128, DT * 128)
    wo = np.ascontiguousarray(np.asarray(L0_attn_Wo, f32))
    wfcb = np.ascontiguousarray(
        np.asarray(L0_ffn_W_fc, f32).reshape(NB, BH, DT, 128).transpose(0, 3, 2, 1)
    ).reshape(NB, 128, DT * BH)
    wpjb = np.ascontiguousarray(
        np.asarray(L0_ffn_W_proj, f32).T.reshape(NB, 2, 128, D).transpose(0, 2, 1, 3)
    ).reshape(NB, 128, 2 * D)
    cvec = np.ascontiguousarray(np.asarray(L0_c, f32).reshape(16, 128))
    wmod = np.asarray(L0_wmod, f32)
    identm = np.eye(128, dtype=ml_dtypes.bfloat16)

    if _NC_CACHE is None:
        _NC_CACHE = build_program()
    nc = _NC_CACHE

    in_maps = []
    for i in range(NCORES):
        in_maps.append({
            "xin": np.ascontiguousarray(x[:, S * i:S * (i + 1)]),
            "wqb": wq, "wkb": wk, "wvb": wv, "wo": wo,
            "wfcb": wfcb, "wpjb": wpjb, "cvec": cvec,
            "wmodc": np.ascontiguousarray(wmod[:, MODW * i:MODW * (i + 1)]),
            "ident": identm,
        })
    res = run_bass_kernel_spmd(nc, in_maps, core_ids=list(range(NCORES)))
    out = np.concatenate([np.asarray(r["out"]).T for r in res.results], axis=0)
    return np.ascontiguousarray(out.reshape(1, SEQ, D), dtype=np.float32)


# version 06
# speedup vs baseline: 1.1095x; 1.0284x over previous
import numpy as np
import ml_dtypes
import contextlib

import concourse.bass as bass
import concourse.mybir as mybir
from concourse.alu_op_type import AluOpType as A
from concourse.bass_utils import run_bass_kernel_spmd

F32 = mybir.dt.float32
BF16 = mybir.dt.bfloat16
I32 = mybir.dt.int32
I16 = mybir.dt.int16
ACT = mybir.ActivationFunctionType

NCORES = 8
D = 2048
DT = 16
S = 1024
NH = 16
TT = 64
SEQ = 8192
HID = 8192
NB = 32
BH = 256
MODW = 12288 // NCORES
SCALE = float(1.0 / np.sqrt(128.0))
LN_EPS = 1e-5
RMS_EPS = float(np.finfo(np.float32).eps)
THR = 0.501953125


class Flow:
    ENGINES = ("sync", "tensor", "vector", "scalar", "gpsimd")

    def __init__(self):
        self.ops = {e: [] for e in self.ENGINES}
        self.cnt = {}
        self.sems = {}

    def sem(self, name):
        self.cnt[name] = 0

    def emit(self, engine, fn):
        self.ops[engine].append(fn)

    def wait(self, engine, sem, val):
        if val is None or val <= 0:
            return
        assert val <= self.cnt[sem], f"wait {sem}>={val} but only {self.cnt[sem]} scheduled"
        self.emit(engine, lambda e, s=sem, v=val: e.wait_ge(self.sems[s], v))

    def dma(self, engine, sem, out, in_, **kw):
        self.cnt[sem] += 16
        def fn(e, out=out, in_=in_, s=sem, kw=kw):
            e.dma_start(out=out, in_=in_, **kw).then_inc(self.sems[s], 16)
        self.emit(engine, fn)
        return self.cnt[sem]

    def op(self, engine, method, *args, sem=None, drain=None, **kw):
        c = None
        if sem is not None:
            self.cnt[sem] += 1
            c = self.cnt[sem]
        if drain is None:
            drain = engine in ("vector", "scalar", "gpsimd")
        def fn(e, m=method, args=args, kw=kw, s=sem, d=drain):
            ins = getattr(e, m)(*args, **kw)
            if d:
                e.drain()
            if s is not None:
                ins.then_inc(self.sems[s], 1)
        self.emit(engine, fn)
        return c

    def mm(self, out, lhsT, rhs, start, stop, sem=None, tile_position=None):
        kw = {}
        if tile_position is not None:
            kw["tile_position"] = tile_position
        return self.op("tensor", "matmul", out, lhsT, rhs, start=start, stop=stop,
                       sem=sem, **kw)


def build_program(debug=False):
    nc = bass.Bass("TRN2", target_bir_lowering=False)
    fl = Flow()
    ctx = contextlib.ExitStack()
    sbuf = lambda n, s, d: ctx.enter_context(nc.sbuf_tensor(n, s, d))
    psum = lambda n: ctx.enter_context(nc.psum_tensor(n, [128, 512], F32))

    din = lambda n, s: nc.dram_tensor(n, s, F32, kind="ExternalInput").ap()
    xin = din("xin", [D, S])
    wqb_d = din("wqb", [NH, 128, DT * 128])
    wkb_d = din("wkb", [128, DT * 128])
    wvb_d = din("wvb", [128, DT * 128])
    wo_d = din("wo", [D, D])
    wfcb_d = din("wfcb", [NB, 128, DT * BH])
    wpjb_d = din("wpjb", [NB, 128, 2 * D])
    c_d = din("cvec", [16, 128])
    wmod_d = din("wmodc", [D, MODW])
    ident_d = nc.dram_tensor("ident", [128, 128], BF16, kind="ExternalInput").ap()
    out_d = nc.dram_tensor("out", [D, S], F32, kind="ExternalOutput").ap()

    if debug:
        dbg = {n: nc.dram_tensor("dbg_" + n, sh, dt, kind="ExternalOutput").ap()
               for n, sh, dt in [
                   ("gates", [128, 96], F32), ("h1", [128, 1024], BF16),
                   ("qt0", [128, 1024], BF16), ("kt", [128, 2048], BF16),
                   ("vaug", [128, 258], BF16), ("ao0", [128, 1024], BF16),
                   ("x1", [128, 1024], F32), ("h2", [128, 1024], BF16),
                   ("wq1b", [128, 4096], BF16), ("acc0", [128, 1024], F32),
                   ("rstd3", [1, 1024], F32), ("wq2b", [128, 4096], BF16),
               ]}
    mod_in = nc.dram_tensor("mod_in", [1, MODW], F32).ap()
    mod_out = nc.dram_tensor("mod_out", [NCORES, MODW], F32).ap()
    kt_in = nc.dram_tensor("kt_in", [128, S], BF16).ap()
    kt_out = nc.dram_tensor("kt_out", [NCORES * 128, S], BF16).ap()
    vn_in = nc.dram_tensor("vn_in", [S, 128], BF16).ap()
    vn_out = nc.dram_tensor("vn_out", [SEQ, 128], BF16).ap()
    x1_sp = nc.dram_tensor("x1_spill", [D, S], F32).ap()
    ssc_fc_d = nc.dram_tensor("ssc_fc", [128, 1024], BF16).ap()
    tlo_fc_d = nc.dram_tensor("tlo_fc", [128, 1024], BF16).ap()
    ssc_pj_d = nc.dram_tensor("ssc_pj", [128, 2048], BF16).ap()
    tlo_pj_d = nc.dram_tensor("tlo_pj", [128, 2048], BF16).ap()

    xT = sbuf("xT", [128, DT, S], F32)
    hh = sbuf("hh", [128, DT, S], BF16)
    AR = sbuf("AR", [128, 24576], F32)
    ident = sbuf("ident_s", [128, 128], BF16)
    identf = sbuf("identf", [128, 128], F32)
    ones1 = sbuf("ones1", [128, 1], BF16)
    onesf = sbuf("onesf", [1, 128], F32)
    G2 = sbuf("G2", [128, 2], BF16)
    maskI = sbuf("maskI", [128, 1], I32)
    maskA32 = sbuf("maskA32", [128, 1], I32)
    maskA16 = sbuf("maskA16", [128, 1], I16)
    maskPK = sbuf("maskPK", [128, 1], I32)
    modT = sbuf("modT", [128, 96], F32)
    modraw = sbuf("modraw", [96, 128], F32)
    gates = sbuf("gates", [128, 96], F32)
    silu_s = sbuf("silu_s", [16, 128], F32)
    siluT = sbuf("siluT", [128, 16], F32)
    siluh = sbuf("siluh", [128, 16], BF16)
    silul = sbuf("silul", [128, 16], BF16)
    silul_f = sbuf("silul_f", [128, 16], F32)
    statacc = sbuf("statacc", [1, S], F32)
    zt = sbuf("zt", [128, 512], F32)
    rr = sbuf("rr", [128, 512], F32)
    rr2 = sbuf("rr2", [128, 512], F32)
    hsqT = sbuf("hsqT", [128, 1024], BF16)
    ao_inv = sbuf("ao_inv", [128, 4], F32)

    def arv(off, nf32, dtype=F32):
        ap = AR[:, off:off + nf32]
        return ap.bitcast(dtype) if dtype != F32 else ap

    wms = [arv(0, 1536), arv(1536, 1536)]
    wmh = arv(3072, 768, BF16)
    wml = arv(3840, 768, BF16)
    xb_t = [arv(4608, 512, BF16), arv(5120, 512, BF16)]
    xsq_t = [arv(5632, 512, BF16), arv(6144, 512, BF16)]
    QT = arv(0, 8192, BF16)
    KT = arv(8192, 4096, BF16)
    Vaug = arv(12288, 4128, BF16)
    probs = arv(16416, 1024, BF16)
    wqh = [arv(17440, 1024, BF16), arv(18464, 1024, BF16)]
    kv_ev = arv(19488, 1024, BF16)
    vnat = arv(20512, 512, BF16)
    wo_b = [arv(0, 1024, BF16), arv(1024, 1024, BF16)]
    wbs1 = [arv(0, 2048, BF16), arv(2048, 2048, BF16)]
    wbs2 = [arv(4096, 2048, BF16), arv(6144, 2048, BF16)]
    wq1 = [arv(8192, 2048, BF16), arv(10240, 2048, BF16)]
    wq2 = arv(12288, 4096, BF16)
    hsb = arv(16384, 2048, BF16)
    scb = arv(18432, 1024, BF16)
    scb2 = arv(19456, 1024, BF16)
    sct = arv(20480, 1024)
    ab_t = arv(21504, 1024, BF16)
    bc_s = arv(22528, 1024, BF16)
    bc_t = arv(23552, 1024, BF16)
    x1rel = [arv(0, 1024), arv(1024, 1024)]
    p0tmpA = arv(6656, 1024)
    p0tmpB = arv(7680, 512)
    aox = arv(23072, 256, BF16)
    wkb = arv(21024, 1024, BF16)
    wvb = arv(22048, 1024, BF16)
    stat = arv(6656, 512)[0:1, :]
    stat2 = arv(7168, 512)[0:1, :]
    stat3 = arv(7680, 512)[0:1, :]
    zt0 = zt

    pAB = ctx.enter_context(nc.psum_tensor("pAB", [128, 1024], F32))
    pA, pB = pAB[:, 0:512], pAB[:, 512:1024]
    pCD = ctx.enter_context(nc.psum_tensor("pCD", [128, 1024], F32))
    pC, pD = pCD[:, 0:512], pCD[:, 512:1024]
    pEF = ctx.enter_context(nc.psum_tensor("pEF", [128, 1024], F32))
    pGH = ctx.enter_context(nc.psum_tensor("pGH", [128, 1024], F32))
    pE, pF = pEF[:, 0:512], pEF[:, 512:1024]
    pG, pH = pGH[:, 0:512], pGH[:, 512:1024]
    pAb = pAB[:, 0:64].bitcast(BF16)
    pBb = pAB[:, 512:576].bitcast(BF16)

    for s in ("ld", "gq", "sq", "pe", "ve", "ac", "cc", "ms", "lc", "lx",
              "lm0", "lm1", "lq0", "lq1", "lo0", "lo1", "lf0", "lf1",
              "lp0", "lp1", "lr0", "lr1", "dbgs", "db1", "db2", "db3", "db4", "db5", "lmr", "lkv", "lsc", "lsc2", "syb", "sq0", "sq1"):
        fl.sem(s)

    SY, PE, VE, SC, GP = "sync", "tensor", "vector", "scalar", "gpsimd"
    W, DMA, OP, MM = fl.wait, fl.dma, fl.op, fl.mm

    OP(VE, "memset", ones1[:, :], 1.0, sem="ms")
    OP(VE, "memset", onesf[:, :], 1.0, sem="ms")
    OP(VE, "memset", G2[0:64, 0:1], 1.0, sem="ms")
    OP(VE, "memset", G2[64:128, 0:1], 0.0, sem="ms")
    OP(VE, "memset", G2[0:64, 1:2], 0.0, sem="ms")
    OP(VE, "memset", G2[64:128, 1:2], 1.0, sem="ms")
    OP(VE, "memset", maskI[:, :], -65536, sem="ms")
    OP(VE, "memset", maskA32[:, :], 2147483647, sem="ms")
    OP(VE, "memset", maskA16[:, :], 32767, sem="ms")
    OP(VE, "memset", maskPK[:, :], 2147450879, sem="ms")
    OP(VE, "memset", Vaug[:, :], 1.0, sem="ms")
    OP(VE, "memset", statacc[:, :], 0.0, sem="ms")
    ms_const = fl.cnt["ms"]
    c_id = DMA(SY, "lc", ident[:, :], ident_d[:, :])
    c_c = DMA(SY, "lc", silu_s[:, :], c_d[:, :])
    W(VE, "lc", 32)
    OP(VE, "tensor_copy", identf[:, :], ident[:, :], sem="ve")

    W(PE, "ms", ms_const)
    W(VE, "ms", ms_const)
    W(SC, "ms", ms_const)
    W(GP, "ms", ms_const)
    W(PE, "lc", 32)
    W(PE, "ve", fl.cnt["ve"])
    OP(PE, "transpose", pA[:, 0:16], silu_s[:, :], identf[0:16, 0:16], sem="pe")
    W(VE, "pe", fl.cnt["pe"])
    OP(VE, "tensor_copy", siluT[:, :], pA[:, 0:16], sem="ve")
    W(SC, "ve", fl.cnt["ve"])
    OP(SC, "activation", siluT[:, :], siluT[:, :], ACT.Silu, sem="ac")
    W(VE, "ac", fl.cnt["ac"])
    OP(VE, "tensor_copy", siluh[:, :], siluT[:, :])
    OP(VE, "tensor_copy", silul_f[:, :], siluh[:, :])
    OP(VE, "tensor_tensor", silul_f[:, :], siluT[:, :], silul_f[:, :], A.subtract)
    OP(VE, "tensor_copy", silul[:, :], silul_f[:, :], sem="ve")
    ve_silu = fl.cnt["ve"]

    pe_modslab = [None] * DT
    for t in range(DT):
        if t >= 2:
            W(SY, "pe", pe_modslab[t - 2])
        c_wm = DMA(SY, "lm0" if t % 2 == 0 else "lm1", wms[t % 2][:, :],
                   wmod_d[128 * t:128 * (t + 1), :])
        W(VE, "lm0" if t % 2 == 0 else "lm1", c_wm)
        if t >= 1:
            W(VE, "pe", pe_modslab[t - 1])
        OP(VE, "tensor_copy", wmh[:, :], wms[t % 2][:, :])
        OP(VE, "tensor_copy", p0tmpA[:, :], wmh[:, 0:1024])
        OP(VE, "tensor_copy", p0tmpB[:, :], wmh[:, 1024:1536])
        OP(VE, "tensor_tensor", p0tmpA[:, :], wms[t % 2][:, 0:1024], p0tmpA[:, :],
           A.subtract)
        OP(VE, "tensor_tensor", p0tmpB[:, :], wms[t % 2][:, 1024:1536], p0tmpB[:, :],
           A.subtract)
        OP(VE, "tensor_copy", wml[:, 0:1024], p0tmpA[:, :])
        OP(VE, "tensor_copy", wml[:, 1024:1536], p0tmpB[:, :], sem="ve")
        W(PE, "ve", fl.cnt["ve"])
        if t == 0:
            W(PE, "ve", ve_silu)
        for ci, bank in ((0, pB), (1, pC), (2, pD)):
            n0 = 512 * ci
            MM(bank[0:1, :], siluh[:, t:t + 1], wmh[:, n0:n0 + 512],
               start=(t == 0), stop=False)
            MM(bank[0:1, :], siluh[:, t:t + 1], wml[:, n0:n0 + 512],
               start=False, stop=False)
            pe_modslab[t] = MM(bank[0:1, :], silul[:, t:t + 1], wmh[:, n0:n0 + 512],
                               start=False, stop=(t == DT - 1), sem="pe")
    pe_mod = fl.cnt["pe"]
    W(VE, "pe", pe_mod)
    OP(VE, "tensor_copy", stat[0:1, :], pB[0:1, :])
    OP(VE, "tensor_copy", stat2[0:1, :], pC[0:1, :])
    OP(VE, "tensor_copy", stat3[0:1, :], pD[0:1, :], sem="ve")
    ve_modev = fl.cnt["ve"]
    gq_modw = None
    W(GP, "ve", ve_modev)
    for ci, src in ((0, stat), (1, stat2), (2, stat3)):
        gq_modw = DMA(GP, "gq", mod_in[0:1, 512 * ci:512 * (ci + 1)], src[0:1, :])
    W(GP, "gq", gq_modw)
    cc_mod = OP(GP, "collective_compute", "AllGather", mybir.AluOpType.bypass,
                replica_groups=[list(range(NCORES))],
                ins=[mod_in], outs=[mod_out], sem="cc")
    W(GP, "cc", cc_mod)
    gq_modr = DMA(GP, "lmr", modraw[:, :],
                  mod_out.rearrange("a b -> (a b)").rearrange("(r c) -> r c", c=128))
    W(PE, "lmr", gq_modr)
    OP(PE, "transpose", pA[:, 0:96], modraw[:, :], identf[0:96, 0:96], sem="pe")
    pe_modT = fl.cnt["pe"]
    W(VE, "pe", pe_modT)
    OP(VE, "tensor_copy", modT[:, :], pA[:, 0:96])
    OP(VE, "tensor_scalar", gates[:, 0:16], modT[:, 16:32], 1.0, None, A.add)
    OP(VE, "tensor_copy", gates[:, 16:32], modT[:, 0:16])
    OP(VE, "tensor_scalar", gates[:, 48:64], modT[:, 64:80], 1.0, None, A.add)
    OP(VE, "tensor_copy", gates[:, 64:80], modT[:, 48:64])
    for (gsrc, gdst) in ((modT[:, 32:48], gates[:, 32:48]),
                         (modT[:, 80:96], gates[:, 80:96])):
        OP(VE, "tensor_scalar", zt0[:, 0:16], gsrc, 0.0, 2.0, A.is_ge, A.mult)
        OP(VE, "tensor_scalar", zt0[:, 0:16], zt0[:, 0:16], 1.0, None, A.subtract)
        OP(VE, "tensor_scalar", zt0[:, 16:32].bitcast(I32), gsrc.bitcast(I32),
           maskA32[:, :], None, A.bitwise_and)
        OP(VE, "tensor_scalar", zt0[:, 16:32], zt0[:, 16:32], 0.25, None, A.max)
        OP(VE, "tensor_tensor", gdst, zt0[:, 0:16], zt0[:, 16:32], A.mult)
    OP(VE, "tensor_copy", zt0[:, 0:1], zt0[:, 0:1], sem="ve")
    ve_gates = fl.cnt["ve"]

    c_x = [DMA(SY, "lx", xT[:, t, :], xin[128 * t:128 * (t + 1), :]) for t in range(DT)]
    pe_stat = [None] * DT
    for t in range(DT):
        W(VE, "lx", c_x[DT - 1])
        if t >= 2:
            W(VE, "pe", pe_stat[t - 2])
        OP(VE, "tensor_copy", xb_t[t % 2][:, :], xT[:, t, :], sem="ve")
        W(SC, "ve", fl.cnt["ve"])
        OP(SC, "activation", xsq_t[t % 2][:, :], xb_t[t % 2][:, :], ACT.Square, sem="ac")
        W(PE, "ac", fl.cnt["ac"])
        MM(pE[0:1, :], ones1[:, :], xb_t[t % 2][:, 0:512], start=(t == 0), stop=(t == DT - 1))
        MM(pF[0:1, :], ones1[:, :], xb_t[t % 2][:, 512:1024], start=(t == 0), stop=(t == DT - 1))
        MM(pG[0:1, :], ones1[:, :], xsq_t[t % 2][:, 0:512], start=(t == 0), stop=(t == DT - 1))
        pe_stat[t] = MM(pH[0:1, :], ones1[:, :], xsq_t[t % 2][:, 512:1024],
                        start=(t == 0), stop=(t == DT - 1), sem="pe")

    def ln_apply(gcol0, pre_pe_waits):
        for c in range(2):
            bmu = pE if c == 0 else pF
            bms = pG if c == 0 else pH
            W(VE, "pe", fl.cnt["pe"])
            OP(VE, "tensor_scalar", stat[0:1, :], bmu[0:1, :], 1.0 / D, None, A.mult)
            OP(VE, "tensor_scalar", stat2[0:1, :], bms[0:1, :], 1.0 / D, None, A.mult)
            OP(VE, "tensor_tensor", stat3[0:1, :], stat[0:1, :], stat[0:1, :], A.mult)
            OP(VE, "tensor_tensor", stat2[0:1, :], stat2[0:1, :], stat3[0:1, :],
               A.subtract, sem="ve")
            OP(VE, "tensor_scalar", stat2[0:1, :], stat2[0:1, :], LN_EPS, None, A.add,
               sem="ve")
            W(SC, "ve", fl.cnt["ve"])
            OP(SC, "activation", stat2[0:1, :], stat2[0:1, :], ACT.Ln)
            OP(SC, "activation", stat2[0:1, :], stat2[0:1, :], ACT.Exp, scale=-0.5,
               sem="ac")
            for sname, v in pre_pe_waits:
                W(PE, sname, v)
            W(PE, "ac", fl.cnt["ac"])
            MM(pA[:, :], onesf[:, :], stat[0:1, :], start=True, stop=True)
            MM(pB[:, :], onesf[:, :], stat2[0:1, :], start=True, stop=True, sem="pe")
            pe_bc = fl.cnt["pe"]
            for t in range(DT):
                W(VE, "pe", pe_bc)
                OP(VE, "tensor_tensor", zt0[:, :], xT[:, t, 512 * c:512 * (c + 1)],
                   pA[:, :], A.subtract)
                OP(VE, "tensor_tensor", zt0[:, :], zt0[:, :], pB[:, :], A.mult)
                OP(VE, "scalar_tensor_tensor", hh[:, t, 512 * c:512 * (c + 1)],
                   zt0[:, :], gates[:, gcol0 + t:gcol0 + t + 1],
                   gates[:, gcol0 + 16 + t:gcol0 + 16 + t + 1].broadcast_to([128, 512]),
                   A.mult, A.add, sem="ve")
            W(PE, "ve", fl.cnt["ve"])

    W(VE, "ve", ve_gates)
    ln_apply(0, [("gq", gq_modw), ("ve", ve_gates)])
    if debug:
        W(SY, "ve", ve_gates)
        DMA(SY, "db1", dbg["gates"], gates[:, :])
    ve_h1 = fl.cnt["ve"]
    d_h1 = 0
    if debug:
        W(SY, "ve", ve_h1)
        d_h1 = DMA(SY, "db1", dbg["h1"], hh[:, 0, :])

    c_wk = DMA(GP, "lkv", wkb[:, :], wkb_d[:, :])
    c_wv = DMA(GP, "lkv", wvb[:, :], wvb_d[:, :])
    c_wq = [None] * NH
    for h in range(2):
        c_wq[h] = DMA(GP, "lq0" if h % 2 == 0 else "lq1", wqh[h][:, :], wqb_d[h, :, :])
    ac_qev = [None] * NH
    W(SC, "ve", fl.cnt["ve"])
    for h in range(NH):
        W(PE, "lq0" if h % 2 == 0 else "lq1", c_wq[h])
        if h == 0:
            W(PE, "ve", ve_h1)
        pe_q = None
        for c, bank in ((0, pE), (1, pF)):
            for t in range(DT):
                pe_q = MM(bank[:, :], wqh[h % 2][:, 128 * t:128 * (t + 1)],
                          hh[:, t, 512 * c:512 * (c + 1)], start=(t == 0),
                          stop=(t == DT - 1),
                          sem=("pe" if (c == 1 and t == DT - 1) else None))
        if h + 2 < NH:
            W(GP, "pe", pe_q)
            c_wq[h + 2] = DMA(GP, "lq0" if h % 2 == 0 else "lq1", wqh[h % 2][:, :],
                              wqb_d[h + 2, :, :])
        W(SC, "pe", pe_q)
        OP(SC, "activation", QT[:, 1024 * h:1024 * h + 512], pE[:, :], ACT.Copy,
           drain=False)
        ac_qev[h] = OP(SC, "activation", QT[:, 1024 * h + 512:1024 * h + 1024], pF[:, :],
                       ACT.Copy, sem="ac")
        if h + 1 < NH:
            W(PE, "ac", ac_qev[h])
    W(PE, "lkv", 32)
    pe_k = None
    for c, bank in ((0, pG), (1, pH)):
        for t in range(DT):
            pe_k = MM(bank[:, :], wkb[:, 128 * t:128 * (t + 1)],
                      hh[:, t, 512 * c:512 * (c + 1)], start=(t == 0), stop=(t == DT - 1),
                      sem=("pe" if (c == 1 and t == DT - 1) else None))
    W(SC, "pe", pe_k)
    OP(SC, "activation", kv_ev[:, 0:512], pG[:, :], ACT.Copy)
    ac_kev = OP(SC, "activation", kv_ev[:, 512:1024], pH[:, :], ACT.Copy, sem="ac")
    W(PE, "ac", ac_kev)
    pe_v = None
    for c, bank in ((0, pG), (1, pH)):
        for t in range(DT):
            pe_v = MM(bank[:, :], wvb[:, 128 * t:128 * (t + 1)],
                      hh[:, t, 512 * c:512 * (c + 1)], start=(t == 0), stop=(t == DT - 1),
                      sem=("pe" if (c == 1 and t == DT - 1) else None))
    W(SC, "pe", pe_v)
    OP(SC, "activation", kv_ev[:, 1024:1536], pG[:, :], ACT.Copy)
    ac_vev = OP(SC, "activation", kv_ev[:, 1536:2048], pH[:, :], ACT.Copy, sem="ac")
    W(GP, "ac", ac_kev)
    gq_k = DMA(GP, "gq", kt_in[:, :], kv_ev[:, 0:1024])
    W(PE, "ac", ac_vev)
    ve_vn = None
    for i in range(8):
        tb = pAb if i % 2 == 0 else pBb
        pe_tv = OP(PE, "transpose", tb,
                   kv_ev[:, 1024 + 128 * i:1024 + 128 * (i + 1)], ident[:, :], sem="pe")
        W(VE, "pe", pe_tv)
        ve_vn = OP(VE, "tensor_copy", vnat[:, 128 * i:128 * (i + 1)], tb,
                   sem="ve")
        W(PE, "ve", ve_vn)
    W(GP, "ve", ve_vn)
    gq_v = DMA(GP, "gq", vn_in.rearrange("(i p) j -> p i j", p=128),
               vnat[:, :].rearrange("p (i j) -> p i j", j=128))
    W(GP, "gq", gq_v)
    cc_k = OP(GP, "collective_compute", "AllGather", mybir.AluOpType.bypass,
              replica_groups=[list(range(NCORES))], ins=[kt_in], outs=[kt_out], sem="cc")
    cc_v = OP(GP, "collective_compute", "AllGather", mybir.AluOpType.bypass,
              replica_groups=[list(range(NCORES))], ins=[vn_in], outs=[vn_out], sem="cc")
    W(GP, "cc", cc_v)
    gq_kt = None
    for j in range(NCORES):
        gq_kt = DMA(GP, "gq", KT[:, 1024 * j:1024 * (j + 1)], kt_out[128 * j:128 * (j + 1), :])
    for tt in range(TT):
        gq_kt = DMA(GP, "gq", Vaug[:, 129 * tt:129 * tt + 128],
                    vn_out[128 * tt:128 * (tt + 1), :])
    gq_kv_done = gq_kt
    if debug:
        W(SY, "gq", gq_kv_done)
        W(SY, "ac", ac_qev[0])
        DMA(SY, "db2", dbg["qt0"], QT[:, 0:1024])
        DMA(SY, "db2", dbg["kt"], KT[:, 0:2048])
        d_kv = DMA(SY, "db2", dbg["vaug"], Vaug[:, 0:258])

    W(PE, "gq", gq_kv_done)
    W(PE, "ms", ms_const)
    if debug:
        W(VE, "db1", d_h1)
    avb = (pE, pF, pG, pH)
    last_av_slot = {}
    last_exp_bank = {}
    for h in range(NH):
        for c in range(2):
            qs = QT[:, 1024 * h + 512 * c:1024 * h + 512 * (c + 1)]
            av_pe = {}
            exp_pair = {}
            for pk in range(TT // 2):
                t0 = 2 * pk
                sbb = pCD if pk % 2 == 0 else pAB
                if pk >= 2:
                    W(PE, "ac", exp_pair[pk - 2])
                else:
                    W(PE, "ac", last_exp_bank.get(pk % 2))
                MM(sbb[:, 0:512], KT[:, 128 * t0:128 * (t0 + 1)], qs, start=True,
                   stop=True)
                MM(sbb[:, 512:1024], KT[:, 128 * (t0 + 1):128 * (t0 + 2)], qs,
                   start=True, stop=True, sem="pe")
                pe_sc = fl.cnt["pe"]
                s0 = t0 % 4
                W(SC, "pe", pe_sc)
                W(SC, "pe", last_av_slot.get(s0))
                OP(SC, "activation", probs[:, 512 * s0:512 * (s0 + 2)], sbb[:, :],
                   ACT.Exp, scale=SCALE, sem="ac", drain=False)
                exp_pair[pk] = fl.cnt["ac"]
                last_exp_bank[pk % 2] = exp_pair[pk]
                W(PE, "ac", exp_pair[pk])
                for tt in (t0, t0 + 1):
                    slot = tt % 4
                    for sl in range(4):
                        av_pe[tt] = MM(
                            avb[sl][:, 0:129],
                            probs[:, 512 * slot + 128 * sl:512 * slot + 128 * (sl + 1)],
                            Vaug[:, 129 * tt:129 * (tt + 1)],
                            start=(tt == 0), stop=(tt == TT - 1),
                            sem=("pe" if sl == 3 else None))
                last_av_slot[s0] = av_pe[t0 + 1]
            W(VE, "pe", av_pe[TT - 1])
            for sl in range(4):
                OP(VE, "reciprocal", ao_inv[:, sl:sl + 1], avb[sl][:, 128:129],
                   sem=("ve" if sl == 3 else None))
            W(SC, "ve", fl.cnt["ve"])
            ac_ao = None
            for sl in range(4):
                ac_ao = OP(SC, "activation", aox[:, 128 * sl:128 * (sl + 1)],
                           avb[sl][:, 0:128], ACT.Copy, scale=ao_inv[:, sl:sl + 1],
                           sem=("ac" if sl == 3 else None), drain=(sl == 3))
            W(PE, "ac", ac_ao)
            ve_ao = None
            ve_ao_sl = {}
            for sl in range(4):
                tb = pAb if sl % 2 == 0 else pBb
                if sl >= 2:
                    W(PE, "ve", ve_ao_sl[sl - 2])
                pe_tr = OP(PE, "transpose", tb, aox[:, 128 * sl:128 * (sl + 1)],
                           ident[:, :], sem="pe")
                W(VE, "pe", pe_tr)
                ve_ao = OP(VE, "tensor_copy",
                           hh[:, h, 512 * c + 128 * sl:512 * c + 128 * (sl + 1)],
                           tb, sem="ve")
                ve_ao_sl[sl] = ve_ao
            W(PE, "ve", ve_ao)

    W(PE, "ac", fl.cnt["ac"])
    W(PE, "ve", fl.cnt["ve"])
    d_ao0 = 0
    if debug:
        W(SY, "ve", fl.cnt["ve"])
        d_ao0 = DMA(SY, "db3", dbg["ao0"], hh[:, 0, :])
    banks8 = (pA, pB, pC, pD, pE, pF, pG, pH)
    order = [(c, half) for c in range(2) for half in range(2)]
    slab_ids = [(r, q) for r in range(4) for q in range(DT)]
    c_wo = {}
    W(GP, "pe", fl.cnt["pe"])
    W(GP, "ac", fl.cnt["ac"])
    W(GP, "ve", fl.cnt["ve"])
    if debug:
        W(GP, "db2", d_kv)
    for i in range(2):
        r, q = slab_ids[i]
        c_wo[(r, q)] = DMA(GP, "lo0" if i % 2 == 0 else "lo1", wo_b[i % 2][:, :],
                           wo_d[128 * q:128 * (q + 1), :])
    idx = 0
    ve_res = None
    for r, (c, half) in enumerate(order):
        for q in range(DT):
            W(PE, "lo0" if idx % 2 == 0 else "lo1", c_wo[(r, q)])
            pe_woq = None
            for k in range(8):
                pe_woq = MM(banks8[k][:, :],
                            wo_b[idx % 2][:, 1024 * half + 128 * k:1024 * half + 128 * (k + 1)],
                            hh[:, q, 512 * c:512 * (c + 1)], start=(q == 0),
                            stop=(q == DT - 1), sem=("pe" if k == 7 else None))
            if idx + 2 < len(slab_ids):
                r2, q2 = slab_ids[idx + 2]
                W(GP, "pe", pe_woq)
                c_wo[(r2, q2)] = DMA(GP, "lo0" if idx % 2 == 0 else "lo1",
                                     wo_b[idx % 2][:, :],
                                     wo_d[128 * q2:128 * (q2 + 1), :])
            idx += 1
        pe_wor = fl.cnt["pe"]
        ve_k2 = {}
        for k in range(8):
            dt_out = 8 * half + k
            rtmp = rr if k % 2 == 0 else rr2
            W(SC, "pe", pe_wor)
            W(SC, "ve", ve_k2.get(k - 2))
            ac_r = OP(SC, "activation", rtmp[:, :], banks8[k][:, :], ACT.Copy,
                      scale=gates[:, 32 + dt_out:32 + dt_out + 1], sem="ac")
            W(VE, "ac", ac_r)
            ve_res = OP(VE, "tensor_tensor", xT[:, dt_out, 512 * c:512 * (c + 1)],
                        xT[:, dt_out, 512 * c:512 * (c + 1)], rtmp[:, :], A.add, sem="ve")
            ve_k2[k] = ve_res
        W(PE, "ve", ve_res)

    d_x1m = 0
    if debug:
        W(SY, "ve", ve_res)
        d_x1m = DMA(SY, "db3", dbg["x1"], xT[:, 0, :])
    xb2 = wbs1
    xq2 = wbs2
    if debug:
        W(VE, "db3", max(d_ao0, d_x1m))
    pe_stat2 = [None] * DT
    for t in range(DT):
        W(VE, "ve", ve_res)
        if t >= 2:
            W(VE, "pe", pe_stat2[t - 2])
        OP(VE, "tensor_copy", xb2[t % 2][:, 0:1024], xT[:, t, :], sem="ve")
        W(SC, "ve", fl.cnt["ve"])
        OP(SC, "activation", xq2[t % 2][:, 0:1024], xb2[t % 2][:, 0:1024], ACT.Square,
           sem="ac")
        W(PE, "ac", fl.cnt["ac"])
        MM(pE[0:1, :], ones1[:, :], xb2[t % 2][:, 0:512], start=(t == 0), stop=(t == DT - 1))
        MM(pF[0:1, :], ones1[:, :], xb2[t % 2][:, 512:1024], start=(t == 0), stop=(t == DT - 1))
        MM(pG[0:1, :], ones1[:, :], xq2[t % 2][:, 0:512], start=(t == 0), stop=(t == DT - 1))
        pe_stat2[t] = MM(pH[0:1, :], ones1[:, :], xq2[t % 2][:, 512:1024],
                         start=(t == 0), stop=(t == DT - 1), sem="pe")
    ln_apply(48, [("ve", fl.cnt["ve"])])
    ve_h2 = fl.cnt["ve"]
    if debug:
        W(SY, "ve", ve_h2)
        DMA(SY, "db4", dbg["h2"], hh[:, 0, :])
    gq_spill = None
    W(GP, "ve", ve_h2)
    for t in range(DT):
        gq_spill = DMA(GP, "gq", x1_sp[128 * t:128 * (t + 1), :], xT[:, t, :])
    W(VE, "gq", gq_spill)
    OP(VE, "tensor_scalar", xT[:, :, :], xT[:, :, :], 0.0, None, A.mult, sem="ve")
    ve_zeroacc = fl.cnt["ve"]

    def quant(kind, wbstage, wqdst):
        sscd, tlod = (ssc_fc_d, tlo_fc_d) if kind == "fc" else (ssc_pj_d, tlo_pj_d)
        wide = 1024 if kind == "fc" else 2048
        OP(VE, "tensor_scalar", ab_t[:, :].bitcast(I32), wbstage[:, 0:2048].bitcast(I32),
           maskPK[:, :], None, A.bitwise_and)
        OP(VE, "tensor_scalar", scb[:, :].bitcast(I32), wbstage[:, 2048:4096].bitcast(I32),
           maskPK[:, :], None, A.bitwise_and, sem="ve")
        W(PE, "ve", fl.cnt["ve"])
        sb4 = (pA, pB, pC, pD)
        pe_sc_mm = None
        if kind == "fc":
            for t in range(DT):
                src = ab_t if t < 8 else scb
                rh = src[:, 256 * (t % 8):256 * (t % 8 + 1)]
                j = t % 4
                pe_sc_mm = MM(sb4[t // 4][32 * j:32 * j + 2, 0:256], G2[:, :], rh,
                              start=True, stop=True, tile_position=(0, 32 * j),
                              sem=("pe" if t == DT - 1 else None))
        else:
            for ht in range(2):
                src = ab_t if ht == 0 else scb
                for chn in range(4):
                    j = chn // 2
                    pe_sc_mm = MM(sb4[2 * ht + chn % 2][32 * j:32 * j + 2, 0:512],
                                  G2[:, :], src[:, 512 * chn:512 * (chn + 1)],
                                  start=True, stop=True, tile_position=(0, 32 * j),
                                  sem=("pe" if (ht == 1 and chn == 3) else None))
        W(VE, "pe", pe_sc_mm)
        seg = wide // 4
        for i in range(4):
            OP(VE, "tensor_scalar", scb2[:, seg * i:seg * (i + 1)], sb4[i][:, 0:seg],
               1.0 / 64.0, 1e-8, A.mult, A.max)
        for hw in range(wide // 1024):
            o = 1024 * hw
            OP(VE, "tensor_scalar", sct[:, :], scb2[:, o:o + 1024], THR, None, A.mult)
            OP(VE, "tensor_scalar", sct[:, :].bitcast(I32), sct[:, :].bitcast(I32),
               maskI[:, :], None, A.bitwise_and)
            OP(VE, "tensor_copy", scb[:, o:o + 1024], sct[:, :])
        OP(VE, "tensor_copy", zt0[:, 0:1], zt0[:, 0:1], sem="ve")
        ve_rows = fl.cnt["ve"]
        W(GP, "ve", ve_rows)
        W(SY, "ve", ve_rows)
        DMA(GP, "lsc", sscd[:, 0:wide], scb2[:, 0:wide])
        c_row2 = DMA(SY, "lsc2", tlod[:, 0:wide], scb[:, 0:wide])
        W(GP, "lsc", fl.cnt["lsc"])
        W(SY, "lsc2", c_row2)
        ve_q = None
        for half in range(2):
            if half == 1:
                W(GP, "ve", fl.cnt["ve"])
                W(SY, "ve", fl.cnt["ve"])
            gq_b = None
            sy_b = None
            for (dr_, dst) in ((sscd, bc_s), (tlod, bc_t)):
                for u in range(2):
                    if kind == "fc":
                        for bl in range(2):
                            bg = 2 * half + bl
                            src = dr_[:, 0:1024].rearrange(
                                "(j z) (b n) -> j z b n", j=4, b=4)
                            src = src[:, u:u + 1, bg:bg + 1].rearrange(
                                "j z b n -> (z b) j n")
                            src = src.broadcast_to([64, 4, 256])
                            dstv = dst[64 * u:64 * (u + 1),
                                       1024 * bl:1024 * (bl + 1)].rearrange(
                                "q (j n) -> q j n", j=4)
                            if dst is bc_s:
                                gq_b = DMA(GP, "gq", dstv, src)
                            else:
                                sy_b = DMA(SY, "syb", dstv, src)
                    else:
                        for c2 in range(2):
                            src = dr_[:, 0:2048].rearrange(
                                "(c2 z) (c1h n) -> c2 z c1h n", c2=4, c1h=4)
                            src = src[c2:c2 + 1, u:u + 1,
                                      2 * half:2 * half + 2].rearrange(
                                "c2 z c1 n -> (c2 z) c1 n")
                            src = src.broadcast_to([64, 2, 512])
                            dstv = dst[64 * u:64 * (u + 1),
                                       1024 * c2:1024 * (c2 + 1)].rearrange(
                                "q (c1 n) -> q c1 n", c1=2)
                            if dst is bc_s:
                                gq_b = DMA(GP, "gq", dstv, src)
                            else:
                                sy_b = DMA(SY, "syb", dstv, src)
            W(VE, "gq", gq_b)
            W(VE, "syb", sy_b)
            wb_h = wbstage[:, 2048 * half:2048 * (half + 1)]
            gt = ab_t[:, 0:2048]
            L = sct[:, :].bitcast(BF16)
            nw = scb2[:, 0:2048]
            OP(VE, "tensor_scalar", nw, wb_h, -1.0, None, A.mult)
            OP(VE, "tensor_tensor", gt, wb_h, bc_t[:, :], A.is_gt)
            OP(VE, "tensor_tensor", L, nw, bc_t[:, :], A.is_gt)
            OP(VE, "tensor_tensor", gt, gt, L, A.subtract)
            ve_q = OP(VE, "tensor_tensor", wqdst[:, 2048 * half:2048 * (half + 1)],
                      gt, bc_s[:, :], A.mult, sem="ve")
        return ve_q

    W(GP, "ve", ve_zeroacc)
    W(GP, "pe", fl.cnt["pe"])
    W(GP, "ac", fl.cnt["ac"])
    c_fc = [None] * NB
    c_pj = [None] * NB
    for i in range(2):
        c_fc[i] = DMA(GP, "lf0" if i % 2 == 0 else "lf1", wbs1[i][:, :], wfcb_d[i, :, :])
        c_pj[i] = DMA(GP, "lp0" if i % 2 == 0 else "lp1", wbs2[i][:, :], wpjb_d[i, :, :])
    ve_q2 = [None] * NB
    pe_fc = [None] * NB
    pe_pj = [None] * (NB // 2)
    gq_acc = [None] * (NB // 2)
    ac_relu = {}
    last_hsq_mm = {}
    for b in range(NB):
        sbx = b % 2
        if b >= 2:
            W(VE, "pe", pe_pj[b // 2 - 1])
            W(VE, "pe", pe_fc[b - 2])
        if debug and b == 2:
            W(VE, "db4", d_wq)
        W(VE, "lf0" if b % 2 == 0 else "lf1", c_fc[b])
        ve_q1 = quant("fc", wbs1[sbx], wq1[sbx])
        W(VE, "lp0" if b % 2 == 0 else "lp1", c_pj[b])
        ve_q2[b] = quant("pj", wbs2[sbx], wq2[:, 4096 * sbx:4096 * (sbx + 1)])
        if debug and b == 0:
            W(SY, "ve", ve_q2[b])
            d_wq = DMA(SY, "db4", dbg["wq1b"], wq1[0][:, :])
            d_wq = DMA(SY, "db4", dbg["wq2b"], wq2[:, 0:4096])

        if b + 2 < NB:
            W(GP, "ve", ve_q2[b])
            c_fc[b + 2] = DMA(GP, "lf0" if sbx == 0 else "lf1", wbs1[sbx][:, :],
                              wfcb_d[b + 2, :, :])
            c_pj[b + 2] = DMA(GP, "lp0" if sbx == 0 else "lp1", wbs2[sbx][:, :],
                              wpjb_d[b + 2, :, :])
        W(PE, "ve", ve_q1)
        if b == 0:
            W(PE, "ve", ve_h2)
        gidx = 0
        for ht in range(2):
            for c in range(2):
                bank = pE if gidx % 2 == 0 else pF
                W(PE, "ac", ac_relu.get(gidx - 2))
                pe_h1 = None
                for t in range(DT):
                    pe_h1 = MM(bank[:, :],
                               wq1[sbx][:, 256 * t + 128 * ht:256 * t + 128 * (ht + 1)],
                               hh[:, t, 512 * c:512 * (c + 1)], start=(t == 0),
                               stop=(t == DT - 1), sem=("pe" if t == DT - 1 else None))
                W(SC, "pe", pe_h1)
                ac_r = OP(SC, "activation", rr[:, :], bank[:, :], ACT.Relu, sem="ac")
                ac_relu[gidx] = ac_r
                W(VE, "ac", ac_r)
                W(VE, "pe", last_hsq_mm.get(c))
                hslot = hsb[:, 1024 * (2 * sbx + ht) + 512 * c:
                            1024 * (2 * sbx + ht) + 512 * (c + 1)]
                OP(VE, "tensor_tensor", hslot, rr[:, :], rr[:, :], A.mult)
                OP(VE, "tensor_tensor", hsqT[:, 512 * c:512 * (c + 1)], hslot, hslot,
                   A.mult, sem="ve")
                W(PE, "ve", fl.cnt["ve"])
                last_hsq_mm[c] = MM((pG if c == 0 else pH)[0:1, :], ones1[:, :],
                                    hsqT[:, 512 * c:512 * (c + 1)],
                                    start=(sbx == 0 and ht == 0),
                                    stop=(sbx == 1 and ht == 1), sem="pe")
                gidx += 1
        pe_fc[b] = fl.cnt["pe"]
        if b % 2 == 1:
            sbi = b // 2
            W(VE, "pe", pe_fc[b])
            OP(VE, "tensor_tensor", statacc[0:1, 0:512], pG[0:1, :], statacc[0:1, 0:512],
               A.add)
            OP(VE, "tensor_tensor", statacc[0:1, 512:1024], pH[0:1, :],
               statacc[0:1, 512:1024], A.add, sem="ve")
            ve_st = fl.cnt["ve"]
            W(PE, "ve", ve_st)
            W(PE, "ve", ve_q2[b])
            for rnd in range(8):
                slots = [(2 * rnd + a, cc2) for a in range(2) for cc2 in range(2)]
                pe_prd = None
                for sl, (dtl, c) in enumerate(slots):
                    bank = (pE, pF, pG, pH)[sl]
                    for k in range(4):
                        pe_prd = MM(bank[:, :],
                                    wq2[:, 2048 * k + 128 * dtl:2048 * k + 128 * (dtl + 1)],
                                    hsb[:, 1024 * k + 512 * c:1024 * k + 512 * (c + 1)],
                                    start=(k == 0), stop=(k == 3),
                                    sem=("pe" if (sl == 3 and k == 3) else None))
                W(VE, "pe", pe_prd)
                dt0 = 2 * rnd
                OP(VE, "tensor_tensor", xT[:, dt0, :], pEF[:, :], xT[:, dt0, :],
                   A.add, sem="ve", drain=False)
                gq_acc[sbi] = OP(VE, "tensor_tensor", xT[:, dt0 + 1, :], pGH[:, :],
                                 xT[:, dt0 + 1, :], A.add, sem="ve")
                W(PE, "ve", gq_acc[sbi])
            pe_pj[sbi] = fl.cnt["pe"]

    W(VE, "pe", pe_pj[-1])
    W(VE, "ve", gq_acc[-1])
    if debug:
        W(SY, "pe", pe_pj[-1])
        DMA(SY, "db5", dbg["acc0"], xT[:, 0, :])
        d_st0 = DMA(SY, "db5", dbg["rstd3"], statacc[0:1, :])
        W(VE, "db5", d_st0)
    OP(VE, "tensor_scalar", statacc[0:1, :], statacc[0:1, :], 1.0 / HID, None, A.mult,
       sem="ve")
    OP(VE, "tensor_scalar", statacc[0:1, :], statacc[0:1, :], RMS_EPS, None, A.add,
       sem="ve")
    W(SC, "ve", fl.cnt["ve"])
    OP(SC, "activation", statacc[0:1, :], statacc[0:1, :], ACT.Ln)
    OP(SC, "activation", statacc[0:1, :], statacc[0:1, :], ACT.Exp, scale=-0.5, sem="ac")
    W(PE, "ac", fl.cnt["ac"])
    MM(pA[:, :], onesf[:, :], statacc[0:1, 0:512], start=True, stop=True)
    pe_r3 = MM(pB[:, :], onesf[:, :], statacc[0:1, 512:1024], start=True, stop=True,
               sem="pe")
    c_x1 = [None] * DT
    for t in range(2):
        W(SY, "ve", fl.cnt["ve"])
        c_x1[t] = DMA(SY, "lr0" if t % 2 == 0 else "lr1", x1rel[t][:, :],
                      x1_sp[128 * t:128 * (t + 1), :])
    sq_out = None
    last_sq = {}
    for t in range(DT):
        W(VE, "lr0" if t % 2 == 0 else "lr1", c_x1[t])
        W(VE, "pe", pe_r3)
        for c in range(2):
            W(VE, "sq0" if c == 0 else "sq1", last_sq.get(c))
            OP(VE, "tensor_tensor", zt[:, :], xT[:, t, 512 * c:512 * (c + 1)],
               (pA if c == 0 else pB)[:, :], A.mult)
            OP(VE, "scalar_tensor_tensor", rr[:, :] if c == 0 else rr2[:, :], zt[:, :],
               gates[:, 80 + t:80 + t + 1], x1rel[t % 2][:, 512 * c:512 * (c + 1)],
               A.mult, A.add, sem="ve")
            W(SC, "ve", fl.cnt["ve"])
            sq_out = DMA(SC, "sq0" if c == 0 else "sq1",
                         out_d[128 * t:128 * (t + 1), 512 * c:512 * (c + 1)],
                         rr[:, :] if c == 0 else rr2[:, :])
            last_sq[c] = sq_out
        if t + 2 < DT:
            W(SY, "ve", fl.cnt["ve"])
            c_x1[t + 2] = DMA(SY, "lr0" if t % 2 == 0 else "lr1", x1rel[t % 2][:, :],
                              x1_sp[128 * (t + 2):128 * (t + 3), :])
    W(SC, "sq0", last_sq[0])
    W(SC, "sq1", last_sq[1])
    if debug:
        for dsn in ("db1", "db2", "db3", "db4", "db5"):
            W(SC, dsn, fl.cnt[dsn])

    sctx = contextlib.ExitStack()
    if True:
        for s in fl.cnt:
            fl.sems[s] = sctx.enter_context(nc.semaphore(s))
        with nc.Block() as block:
            @block.sync
            def _(e):
                for fn in fl.ops["sync"]:
                    fn(e)

            @block.tensor
            def _(e):
                for fn in fl.ops["tensor"]:
                    fn(e)

            @block.vector
            def _(e):
                for fn in fl.ops["vector"]:
                    fn(e)

            @block.scalar
            def _(e):
                for fn in fl.ops["scalar"]:
                    fn(e)

            @block.gpsimd
            def _(e):
                for fn in fl.ops["gpsimd"]:
                    fn(e)
    nc._keepalive = (ctx, sctx)
    return nc


_NC_CACHE = None


def kernel(L0_attn_Q_input, L0_attn_Wq, L0_attn_Wk, L0_attn_Wv, L0_attn_Wo,
           L0_ffn_W_fc, L0_ffn_W_proj, L0_c, L0_wmod):
    global _NC_CACHE
    f32 = np.float32
    x = np.ascontiguousarray(np.asarray(L0_attn_Q_input, f32).reshape(SEQ, D).T)
    wq = np.ascontiguousarray(
        np.asarray(L0_attn_Wq, f32).reshape(DT, 128, NH, 128).transpose(2, 1, 0, 3)
    ).reshape(NH, 128, DT * 128)
    wk = np.ascontiguousarray(
        np.asarray(L0_attn_Wk, f32).reshape(DT, 128, 128).transpose(1, 0, 2)
    ).reshape(128, DT * 128)
    wv = np.ascontiguousarray(
        np.asarray(L0_attn_Wv, f32).reshape(DT, 128, 128).transpose(1, 0, 2)
    ).reshape(128, DT * 128)
    wo = np.ascontiguousarray(np.asarray(L0_attn_Wo, f32))
    wfcb = np.ascontiguousarray(
        np.asarray(L0_ffn_W_fc, f32).reshape(NB, BH, DT, 128).transpose(0, 3, 2, 1)
    ).reshape(NB, 128, DT * BH)
    wpjb = np.ascontiguousarray(
        np.asarray(L0_ffn_W_proj, f32).T.reshape(NB, 2, 128, D).transpose(0, 2, 1, 3)
    ).reshape(NB, 128, 2 * D)
    cvec = np.ascontiguousarray(np.asarray(L0_c, f32).reshape(16, 128))
    wmod = np.asarray(L0_wmod, f32)
    identm = np.eye(128, dtype=ml_dtypes.bfloat16)

    if _NC_CACHE is None:
        _NC_CACHE = build_program()
    nc = _NC_CACHE

    in_maps = []
    for i in range(NCORES):
        in_maps.append({
            "xin": np.ascontiguousarray(x[:, S * i:S * (i + 1)]),
            "wqb": wq, "wkb": wk, "wvb": wv, "wo": wo,
            "wfcb": wfcb, "wpjb": wpjb, "cvec": cvec,
            "wmodc": np.ascontiguousarray(wmod[:, MODW * i:MODW * (i + 1)]),
            "ident": identm,
        })
    res = run_bass_kernel_spmd(nc, in_maps, core_ids=list(range(NCORES)))
    out = np.concatenate([np.asarray(r["out"]).T for r in res.results], axis=0)
    return np.ascontiguousarray(out.reshape(1, SEQ, D), dtype=np.float32)


# version 07 (final)
# speedup vs baseline: 1.1134x; 1.0035x over previous
import numpy as np
import ml_dtypes
import contextlib

import concourse.bass as bass
import concourse.mybir as mybir
from concourse.alu_op_type import AluOpType as A
from concourse.bass_utils import run_bass_kernel_spmd

F32 = mybir.dt.float32
BF16 = mybir.dt.bfloat16
I32 = mybir.dt.int32
I16 = mybir.dt.int16
ACT = mybir.ActivationFunctionType

NCORES = 8
D = 2048
DT = 16
S = 1024
NH = 16
TT = 64
SEQ = 8192
HID = 8192
NB = 32
BH = 256
MODW = 12288 // NCORES
SCALE = float(1.0 / np.sqrt(128.0))
LN_EPS = 1e-5
RMS_EPS = float(np.finfo(np.float32).eps)
THR = 0.501953125


class Flow:
    ENGINES = ("sync", "tensor", "vector", "scalar", "gpsimd")

    def __init__(self):
        self.ops = {e: [] for e in self.ENGINES}
        self.cnt = {}
        self.sems = {}

    def sem(self, name):
        self.cnt[name] = 0

    def emit(self, engine, fn):
        self.ops[engine].append(fn)

    def wait(self, engine, sem, val):
        if val is None or val <= 0:
            return
        assert val <= self.cnt[sem], f"wait {sem}>={val} but only {self.cnt[sem]} scheduled"
        self.emit(engine, lambda e, s=sem, v=val: e.wait_ge(self.sems[s], v))

    def dma(self, engine, sem, out, in_, **kw):
        self.cnt[sem] += 16
        def fn(e, out=out, in_=in_, s=sem, kw=kw):
            e.dma_start(out=out, in_=in_, **kw).then_inc(self.sems[s], 16)
        self.emit(engine, fn)
        return self.cnt[sem]

    def op(self, engine, method, *args, sem=None, drain=None, **kw):
        c = None
        if sem is not None:
            self.cnt[sem] += 1
            c = self.cnt[sem]
        if drain is None:
            drain = engine in ("vector", "scalar", "gpsimd")
        def fn(e, m=method, args=args, kw=kw, s=sem, d=drain):
            ins = getattr(e, m)(*args, **kw)
            if d:
                e.drain()
            if s is not None:
                ins.then_inc(self.sems[s], 1)
        self.emit(engine, fn)
        return c

    def mm(self, out, lhsT, rhs, start, stop, sem=None, tile_position=None):
        kw = {}
        if tile_position is not None:
            kw["tile_position"] = tile_position
        return self.op("tensor", "matmul", out, lhsT, rhs, start=start, stop=stop,
                       sem=sem, **kw)


def build_program(debug=False):
    nc = bass.Bass("TRN2", target_bir_lowering=False)
    fl = Flow()
    ctx = contextlib.ExitStack()
    sbuf = lambda n, s, d: ctx.enter_context(nc.sbuf_tensor(n, s, d))
    psum = lambda n: ctx.enter_context(nc.psum_tensor(n, [128, 512], F32))

    din = lambda n, s: nc.dram_tensor(n, s, F32, kind="ExternalInput").ap()
    xin = din("xin", [D, S])
    wqb_d = din("wqb", [NH, 128, DT * 128])
    wkb_d = din("wkb", [128, DT * 128])
    wvb_d = din("wvb", [128, DT * 128])
    wo_d = din("wo", [D, D])
    wfcb_d = din("wfcb", [NB, 128, DT * BH])
    wpjb_d = din("wpjb", [NB, 128, 2 * D])
    c_d = din("cvec", [16, 128])
    wmod_d = din("wmodc", [D, MODW])
    ident_d = nc.dram_tensor("ident", [128, 128], BF16, kind="ExternalInput").ap()
    out_d = nc.dram_tensor("out", [D, S], F32, kind="ExternalOutput").ap()

    if debug:
        dbg = {n: nc.dram_tensor("dbg_" + n, sh, dt, kind="ExternalOutput").ap()
               for n, sh, dt in [
                   ("gates", [128, 96], F32), ("h1", [128, 1024], BF16),
                   ("qt0", [128, 1024], BF16), ("kt", [128, 2048], BF16),
                   ("vaug", [128, 258], BF16), ("ao0", [128, 1024], BF16),
                   ("x1", [128, 1024], F32), ("h2", [128, 1024], BF16),
                   ("wq1b", [128, 4096], BF16), ("acc0", [128, 1024], F32),
                   ("rstd3", [1, 1024], F32), ("wq2b", [128, 4096], BF16),
               ]}
    mod_in = nc.dram_tensor("mod_in", [1, MODW], F32).ap()
    mod_out = nc.dram_tensor("mod_out", [NCORES, MODW], F32).ap()
    kt_in = nc.dram_tensor("kt_in", [128, S], BF16).ap()
    kt_out = nc.dram_tensor("kt_out", [NCORES * 128, S], BF16).ap()
    vn_in = nc.dram_tensor("vn_in", [S, 128], BF16).ap()
    vn_out = nc.dram_tensor("vn_out", [SEQ, 128], BF16).ap()
    x1_sp = nc.dram_tensor("x1_spill", [D, S], F32).ap()
    ssc_fc_d = nc.dram_tensor("ssc_fc", [128, 1024], BF16).ap()
    tlo_fc_d = nc.dram_tensor("tlo_fc", [128, 1024], BF16).ap()
    ssc_pj_d = nc.dram_tensor("ssc_pj", [128, 2048], BF16).ap()
    tlo_pj_d = nc.dram_tensor("tlo_pj", [128, 2048], BF16).ap()

    xT = sbuf("xT", [128, DT, S], F32)
    hh = sbuf("hh", [128, DT, S], BF16)
    AR = sbuf("AR", [128, 24576], F32)
    ident = sbuf("ident_s", [128, 128], BF16)
    identf = sbuf("identf", [128, 128], F32)
    ones1 = sbuf("ones1", [128, 1], BF16)
    onesf = sbuf("onesf", [1, 128], F32)
    G2 = sbuf("G2", [128, 2], BF16)
    maskI = sbuf("maskI", [128, 1], I32)
    maskA32 = sbuf("maskA32", [128, 1], I32)
    maskA16 = sbuf("maskA16", [128, 1], I16)
    maskPK = sbuf("maskPK", [128, 1], I32)
    modT = sbuf("modT", [128, 96], F32)
    modraw = sbuf("modraw", [96, 128], F32)
    gates = sbuf("gates", [128, 96], F32)
    silu_s = sbuf("silu_s", [16, 128], F32)
    siluT = sbuf("siluT", [128, 16], F32)
    siluh = sbuf("siluh", [128, 16], BF16)
    silul = sbuf("silul", [128, 16], BF16)
    silul_f = sbuf("silul_f", [128, 16], F32)
    statacc = sbuf("statacc", [1, S], F32)
    zt = sbuf("zt", [128, 512], F32)
    rr = sbuf("rr", [128, 512], F32)
    rr2 = sbuf("rr2", [128, 512], F32)
    hsqT = sbuf("hsqT", [128, 1024], BF16)
    ao_inv = sbuf("ao_inv", [128, 4], F32)

    def arv(off, nf32, dtype=F32):
        ap = AR[:, off:off + nf32]
        return ap.bitcast(dtype) if dtype != F32 else ap

    wms = [arv(0, 1536), arv(1536, 1536)]
    wmh = arv(3072, 768, BF16)
    wml = arv(3840, 768, BF16)
    xb_t = [arv(4608, 512, BF16), arv(5120, 512, BF16)]
    xsq_t = [arv(5632, 512, BF16), arv(6144, 512, BF16)]
    QT = arv(0, 8192, BF16)
    KT = arv(8192, 4096, BF16)
    Vaug = arv(12288, 4128, BF16)
    probs = arv(16416, 1024, BF16)
    wqh = [arv(17440, 1024, BF16), arv(18464, 1024, BF16)]
    kv_ev = arv(19488, 1024, BF16)
    vnat = arv(20512, 512, BF16)
    wo_b = [arv(0, 1024, BF16), arv(1024, 1024, BF16)]
    wbs1 = [arv(0, 2048, BF16), arv(2048, 2048, BF16)]
    wbs2 = [arv(4096, 2048, BF16), arv(6144, 2048, BF16)]
    wq1 = [arv(8192, 2048, BF16), arv(10240, 2048, BF16)]
    wq2 = arv(12288, 4096, BF16)
    hsb = arv(16384, 2048, BF16)
    scb = arv(18432, 1024, BF16)
    scb2 = arv(19456, 1024, BF16)
    sct = arv(20480, 1024)
    ab_t = arv(21504, 1024, BF16)
    bc_s = arv(22528, 1024, BF16)
    bc_t = arv(23552, 1024, BF16)
    x1rel = [arv(0, 1024), arv(1024, 1024)]
    p0tmpA = arv(6656, 1024)
    p0tmpB = arv(7680, 512)
    aox = arv(23072, 256, BF16)
    wkb = arv(21024, 1024, BF16)
    wvb = arv(22048, 1024, BF16)
    stat = arv(6656, 512)[0:1, :]
    stat2 = arv(7168, 512)[0:1, :]
    stat3 = arv(7680, 512)[0:1, :]
    zt0 = zt

    pAB = ctx.enter_context(nc.psum_tensor("pAB", [128, 1024], F32))
    pA, pB = pAB[:, 0:512], pAB[:, 512:1024]
    pCD = ctx.enter_context(nc.psum_tensor("pCD", [128, 1024], F32))
    pC, pD = pCD[:, 0:512], pCD[:, 512:1024]
    pEF = ctx.enter_context(nc.psum_tensor("pEF", [128, 1024], F32))
    pGH = ctx.enter_context(nc.psum_tensor("pGH", [128, 1024], F32))
    pE, pF = pEF[:, 0:512], pEF[:, 512:1024]
    pG, pH = pGH[:, 0:512], pGH[:, 512:1024]
    pAb = pAB[:, 0:64].bitcast(BF16)
    pBb = pAB[:, 512:576].bitcast(BF16)

    for s in ("ld", "gq", "sq", "pe", "ve", "ac", "cc", "ms", "lc", "lx",
              "lm0", "lm1", "lq0", "lq1", "lo0", "lo1", "lf0", "lf1",
              "lp0", "lp1", "lr0", "lr1", "dbgs", "db1", "db2", "db3", "db4", "db5", "lmr", "lkv", "lsc", "lsc2", "syb", "sq0", "sq1"):
        fl.sem(s)

    SY, PE, VE, SC, GP = "sync", "tensor", "vector", "scalar", "gpsimd"
    W, DMA, OP, MM = fl.wait, fl.dma, fl.op, fl.mm

    OP(VE, "memset", ones1[:, :], 1.0, sem="ms")
    OP(VE, "memset", onesf[:, :], 1.0, sem="ms")
    OP(VE, "memset", G2[0:64, 0:1], 1.0, sem="ms")
    OP(VE, "memset", G2[64:128, 0:1], 0.0, sem="ms")
    OP(VE, "memset", G2[0:64, 1:2], 0.0, sem="ms")
    OP(VE, "memset", G2[64:128, 1:2], 1.0, sem="ms")
    OP(VE, "memset", maskI[:, :], -65536, sem="ms")
    OP(VE, "memset", maskA32[:, :], 2147483647, sem="ms")
    OP(VE, "memset", maskA16[:, :], 32767, sem="ms")
    OP(VE, "memset", maskPK[:, :], 2147450879, sem="ms")
    OP(VE, "memset", Vaug[:, :], 1.0, sem="ms")
    OP(VE, "memset", statacc[:, :], 0.0, sem="ms")
    ms_const = fl.cnt["ms"]
    c_id = DMA(SY, "lc", ident[:, :], ident_d[:, :])
    c_c = DMA(SY, "lc", silu_s[:, :], c_d[:, :])
    W(VE, "lc", 32)
    OP(VE, "tensor_copy", identf[:, :], ident[:, :], sem="ve")

    W(PE, "ms", ms_const)
    W(VE, "ms", ms_const)
    W(SC, "ms", ms_const)
    W(GP, "ms", ms_const)
    W(PE, "lc", 32)
    W(PE, "ve", fl.cnt["ve"])
    OP(PE, "transpose", pA[:, 0:16], silu_s[:, :], identf[0:16, 0:16], sem="pe")
    W(VE, "pe", fl.cnt["pe"])
    OP(VE, "tensor_copy", siluT[:, :], pA[:, 0:16], sem="ve")
    W(SC, "ve", fl.cnt["ve"])
    OP(SC, "activation", siluT[:, :], siluT[:, :], ACT.Silu, sem="ac")
    W(VE, "ac", fl.cnt["ac"])
    OP(VE, "tensor_copy", siluh[:, :], siluT[:, :])
    OP(VE, "tensor_copy", silul_f[:, :], siluh[:, :])
    OP(VE, "tensor_tensor", silul_f[:, :], siluT[:, :], silul_f[:, :], A.subtract)
    OP(VE, "tensor_copy", silul[:, :], silul_f[:, :], sem="ve")
    ve_silu = fl.cnt["ve"]

    pe_modslab = [None] * DT
    for t in range(DT):
        if t >= 2:
            W(SY, "pe", pe_modslab[t - 2])
        c_wm = DMA(SY, "lm0" if t % 2 == 0 else "lm1", wms[t % 2][:, :],
                   wmod_d[128 * t:128 * (t + 1), :])
        W(VE, "lm0" if t % 2 == 0 else "lm1", c_wm)
        if t >= 1:
            W(VE, "pe", pe_modslab[t - 1])
        OP(VE, "tensor_copy", wmh[:, :], wms[t % 2][:, :])
        OP(VE, "tensor_copy", p0tmpA[:, :], wmh[:, 0:1024])
        OP(VE, "tensor_copy", p0tmpB[:, :], wmh[:, 1024:1536])
        OP(VE, "tensor_tensor", p0tmpA[:, :], wms[t % 2][:, 0:1024], p0tmpA[:, :],
           A.subtract)
        OP(VE, "tensor_tensor", p0tmpB[:, :], wms[t % 2][:, 1024:1536], p0tmpB[:, :],
           A.subtract)
        OP(VE, "tensor_copy", wml[:, 0:1024], p0tmpA[:, :])
        OP(VE, "tensor_copy", wml[:, 1024:1536], p0tmpB[:, :], sem="ve")
        W(PE, "ve", fl.cnt["ve"])
        if t == 0:
            W(PE, "ve", ve_silu)
        for ci, bank in ((0, pB), (1, pC), (2, pD)):
            n0 = 512 * ci
            MM(bank[0:1, :], siluh[:, t:t + 1], wmh[:, n0:n0 + 512],
               start=(t == 0), stop=False)
            MM(bank[0:1, :], siluh[:, t:t + 1], wml[:, n0:n0 + 512],
               start=False, stop=False)
            pe_modslab[t] = MM(bank[0:1, :], silul[:, t:t + 1], wmh[:, n0:n0 + 512],
                               start=False, stop=(t == DT - 1), sem="pe")
    pe_mod = fl.cnt["pe"]
    W(VE, "pe", pe_mod)
    OP(VE, "tensor_copy", stat[0:1, :], pB[0:1, :])
    OP(VE, "tensor_copy", stat2[0:1, :], pC[0:1, :])
    OP(VE, "tensor_copy", stat3[0:1, :], pD[0:1, :], sem="ve")
    ve_modev = fl.cnt["ve"]
    gq_modw = None
    W(GP, "ve", ve_modev)
    for ci, src in ((0, stat), (1, stat2), (2, stat3)):
        gq_modw = DMA(GP, "gq", mod_in[0:1, 512 * ci:512 * (ci + 1)], src[0:1, :])
    W(GP, "gq", gq_modw)
    cc_mod = OP(GP, "collective_compute", "AllGather", mybir.AluOpType.bypass,
                replica_groups=[list(range(NCORES))],
                ins=[mod_in], outs=[mod_out], sem="cc")
    W(GP, "cc", cc_mod)
    gq_modr = DMA(GP, "lmr", modraw[:, :],
                  mod_out.rearrange("a b -> (a b)").rearrange("(r c) -> r c", c=128))
    W(PE, "lmr", gq_modr)
    OP(PE, "transpose", pA[:, 0:96], modraw[:, :], identf[0:96, 0:96], sem="pe")
    pe_modT = fl.cnt["pe"]
    W(VE, "pe", pe_modT)
    OP(VE, "tensor_copy", modT[:, :], pA[:, 0:96])
    OP(VE, "tensor_scalar", gates[:, 0:16], modT[:, 16:32], 1.0, None, A.add)
    OP(VE, "tensor_copy", gates[:, 16:32], modT[:, 0:16])
    OP(VE, "tensor_scalar", gates[:, 48:64], modT[:, 64:80], 1.0, None, A.add)
    OP(VE, "tensor_copy", gates[:, 64:80], modT[:, 48:64])
    for (gsrc, gdst) in ((modT[:, 32:48], gates[:, 32:48]),
                         (modT[:, 80:96], gates[:, 80:96])):
        OP(VE, "tensor_scalar", zt0[:, 0:16], gsrc, 0.0, 2.0, A.is_ge, A.mult)
        OP(VE, "tensor_scalar", zt0[:, 0:16], zt0[:, 0:16], 1.0, None, A.subtract)
        OP(VE, "tensor_scalar", zt0[:, 16:32].bitcast(I32), gsrc.bitcast(I32),
           maskA32[:, :], None, A.bitwise_and)
        OP(VE, "tensor_scalar", zt0[:, 16:32], zt0[:, 16:32], 0.25, None, A.max)
        OP(VE, "tensor_tensor", gdst, zt0[:, 0:16], zt0[:, 16:32], A.mult)
    OP(VE, "tensor_copy", zt0[:, 0:1], zt0[:, 0:1], sem="ve")
    ve_gates = fl.cnt["ve"]

    c_x = [DMA(SY, "lx", xT[:, t, :], xin[128 * t:128 * (t + 1), :]) for t in range(DT)]
    pe_stat = [None] * DT
    for t in range(DT):
        W(VE, "lx", c_x[DT - 1])
        if t >= 2:
            W(VE, "pe", pe_stat[t - 2])
        OP(VE, "tensor_copy", xb_t[t % 2][:, :], xT[:, t, :], sem="ve")
        W(SC, "ve", fl.cnt["ve"])
        OP(SC, "activation", xsq_t[t % 2][:, :], xb_t[t % 2][:, :], ACT.Square, sem="ac")
        W(PE, "ac", fl.cnt["ac"])
        MM(pE[0:1, :], ones1[:, :], xb_t[t % 2][:, 0:512], start=(t == 0), stop=(t == DT - 1))
        MM(pF[0:1, :], ones1[:, :], xb_t[t % 2][:, 512:1024], start=(t == 0), stop=(t == DT - 1))
        MM(pG[0:1, :], ones1[:, :], xsq_t[t % 2][:, 0:512], start=(t == 0), stop=(t == DT - 1))
        pe_stat[t] = MM(pH[0:1, :], ones1[:, :], xsq_t[t % 2][:, 512:1024],
                        start=(t == 0), stop=(t == DT - 1), sem="pe")

    def ln_apply(gcol0, pre_pe_waits):
        for c in range(2):
            bmu = pE if c == 0 else pF
            bms = pG if c == 0 else pH
            W(VE, "pe", fl.cnt["pe"])
            OP(VE, "tensor_scalar", stat[0:1, :], bmu[0:1, :], 1.0 / D, None, A.mult)
            OP(VE, "tensor_scalar", stat2[0:1, :], bms[0:1, :], 1.0 / D, None, A.mult)
            OP(VE, "tensor_tensor", stat3[0:1, :], stat[0:1, :], stat[0:1, :], A.mult)
            OP(VE, "tensor_tensor", stat2[0:1, :], stat2[0:1, :], stat3[0:1, :],
               A.subtract, sem="ve")
            OP(VE, "tensor_scalar", stat2[0:1, :], stat2[0:1, :], LN_EPS, None, A.add,
               sem="ve")
            W(SC, "ve", fl.cnt["ve"])
            OP(SC, "activation", stat2[0:1, :], stat2[0:1, :], ACT.Ln)
            OP(SC, "activation", stat2[0:1, :], stat2[0:1, :], ACT.Exp, scale=-0.5,
               sem="ac")
            for sname, v in pre_pe_waits:
                W(PE, sname, v)
            W(PE, "ac", fl.cnt["ac"])
            MM(pA[:, :], onesf[:, :], stat[0:1, :], start=True, stop=True)
            MM(pB[:, :], onesf[:, :], stat2[0:1, :], start=True, stop=True, sem="pe")
            pe_bc = fl.cnt["pe"]
            for t in range(DT):
                W(VE, "pe", pe_bc)
                OP(VE, "tensor_tensor", zt0[:, :], xT[:, t, 512 * c:512 * (c + 1)],
                   pA[:, :], A.subtract)
                OP(VE, "tensor_tensor", zt0[:, :], zt0[:, :], pB[:, :], A.mult)
                OP(VE, "scalar_tensor_tensor", hh[:, t, 512 * c:512 * (c + 1)],
                   zt0[:, :], gates[:, gcol0 + t:gcol0 + t + 1],
                   gates[:, gcol0 + 16 + t:gcol0 + 16 + t + 1].broadcast_to([128, 512]),
                   A.mult, A.add, sem="ve")
            W(PE, "ve", fl.cnt["ve"])

    W(VE, "ve", ve_gates)
    ln_apply(0, [("gq", gq_modw), ("ve", ve_gates)])
    if debug:
        W(SY, "ve", ve_gates)
        DMA(SY, "db1", dbg["gates"], gates[:, :])
    ve_h1 = fl.cnt["ve"]
    d_h1 = 0
    if debug:
        W(SY, "ve", ve_h1)
        d_h1 = DMA(SY, "db1", dbg["h1"], hh[:, 0, :])

    c_wk = DMA(GP, "lkv", wkb[:, :], wkb_d[:, :])
    c_wv = DMA(GP, "lkv", wvb[:, :], wvb_d[:, :])
    c_wq = [None] * NH
    for h in range(2):
        c_wq[h] = DMA(GP, "lq0" if h % 2 == 0 else "lq1", wqh[h][:, :], wqb_d[h, :, :])
    ac_qev = [None] * NH
    W(SC, "ve", fl.cnt["ve"])
    for h in range(NH):
        W(PE, "lq0" if h % 2 == 0 else "lq1", c_wq[h])
        if h == 0:
            W(PE, "ve", ve_h1)
        pe_q = None
        for c, bank in ((0, pE), (1, pF)):
            for t in range(DT):
                pe_q = MM(bank[:, :], wqh[h % 2][:, 128 * t:128 * (t + 1)],
                          hh[:, t, 512 * c:512 * (c + 1)], start=(t == 0),
                          stop=(t == DT - 1),
                          sem=("pe" if (c == 1 and t == DT - 1) else None))
        if h + 2 < NH:
            W(GP, "pe", pe_q)
            c_wq[h + 2] = DMA(GP, "lq0" if h % 2 == 0 else "lq1", wqh[h % 2][:, :],
                              wqb_d[h + 2, :, :])
        W(SC, "pe", pe_q)
        OP(SC, "activation", QT[:, 1024 * h:1024 * h + 512], pE[:, :], ACT.Copy,
           drain=False)
        ac_qev[h] = OP(SC, "activation", QT[:, 1024 * h + 512:1024 * h + 1024], pF[:, :],
                       ACT.Copy, sem="ac")
        if h + 1 < NH:
            W(PE, "ac", ac_qev[h])
    W(PE, "lkv", 32)
    pe_k = None
    for c, bank in ((0, pG), (1, pH)):
        for t in range(DT):
            pe_k = MM(bank[:, :], wkb[:, 128 * t:128 * (t + 1)],
                      hh[:, t, 512 * c:512 * (c + 1)], start=(t == 0), stop=(t == DT - 1),
                      sem=("pe" if (c == 1 and t == DT - 1) else None))
    W(SC, "pe", pe_k)
    OP(SC, "activation", kv_ev[:, 0:512], pG[:, :], ACT.Copy)
    ac_kev = OP(SC, "activation", kv_ev[:, 512:1024], pH[:, :], ACT.Copy, sem="ac")
    W(PE, "ac", ac_kev)
    pe_v = None
    for c, bank in ((0, pG), (1, pH)):
        for t in range(DT):
            pe_v = MM(bank[:, :], wvb[:, 128 * t:128 * (t + 1)],
                      hh[:, t, 512 * c:512 * (c + 1)], start=(t == 0), stop=(t == DT - 1),
                      sem=("pe" if (c == 1 and t == DT - 1) else None))
    W(SC, "pe", pe_v)
    OP(SC, "activation", kv_ev[:, 1024:1536], pG[:, :], ACT.Copy)
    ac_vev = OP(SC, "activation", kv_ev[:, 1536:2048], pH[:, :], ACT.Copy, sem="ac")
    W(GP, "ac", ac_kev)
    gq_k = DMA(GP, "gq", kt_in[:, :], kv_ev[:, 0:1024])
    W(PE, "ac", ac_vev)
    ve_vn = None
    for i in range(8):
        tb = pAb if i % 2 == 0 else pBb
        pe_tv = OP(PE, "transpose", tb,
                   kv_ev[:, 1024 + 128 * i:1024 + 128 * (i + 1)], ident[:, :], sem="pe")
        W(VE, "pe", pe_tv)
        ve_vn = OP(VE, "tensor_copy", vnat[:, 128 * i:128 * (i + 1)], tb,
                   sem="ve")
        W(PE, "ve", ve_vn)
    W(GP, "ve", ve_vn)
    gq_v = DMA(GP, "gq", vn_in.rearrange("(i p) j -> p i j", p=128),
               vnat[:, :].rearrange("p (i j) -> p i j", j=128))
    W(GP, "gq", gq_v)
    cc_k = OP(GP, "collective_compute", "AllGather", mybir.AluOpType.bypass,
              replica_groups=[list(range(NCORES))], ins=[kt_in], outs=[kt_out], sem="cc")
    cc_v = OP(GP, "collective_compute", "AllGather", mybir.AluOpType.bypass,
              replica_groups=[list(range(NCORES))], ins=[vn_in], outs=[vn_out], sem="cc")
    W(GP, "cc", cc_v)
    gq_kt = None
    for j in range(NCORES):
        gq_kt = DMA(GP, "gq", KT[:, 1024 * j:1024 * (j + 1)], kt_out[128 * j:128 * (j + 1), :])
    for tt in range(TT):
        gq_kt = DMA(GP, "gq", Vaug[:, 129 * tt:129 * tt + 128],
                    vn_out[128 * tt:128 * (tt + 1), :])
    gq_kv_done = gq_kt
    if debug:
        W(SY, "gq", gq_kv_done)
        W(SY, "ac", ac_qev[0])
        DMA(SY, "db2", dbg["qt0"], QT[:, 0:1024])
        DMA(SY, "db2", dbg["kt"], KT[:, 0:2048])
        d_kv = DMA(SY, "db2", dbg["vaug"], Vaug[:, 0:258])

    W(PE, "gq", gq_kv_done)
    W(PE, "ms", ms_const)
    if debug:
        W(VE, "db1", d_h1)
    avb = (pE, pF, pG, pH)
    last_av_slot = {}
    last_exp_bank = {}
    for h in range(NH):
        for c in range(2):
            qs = QT[:, 1024 * h + 512 * c:1024 * h + 512 * (c + 1)]
            av_pe = {}
            exp_pair = {}
            for pk in range(TT // 2):
                t0 = 2 * pk
                sbb = pCD if pk % 2 == 0 else pAB
                if pk >= 2:
                    W(PE, "ac", exp_pair[pk - 2])
                else:
                    W(PE, "ac", last_exp_bank.get(pk % 2))
                MM(sbb[:, 0:512], KT[:, 128 * t0:128 * (t0 + 1)], qs, start=True,
                   stop=True)
                MM(sbb[:, 512:1024], KT[:, 128 * (t0 + 1):128 * (t0 + 2)], qs,
                   start=True, stop=True, sem="pe")
                pe_sc = fl.cnt["pe"]
                s0 = t0 % 4
                W(SC, "pe", pe_sc)
                W(SC, "pe", last_av_slot.get(s0))
                OP(SC, "activation", probs[:, 512 * s0:512 * (s0 + 2)], sbb[:, :],
                   ACT.Exp, scale=SCALE, sem="ac", drain=False)
                exp_pair[pk] = fl.cnt["ac"]
                last_exp_bank[pk % 2] = exp_pair[pk]
                W(PE, "ac", exp_pair[pk])
                for tt in (t0, t0 + 1):
                    slot = tt % 4
                    for sl in range(4):
                        av_pe[tt] = MM(
                            avb[sl][:, 0:129],
                            probs[:, 512 * slot + 128 * sl:512 * slot + 128 * (sl + 1)],
                            Vaug[:, 129 * tt:129 * (tt + 1)],
                            start=(tt == 0), stop=(tt == TT - 1),
                            sem=("pe" if sl == 3 else None))
                last_av_slot[s0] = av_pe[t0 + 1]
            W(VE, "pe", av_pe[TT - 1])
            for sl in range(4):
                OP(VE, "reciprocal", ao_inv[:, sl:sl + 1], avb[sl][:, 128:129],
                   sem=("ve" if sl == 3 else None))
            W(SC, "ve", fl.cnt["ve"])
            ac_ao = None
            for sl in range(4):
                ac_ao = OP(SC, "activation", aox[:, 128 * sl:128 * (sl + 1)],
                           avb[sl][:, 0:128], ACT.Copy, scale=ao_inv[:, sl:sl + 1],
                           sem=("ac" if sl == 3 else None), drain=(sl == 3))
            W(PE, "ac", ac_ao)
            ve_ao = None
            ve_ao_sl = {}
            for sl in range(4):
                tb = pAb if sl % 2 == 0 else pBb
                if sl >= 2:
                    W(PE, "ve", ve_ao_sl[sl - 2])
                pe_tr = OP(PE, "transpose", tb, aox[:, 128 * sl:128 * (sl + 1)],
                           ident[:, :], sem="pe")
                W(VE, "pe", pe_tr)
                ve_ao = OP(VE, "tensor_copy",
                           hh[:, h, 512 * c + 128 * sl:512 * c + 128 * (sl + 1)],
                           tb, sem="ve")
                ve_ao_sl[sl] = ve_ao
            W(PE, "ve", ve_ao)

    W(PE, "ac", fl.cnt["ac"])
    W(PE, "ve", fl.cnt["ve"])
    d_ao0 = 0
    if debug:
        W(SY, "ve", fl.cnt["ve"])
        d_ao0 = DMA(SY, "db3", dbg["ao0"], hh[:, 0, :])
    banks8 = (pA, pB, pC, pD, pE, pF, pG, pH)
    order = [(c, half) for c in range(2) for half in range(2)]
    slab_ids = [(r, q) for r in range(4) for q in range(DT)]
    c_wo = {}
    W(GP, "pe", fl.cnt["pe"])
    W(GP, "ac", fl.cnt["ac"])
    W(GP, "ve", fl.cnt["ve"])
    if debug:
        W(GP, "db2", d_kv)
    for i in range(2):
        r, q = slab_ids[i]
        c_wo[(r, q)] = DMA(GP, "lo0" if i % 2 == 0 else "lo1", wo_b[i % 2][:, :],
                           wo_d[128 * q:128 * (q + 1), :])
    idx = 0
    ve_res = None
    for r, (c, half) in enumerate(order):
        for q in range(DT):
            W(PE, "lo0" if idx % 2 == 0 else "lo1", c_wo[(r, q)])
            pe_woq = None
            for k in range(8):
                pe_woq = MM(banks8[k][:, :],
                            wo_b[idx % 2][:, 1024 * half + 128 * k:1024 * half + 128 * (k + 1)],
                            hh[:, q, 512 * c:512 * (c + 1)], start=(q == 0),
                            stop=(q == DT - 1), sem=("pe" if k == 7 else None))
            if idx + 2 < len(slab_ids):
                r2, q2 = slab_ids[idx + 2]
                W(GP, "pe", pe_woq)
                c_wo[(r2, q2)] = DMA(GP, "lo0" if idx % 2 == 0 else "lo1",
                                     wo_b[idx % 2][:, :],
                                     wo_d[128 * q2:128 * (q2 + 1), :])
            idx += 1
        pe_wor = fl.cnt["pe"]
        ve_k2 = {}
        for k in range(8):
            dt_out = 8 * half + k
            rtmp = rr if k % 2 == 0 else rr2
            W(SC, "pe", pe_wor)
            W(SC, "ve", ve_k2.get(k - 2))
            ac_r = OP(SC, "activation", rtmp[:, :], banks8[k][:, :], ACT.Copy,
                      scale=gates[:, 32 + dt_out:32 + dt_out + 1], sem="ac")
            W(VE, "ac", ac_r)
            ve_res = OP(VE, "tensor_tensor", xT[:, dt_out, 512 * c:512 * (c + 1)],
                        xT[:, dt_out, 512 * c:512 * (c + 1)], rtmp[:, :], A.add, sem="ve")
            ve_k2[k] = ve_res
        W(PE, "ve", ve_res)

    d_x1m = 0
    if debug:
        W(SY, "ve", ve_res)
        d_x1m = DMA(SY, "db3", dbg["x1"], xT[:, 0, :])
    xb2 = wbs1
    xq2 = wbs2
    if debug:
        W(VE, "db3", max(d_ao0, d_x1m))
    pe_stat2 = [None] * DT
    for t in range(DT):
        W(VE, "ve", ve_res)
        if t >= 2:
            W(VE, "pe", pe_stat2[t - 2])
        OP(VE, "tensor_copy", xb2[t % 2][:, 0:1024], xT[:, t, :], sem="ve")
        W(SC, "ve", fl.cnt["ve"])
        OP(SC, "activation", xq2[t % 2][:, 0:1024], xb2[t % 2][:, 0:1024], ACT.Square,
           sem="ac")
        W(PE, "ac", fl.cnt["ac"])
        MM(pE[0:1, :], ones1[:, :], xb2[t % 2][:, 0:512], start=(t == 0), stop=(t == DT - 1))
        MM(pF[0:1, :], ones1[:, :], xb2[t % 2][:, 512:1024], start=(t == 0), stop=(t == DT - 1))
        MM(pG[0:1, :], ones1[:, :], xq2[t % 2][:, 0:512], start=(t == 0), stop=(t == DT - 1))
        pe_stat2[t] = MM(pH[0:1, :], ones1[:, :], xq2[t % 2][:, 512:1024],
                         start=(t == 0), stop=(t == DT - 1), sem="pe")
    ln_apply(48, [("ve", fl.cnt["ve"])])
    ve_h2 = fl.cnt["ve"]
    if debug:
        W(SY, "ve", ve_h2)
        DMA(SY, "db4", dbg["h2"], hh[:, 0, :])
    gq_spill = None
    W(GP, "ve", ve_h2)
    for t in range(DT):
        gq_spill = DMA(GP, "gq", x1_sp[128 * t:128 * (t + 1), :], xT[:, t, :])
    W(VE, "gq", gq_spill)
    OP(VE, "tensor_scalar", xT[:, :, :], xT[:, :, :], 0.0, None, A.mult, sem="ve")
    ve_zeroacc = fl.cnt["ve"]

    def quant(kind, wbstage, wqdst):
        sscd, tlod = (ssc_fc_d, tlo_fc_d) if kind == "fc" else (ssc_pj_d, tlo_pj_d)
        wide = 1024 if kind == "fc" else 2048
        OP(VE, "tensor_scalar", ab_t[:, :].bitcast(I32), wbstage[:, 0:2048].bitcast(I32),
           maskPK[:, :], None, A.bitwise_and)
        OP(VE, "tensor_scalar", scb[:, :].bitcast(I32), wbstage[:, 2048:4096].bitcast(I32),
           maskPK[:, :], None, A.bitwise_and, sem="ve")
        W(PE, "ve", fl.cnt["ve"])
        sb4 = (pA, pB, pC, pD)
        pe_sc_mm = None
        if kind == "fc":
            for t in range(DT):
                src = ab_t if t < 8 else scb
                rh = src[:, 256 * (t % 8):256 * (t % 8 + 1)]
                j = t % 4
                pe_sc_mm = MM(sb4[t // 4][32 * j:32 * j + 2, 0:256], G2[:, :], rh,
                              start=True, stop=True, tile_position=(0, 32 * j),
                              sem=("pe" if t == DT - 1 else None))
        else:
            for ht in range(2):
                src = ab_t if ht == 0 else scb
                for chn in range(4):
                    j = chn // 2
                    pe_sc_mm = MM(sb4[2 * ht + chn % 2][32 * j:32 * j + 2, 0:512],
                                  G2[:, :], src[:, 512 * chn:512 * (chn + 1)],
                                  start=True, stop=True, tile_position=(0, 32 * j),
                                  sem=("pe" if (ht == 1 and chn == 3) else None))
        W(VE, "pe", pe_sc_mm)
        seg = wide // 4
        for i in range(4):
            OP(VE, "tensor_scalar", scb2[:, seg * i:seg * (i + 1)], sb4[i][:, 0:seg],
               1.0 / 64.0, 1e-8, A.mult, A.max)
        for hw in range(wide // 1024):
            o = 1024 * hw
            OP(VE, "tensor_scalar", sct[:, :], scb2[:, o:o + 1024], THR, None, A.mult)
            OP(VE, "tensor_scalar", sct[:, :].bitcast(I32), sct[:, :].bitcast(I32),
               maskI[:, :], None, A.bitwise_and)
            OP(VE, "tensor_copy", scb[:, o:o + 1024], sct[:, :])
        OP(VE, "tensor_copy", zt0[:, 0:1], zt0[:, 0:1], sem="ve")
        ve_rows = fl.cnt["ve"]
        W(GP, "ve", ve_rows)
        W(SY, "ve", ve_rows)
        DMA(GP, "lsc", sscd[:, 0:wide], scb2[:, 0:wide])
        c_row2 = DMA(SY, "lsc2", tlod[:, 0:wide], scb[:, 0:wide])
        W(GP, "lsc", fl.cnt["lsc"])
        W(SY, "lsc2", c_row2)
        ve_q = None
        for half in range(2):
            if half == 1:
                W(GP, "ve", fl.cnt["ve"])
                W(SY, "ve", fl.cnt["ve"])
            gq_b = None
            sy_b = None
            for (dr_, dst) in ((sscd, bc_s), (tlod, bc_t)):
                for u in range(2):
                    if kind == "fc":
                        for bl in range(2):
                            bg = 2 * half + bl
                            src = dr_[:, 0:1024].rearrange(
                                "(j z) (b n) -> j z b n", j=4, b=4)
                            src = src[:, u:u + 1, bg:bg + 1].rearrange(
                                "j z b n -> (z b) j n")
                            src = src.broadcast_to([64, 4, 256])
                            dstv = dst[64 * u:64 * (u + 1),
                                       1024 * bl:1024 * (bl + 1)].rearrange(
                                "q (j n) -> q j n", j=4)
                            if dst is bc_s:
                                gq_b = DMA(GP, "gq", dstv, src)
                            else:
                                sy_b = DMA(SY, "syb", dstv, src)
                    else:
                        for c2 in range(2):
                            src = dr_[:, 0:2048].rearrange(
                                "(c2 z) (c1h n) -> c2 z c1h n", c2=4, c1h=4)
                            src = src[c2:c2 + 1, u:u + 1,
                                      2 * half:2 * half + 2].rearrange(
                                "c2 z c1 n -> (c2 z) c1 n")
                            src = src.broadcast_to([64, 2, 512])
                            dstv = dst[64 * u:64 * (u + 1),
                                       1024 * c2:1024 * (c2 + 1)].rearrange(
                                "q (c1 n) -> q c1 n", c1=2)
                            if dst is bc_s:
                                gq_b = DMA(GP, "gq", dstv, src)
                            else:
                                sy_b = DMA(SY, "syb", dstv, src)
            W(VE, "gq", gq_b)
            W(VE, "syb", sy_b)
            wb_h = wbstage[:, 2048 * half:2048 * (half + 1)]
            gt = ab_t[:, 0:2048]
            L = sct[:, :].bitcast(BF16)
            nw = scb2[:, 0:2048]
            OP(VE, "tensor_scalar", nw, wb_h, -1.0, None, A.mult)
            OP(VE, "tensor_tensor", gt, wb_h, bc_t[:, :], A.is_gt)
            OP(VE, "tensor_tensor", L, nw, bc_t[:, :], A.is_gt)
            OP(VE, "tensor_tensor", gt, gt, L, A.subtract)
            ve_q = OP(VE, "tensor_tensor", wqdst[:, 2048 * half:2048 * (half + 1)],
                      gt, bc_s[:, :], A.mult, sem="ve")
        return ve_q

    W(GP, "ve", ve_zeroacc)
    W(GP, "pe", fl.cnt["pe"])
    W(GP, "ac", fl.cnt["ac"])
    c_fc = [None] * NB
    c_pj = [None] * NB
    for i in range(2):
        c_fc[i] = DMA(GP, "lf0" if i % 2 == 0 else "lf1", wbs1[i][:, :], wfcb_d[i, :, :])
        c_pj[i] = DMA(GP, "lp0" if i % 2 == 0 else "lp1", wbs2[i][:, :], wpjb_d[i, :, :])
    ve_q2 = [None] * NB
    pe_fc = [None] * NB
    pe_pj = [None] * (NB // 2)
    gq_acc = [None] * (NB // 2)
    ac_relu = {}
    last_hsq_mm = {}
    for b in range(NB):
        sbx = b % 2
        if b >= 2:
            W(VE, "pe", pe_pj[b // 2 - 1])
            W(VE, "pe", pe_fc[b - 2])
        if debug and b == 2:
            W(VE, "db4", d_wq)
        W(VE, "lf0" if b % 2 == 0 else "lf1", c_fc[b])
        ve_q1 = quant("fc", wbs1[sbx], wq1[sbx])
        W(VE, "lp0" if b % 2 == 0 else "lp1", c_pj[b])
        ve_q2[b] = quant("pj", wbs2[sbx], wq2[:, 4096 * sbx:4096 * (sbx + 1)])
        if debug and b == 0:
            W(SY, "ve", ve_q2[b])
            d_wq = DMA(SY, "db4", dbg["wq1b"], wq1[0][:, :])
            d_wq = DMA(SY, "db4", dbg["wq2b"], wq2[:, 0:4096])

        if b + 2 < NB:
            W(GP, "ve", ve_q2[b])
            c_fc[b + 2] = DMA(GP, "lf0" if sbx == 0 else "lf1", wbs1[sbx][:, :],
                              wfcb_d[b + 2, :, :])
            c_pj[b + 2] = DMA(GP, "lp0" if sbx == 0 else "lp1", wbs2[sbx][:, :],
                              wpjb_d[b + 2, :, :])
        W(PE, "ve", ve_q1)
        if b == 0:
            W(PE, "ve", ve_h2)
        gidx = 0
        for ht in range(2):
            for c in range(2):
                bank = pE if gidx % 2 == 0 else pF
                W(PE, "ac", ac_relu.get(gidx - 2))
                pe_h1 = None
                for t in range(DT):
                    pe_h1 = MM(bank[:, :],
                               wq1[sbx][:, 256 * t + 128 * ht:256 * t + 128 * (ht + 1)],
                               hh[:, t, 512 * c:512 * (c + 1)], start=(t == 0),
                               stop=(t == DT - 1), sem=("pe" if t == DT - 1 else None))
                W(SC, "pe", pe_h1)
                ac_r = OP(SC, "activation", rr[:, :], bank[:, :], ACT.Relu, sem="ac")
                ac_relu[gidx] = ac_r
                W(VE, "ac", ac_r)
                W(VE, "pe", last_hsq_mm.get(c))
                hslot = hsb[:, 1024 * (2 * sbx + ht) + 512 * c:
                            1024 * (2 * sbx + ht) + 512 * (c + 1)]
                OP(VE, "tensor_tensor", hslot, rr[:, :], rr[:, :], A.mult)
                OP(VE, "tensor_tensor", hsqT[:, 512 * c:512 * (c + 1)], hslot, hslot,
                   A.mult, sem="ve")
                W(PE, "ve", fl.cnt["ve"])
                last_hsq_mm[c] = MM((pG if c == 0 else pH)[0:1, :], ones1[:, :],
                                    hsqT[:, 512 * c:512 * (c + 1)],
                                    start=(sbx == 0 and ht == 0),
                                    stop=(sbx == 1 and ht == 1), sem="pe")
                gidx += 1
        pe_fc[b] = fl.cnt["pe"]
        if b % 2 == 1:
            sbi = b // 2
            W(VE, "pe", pe_fc[b])
            OP(VE, "tensor_tensor", statacc[0:1, 0:512], pG[0:1, :], statacc[0:1, 0:512],
               A.add)
            OP(VE, "tensor_tensor", statacc[0:1, 512:1024], pH[0:1, :],
               statacc[0:1, 512:1024], A.add, sem="ve")
            ve_st = fl.cnt["ve"]
            W(PE, "ve", ve_st)
            W(PE, "ve", ve_q2[b])
            flush = {}
            for rnd in range(8):
                pr0, pr1 = (pEF, pGH) if rnd % 2 == 0 else (pCD, pAB)
                banks4 = (pr0[:, 0:512], pr0[:, 512:1024], pr1[:, 0:512], pr1[:, 512:1024])
                slots = [(2 * rnd + a, cc2) for a in range(2) for cc2 in range(2)]
                W(PE, "ve", flush.get(rnd - 1))
                pe_prd = None
                for sl, (dtl, c) in enumerate(slots):
                    bank = banks4[sl]
                    for k in range(4):
                        pe_prd = MM(bank,
                                    wq2[:, 2048 * k + 128 * dtl:2048 * k + 128 * (dtl + 1)],
                                    hsb[:, 1024 * k + 512 * c:1024 * k + 512 * (c + 1)],
                                    start=(k == 0), stop=(k == 3),
                                    sem=("pe" if (sl == 3 and k == 3) else None))
                W(VE, "pe", pe_prd)
                dt0 = 2 * rnd
                OP(VE, "tensor_tensor", xT[:, dt0, :], pr0[:, :], xT[:, dt0, :],
                   A.add, sem="ve", drain=False)
                flush[rnd] = OP(VE, "tensor_tensor", xT[:, dt0 + 1, :], pr1[:, :],
                                xT[:, dt0 + 1, :], A.add, sem="ve")
            gq_acc[sbi] = flush[7]
            W(PE, "ve", flush[6])
            pe_pj[sbi] = fl.cnt["pe"]

    W(VE, "pe", pe_pj[-1])
    W(VE, "ve", gq_acc[-1])
    if debug:
        W(SY, "pe", pe_pj[-1])
        DMA(SY, "db5", dbg["acc0"], xT[:, 0, :])
        d_st0 = DMA(SY, "db5", dbg["rstd3"], statacc[0:1, :])
        W(VE, "db5", d_st0)
    OP(VE, "tensor_scalar", statacc[0:1, :], statacc[0:1, :], 1.0 / HID, None, A.mult,
       sem="ve")
    OP(VE, "tensor_scalar", statacc[0:1, :], statacc[0:1, :], RMS_EPS, None, A.add,
       sem="ve")
    W(SC, "ve", fl.cnt["ve"])
    OP(SC, "activation", statacc[0:1, :], statacc[0:1, :], ACT.Ln)
    OP(SC, "activation", statacc[0:1, :], statacc[0:1, :], ACT.Exp, scale=-0.5, sem="ac")
    W(PE, "ac", fl.cnt["ac"])
    MM(pA[:, :], onesf[:, :], statacc[0:1, 0:512], start=True, stop=True)
    pe_r3 = MM(pB[:, :], onesf[:, :], statacc[0:1, 512:1024], start=True, stop=True,
               sem="pe")
    c_x1 = [None] * DT
    for t in range(2):
        W(SY, "ve", fl.cnt["ve"])
        c_x1[t] = DMA(SY, "lr0" if t % 2 == 0 else "lr1", x1rel[t][:, :],
                      x1_sp[128 * t:128 * (t + 1), :])
    sq_out = None
    last_sq = {}
    for t in range(DT):
        W(VE, "lr0" if t % 2 == 0 else "lr1", c_x1[t])
        W(VE, "pe", pe_r3)
        for c in range(2):
            W(VE, "sq0" if c == 0 else "sq1", last_sq.get(c))
            OP(VE, "tensor_tensor", zt[:, :], xT[:, t, 512 * c:512 * (c + 1)],
               (pA if c == 0 else pB)[:, :], A.mult)
            OP(VE, "scalar_tensor_tensor", rr[:, :] if c == 0 else rr2[:, :], zt[:, :],
               gates[:, 80 + t:80 + t + 1], x1rel[t % 2][:, 512 * c:512 * (c + 1)],
               A.mult, A.add, sem="ve")
            W(SC, "ve", fl.cnt["ve"])
            sq_out = DMA(SC, "sq0" if c == 0 else "sq1",
                         out_d[128 * t:128 * (t + 1), 512 * c:512 * (c + 1)],
                         rr[:, :] if c == 0 else rr2[:, :])
            last_sq[c] = sq_out
        if t + 2 < DT:
            W(SY, "ve", fl.cnt["ve"])
            c_x1[t + 2] = DMA(SY, "lr0" if t % 2 == 0 else "lr1", x1rel[t % 2][:, :],
                              x1_sp[128 * (t + 2):128 * (t + 3), :])
    W(SC, "sq0", last_sq[0])
    W(SC, "sq1", last_sq[1])
    if debug:
        for dsn in ("db1", "db2", "db3", "db4", "db5"):
            W(SC, dsn, fl.cnt[dsn])

    sctx = contextlib.ExitStack()
    if True:
        for s in fl.cnt:
            fl.sems[s] = sctx.enter_context(nc.semaphore(s))
        with nc.Block() as block:
            @block.sync
            def _(e):
                for fn in fl.ops["sync"]:
                    fn(e)

            @block.tensor
            def _(e):
                for fn in fl.ops["tensor"]:
                    fn(e)

            @block.vector
            def _(e):
                for fn in fl.ops["vector"]:
                    fn(e)

            @block.scalar
            def _(e):
                for fn in fl.ops["scalar"]:
                    fn(e)

            @block.gpsimd
            def _(e):
                for fn in fl.ops["gpsimd"]:
                    fn(e)
    nc._keepalive = (ctx, sctx)
    return nc


_NC_CACHE = None


def kernel(L0_attn_Q_input, L0_attn_Wq, L0_attn_Wk, L0_attn_Wv, L0_attn_Wo,
           L0_ffn_W_fc, L0_ffn_W_proj, L0_c, L0_wmod):
    global _NC_CACHE
    f32 = np.float32
    x = np.ascontiguousarray(np.asarray(L0_attn_Q_input, f32).reshape(SEQ, D).T)
    wq = np.ascontiguousarray(
        np.asarray(L0_attn_Wq, f32).reshape(DT, 128, NH, 128).transpose(2, 1, 0, 3)
    ).reshape(NH, 128, DT * 128)
    wk = np.ascontiguousarray(
        np.asarray(L0_attn_Wk, f32).reshape(DT, 128, 128).transpose(1, 0, 2)
    ).reshape(128, DT * 128)
    wv = np.ascontiguousarray(
        np.asarray(L0_attn_Wv, f32).reshape(DT, 128, 128).transpose(1, 0, 2)
    ).reshape(128, DT * 128)
    wo = np.ascontiguousarray(np.asarray(L0_attn_Wo, f32))
    wfcb = np.ascontiguousarray(
        np.asarray(L0_ffn_W_fc, f32).reshape(NB, BH, DT, 128).transpose(0, 3, 2, 1)
    ).reshape(NB, 128, DT * BH)
    wpjb = np.ascontiguousarray(
        np.asarray(L0_ffn_W_proj, f32).T.reshape(NB, 2, 128, D).transpose(0, 2, 1, 3)
    ).reshape(NB, 128, 2 * D)
    cvec = np.ascontiguousarray(np.asarray(L0_c, f32).reshape(16, 128))
    wmod = np.asarray(L0_wmod, f32)
    identm = np.eye(128, dtype=ml_dtypes.bfloat16)

    if _NC_CACHE is None:
        _NC_CACHE = build_program()
    nc = _NC_CACHE

    in_maps = []
    for i in range(NCORES):
        in_maps.append({
            "xin": np.ascontiguousarray(x[:, S * i:S * (i + 1)]),
            "wqb": wq, "wkb": wk, "wvb": wv, "wo": wo,
            "wfcb": wfcb, "wpjb": wpjb, "cvec": cvec,
            "wmodc": np.ascontiguousarray(wmod[:, MODW * i:MODW * (i + 1)]),
            "ident": identm,
        })
    res = run_bass_kernel_spmd(nc, in_maps, core_ids=list(range(NCORES)))
    out = np.concatenate([np.asarray(r["out"]).T for r in res.results], axis=0)
    return np.ascontiguousarray(out.reshape(1, SEQ, D), dtype=np.float32)
